# version 19
# baseline (speedup 1.0000x reference)
import os
import numpy as np
import concourse.bass as bass
import concourse.mybir as mybir

SIM_DRAIN = False
SCL_DRAIN = False

F32 = mybir.dt.float32
BF16 = mybir.dt.bfloat16
AX = mybir.AxisListType
ALU = mybir.AluOpType
ACT = mybir.ActivationFunctionType


class Sched:
    ENGINES = ("sync", "scalar", "vector", "tensor", "gpsimd")

    def __init__(self, nc):
        self.nc = nc
        self.streams = {e: [] for e in self.ENGINES}
        self.sem_handles = {}
        self.sem_counts = {}
        self._sem_ctx = []

    def sem(self, name):
        if name not in self.sem_handles:
            cm = self.nc.semaphore(name)
            h = cm.__enter__()
            self._sem_ctx.append(cm)
            self.sem_handles[name] = h
            self.sem_counts[name] = 0
        return self.sem_handles[name]

    def op(self, eng, emit, waits=(), inc=None, tag=""):
        n_after = None
        if inc is not None:
            name, amt = inc
            self.sem(name)
            self.sem_counts[name] += amt
            n_after = self.sem_counts[name]
        for (s, v) in waits:
            self.sem(s)
        self.streams[eng].append((emit, tuple(waits), inc, tag))
        return (inc[0], n_after) if inc is not None else None

    def count(self, name):
        self.sem(name)
        return self.sem_counts[name]

    def emit_stream(self, eng_name, e):
        for (emit, waits, inc, tag) in self.streams[eng_name]:
            for (s, v) in waits:
                if v > 0:
                    e.wait_ge(self.sem_handles[s], v)
            ins = emit(e)
            if inc is not None:
                name, amt = inc
                if ins is None:
                    e.sem_inc(self.sem_handles[name], amt)
                else:
                    ins.then_inc(self.sem_handles[name], amt)

    def n_ops(self):
        return {k: len(v) for k, v in self.streams.items()}

    def finalize_waits(self, eng="gpsimd"):
        names = [n for n, c in self.sem_counts.items() if c > 0]
        def w(e):
            for n in names:
                e.wait_ge(self.sem_handles[n], self.sem_counts[n])
            return None
        self.streams[eng].append((w, (), None, "final_waits"))


class Arena:
    def __init__(self, nc, name, n_elems, dtype):
        self.t = nc.alloc_sbuf_tensor(name, [128, n_elems], dtype)
        self.n = n_elems
        self.off = 0
        self.marks = []
        self.peak = 0

    def alloc(self, n, align=32):
        o = (self.off + align - 1) // align * align
        assert o + n <= self.n, f"arena {self.t.name} overflow: {o}+{n} > {self.n}"
        self.off = o + n
        self.peak = max(self.peak, self.off)
        return self.t.ap()[:, o:o + n]

    def mark(self):
        self.marks.append(self.off)

    def release(self):
        self.off = self.marks.pop()


class MK:
    def __init__(self, nc, NT=32, QT=8, D=1024, I=2816, expose=()):
        self.nc = nc
        self.NT = NT
        self.QT = QT
        self.NQ = NT // QT
        self.D = D
        self.DCH = D // 128
        self.I = I
        self.ICH = I // 128
        self.IH = I // 2
        self.sc = Sched(nc)
        self.expose = set(expose)

        def dram(name, shape, dt, kind="Internal"):
            if kind == "Internal" and name in self.expose:
                kind = "ExternalOutput"
            return nc.dram_tensor(name, shape, dt, kind=kind)
        self.x_in = dram("x", [NT * 128, D], F32, kind="ExternalInput").ap()
        self.wq_in = dram("Wq", [D, D], F32, kind="ExternalInput").ap()
        self.wk_in = dram("Wk", [D, D], F32, kind="ExternalInput").ap()
        self.wv_in = dram("Wv", [D, D], F32, kind="ExternalInput").ap()
        self.wfc_in = dram("Wfc", [D, D], F32, kind="ExternalInput").ap()
        self.bfc_in = dram("bfc", [1, D], F32, kind="ExternalInput").ap()
        self.gam_in = dram("gamma", [1, D], F32, kind="ExternalInput").ap()
        self.bet_in = dram("beta", [1, D], F32, kind="ExternalInput").ap()
        self.wg_in = dram("w_gate", [I, D], F32, kind="ExternalInput").ap()
        self.wu_in = dram("w_up", [I, D], F32, kind="ExternalInput").ap()
        self.wd_in = dram("w_down", [D, I], F32, kind="ExternalInput").ap()
        self.tri_in = dram("tri", [128, 128], BF16, kind="ExternalInput").ap()
        self.y_out = dram("y", [NT * 128, D], F32, kind="ExternalOutput").ap()
        self.x1_hbm = dram("x1s", [NT * 128, D], BF16).ap()
        self.x2_hbm = dram("x2s", [NT * 128, D], BF16).ap()
        self.kt_hbm = dram("kts", [self.NQ, 128, self.DCH * QT * 128], BF16).ap()
        self.v_hbm = dram("vs", [NT, 128, D], BF16).ap()
        self.wg_hbm = dram("wgs", [128, self.DCH * I], BF16).ap()
        self.wu_hbm = dram("wus", [128, self.DCH * I], BF16).ap()
        self.wd_hbm = dram("wds", [128, self.ICH * D], BF16).ap()
        self.wa_hbm = dram("was", [128, 4 * self.DCH * D], BF16).ap()

        self.AB = Arena(nc, "AB", 86784, BF16)
        self.AF = Arena(nc, "AF", 9504, F32)
        self.c_eps = nc.alloc_sbuf_tensor("c_eps", [128, 1], F32).ap()
        self.c_b20 = nc.alloc_sbuf_tensor("c_b20", [128, 1], F32).ap()
        self.c_s32 = nc.alloc_sbuf_tensor("c_s32", [128, 1], F32).ap()
        self.c_one = nc.alloc_sbuf_tensor("c_one", [128, 1], F32).ap()
        self.c_mh = nc.alloc_sbuf_tensor("c_mh", [128, 1], F32).ap()
        self.tri = nc.alloc_sbuf_tensor("tri_sb", [128, 128], BF16).ap()
        self.ps = [nc.alloc_psum_tensor(f"ps{i}", [128, 512], F32) for i in range(8)]

    def v(self, emit, waits=(), inc=None, tag=""):
        return self.sc.op("vector", emit, waits, inc, tag)

    def scl(self, emit, waits=(), inc=None, tag=""):
        return self.sc.op("scalar", emit, waits, inc, tag)

    def gdma(self, emit, waits=(), inc=None, tag=""):
        return self.sc.op("gpsimd", emit, waits, inc, tag)

    def sy(self, emit, waits=(), inc=None, tag=""):
        return self.sc.op("sync", emit, waits, inc, tag)

    def t(self, emit, waits=(), inc=None, tag=""):
        return self.sc.op("tensor", emit, waits, inc, tag)

    def emit_quant(self, qin32, N, waits_src, out_q, rel_key=None, cmp_bf16=False):
        s = self.sc
        q = self.qb
        NB = N // 16
        NH = N // 2
        NBH = NB // 2
        pre = "qs"
        w_src = list(waits_src)
        H = ((0, NH, 0, NBH), (NH, N, NBH, NB))
        sp = self.qpar
        self.qpar ^= 1
        sgbuf = q["sg"][sp]
        sgg = self.sg_guard[sp] or []

        gtok = []
        for (lo, hi, _, _) in H:
            gtok.append(self.scl(lambda e, lo=lo, hi=hi: e.activation(
                out=sgbuf[:, lo:hi], in_=qin32[:, lo:hi], func=ACT.Sign),
                waits=w_src + sgg, inc=(pre + "_as", 1)))

        def VH(emitfn, dep=(), extra=()):
            toks = []
            for i, (lo, hi, blo, bhi) in enumerate(H):
                w = list(extra)
                if dep:
                    w.append(dep[i])
                toks.append(self.v(lambda e, lo=lo, hi=hi, blo=blo, bhi=bhi:
                                   emitfn(e, lo, hi, blo, bhi),
                                   waits=w, inc=(pre + "_v", 1)))
            return toks

        ab_ = []
        for i, (lo, hi, _, _) in enumerate(H):
            ab_.append(self.scl(lambda e, lo=lo, hi=hi: e.activation(
                out=qin32[:, lo:hi], in_=qin32[:, lo:hi], func=ACT.Abs),
                waits=[gtok[i]], inc=(pre + "_as", 1)))
        bm = VH(lambda e, lo, hi, blo, bhi: e.tensor_reduce(
            out=q["bm"][:, blo:bhi].rearrange("p (b i) -> p b i", i=1),
            in_=qin32[:, lo:hi].rearrange("p (b i) -> p b i", i=16),
            axis=AX.X, op=ALU.max), dep=ab_)
        s32 = VH(lambda e, lo, hi, blo, bhi: e.tensor_scalar(
            out=q["s32"][:, blo:bhi], in0=q["bm"][:, blo:bhi],
            scalar1=1.0 / 6.0, scalar2=1e-12, op0=ALU.mult, op1=ALU.max), dep=bm)
        r32 = VH(lambda e, lo, hi, blo, bhi: e.reciprocal(
            out=q["r32"][:, blo:bhi], in_=q["s32"][:, blo:bhi]), dep=s32)
        adst = q["axn"] if cmp_bf16 else qin32
        t1 = VH(lambda e, lo, hi, blo, bhi: e.tensor_tensor(
            out=adst[:, lo:hi].rearrange("p (b i) -> p b i", i=16),
            in0=qin32[:, lo:hi].rearrange("p (b i) -> p b i", i=16),
            in1=q["r32"][:, blo:bhi].rearrange("p (b i) -> p b i", i=1)
                .broadcast_to([128, NBH, 16]),
            op=ALU.mult), dep=r32)
        csrc = adst
        t2 = VH(lambda e, lo, hi, blo, bhi: e.tensor_single_scalar(
            out=q["t2"][:, lo:hi], in_=csrc[:, lo:hi], scalar=0.25, op=ALU.is_gt),
            dep=t1)
        cbp = t1
        for th in (0.75, 1.25, 1.75):
            cb = VH(lambda e, lo, hi, blo, bhi, th=th: e.tensor_single_scalar(
                out=q["cb"][:, lo:hi], in_=csrc[:, lo:hi], scalar=th, op=ALU.is_gt),
                dep=t1 if th == 0.75 else cbp)
            t2 = VH(lambda e, lo, hi, blo, bhi: e.tensor_add(
                out=q["t2"][:, lo:hi], in0=q["t2"][:, lo:hi], in1=q["cb"][:, lo:hi]),
                dep=[cb[0], cb[1]], extra=[t2[0], t2[1]])
            cbp = t2
        b1 = VH(lambda e, lo, hi, blo, bhi: e.tensor_single_scalar(
            out=q["cb"][:, lo:hi], in_=csrc[:, lo:hi], scalar=2.5, op=ALU.is_gt),
            dep=t2)
        b3 = VH(lambda e, lo, hi, blo, bhi: e.scalar_tensor_tensor(
            out=q["cb"][:, lo:hi], in0=csrc[:, lo:hi], scalar=3.5, in1=q["cb"][:, lo:hi],
            op0=ALU.is_gt, op1=ALU.add), dep=b1)
        eh = VH(lambda e, lo, hi, blo, bhi: e.tensor_single_scalar(
            out=q["t2"][:, lo:hi], in_=q["t2"][:, lo:hi], scalar=0.5, op=ALU.mult),
            dep=t2)
        e1 = VH(lambda e, lo, hi, blo, bhi: e.tensor_add(
            out=q["t2"][:, lo:hi], in0=q["t2"][:, lo:hi], in1=q["cb"][:, lo:hi]),
            dep=[b3[0], b3[1]], extra=[eh[0], eh[1]])
        c7 = VH(lambda e, lo, hi, blo, bhi: e.tensor_scalar(
            out=q["cb"][:, lo:hi], in0=csrc[:, lo:hi], scalar1=5.0, scalar2=2.0,
            op0=ALU.is_gt, op1=ALU.mult), dep=e1)
        if rel_key is not None:
            self.qrel[rel_key] = [c7[0], c7[1]]
        e2 = VH(lambda e, lo, hi, blo, bhi: e.tensor_add(
            out=q["t2"][:, lo:hi], in0=q["t2"][:, lo:hi], in1=q["cb"][:, lo:hi]),
            dep=[c7[0], c7[1]], extra=[e1[0], e1[1]])
        ss = VH(lambda e, lo, hi, blo, bhi: e.tensor_tensor(
            out=q["axn"][:, lo:hi].rearrange("p (b i) -> p b i", i=16),
            in0=sgbuf[:, lo:hi].rearrange("p (b i) -> p b i", i=16),
            in1=q["s32"][:, blo:bhi].rearrange("p (b i) -> p b i", i=1)
                .broadcast_to([128, NBH, 16]),
            op=ALU.mult), dep=e2, extra=[s32[0], s32[1]])
        self.sg_guard[sp] = [ss[0], ss[1]]
        (lo0, hi0, _, _), (lo1, hi1, _, _) = H
        tok0 = self.v(lambda e: e.tensor_tensor(
            out=out_q[:, lo0:hi0], in0=q["t2"][:, lo0:hi0], in1=q["axn"][:, lo0:hi0],
            op=ALU.mult), waits=[ss[0]] + list(self.qout_free), inc=(pre + "_v", 1))
        return self.v(lambda e: e.tensor_tensor(
            out=out_q[:, lo1:hi1], in0=q["t2"][:, lo1:hi1], in1=q["axn"][:, lo1:hi1],
            op=ALU.mult), waits=[ss[1], tok0], inc=(pre + "_q", 1))

    def emit_ln_stats(self, src, N, pre, use_expln, rcol=0):
        st = self.lnb["stats"][rcol % 4]
        rs = self.lnb["rs"][rcol % 4]
        s = self.sc
        nch = (N + 511) // 512
        prev_rs = max(0, s.count(pre + "_rs") - 1)

        w0 = list(self.lnb["waits"]) + ([(pre + "_rs", prev_rs)] if prev_rs else [])
        btok = []
        for c in range(nch):
            lo = c * 512
            hi = min(N, lo + 512)
            btok.append(self.v(lambda e, c=c, lo=lo, hi=hi: e.bn_stats(
                out=st[:, 6 * c:6 * c + 6], in_=src[:, lo:hi]), waits=w0,
                inc=("qs_v", 1)))
        ag = self.v(lambda e: e.bn_aggr(out=st[:, 12:14], in_=st[:, 0:6 * nch]),
                    waits=btok, inc=("qs_v", 1))
        ve = self.v(lambda e: e.tensor_scalar(out=st[:, 14:15], in0=st[:, 13:14],
                                              scalar1=1e-5, scalar2=None, op0=ALU.add),
                    waits=[ag], inc=("qs_v", 1))
        v1 = self.v(lambda e: e.reciprocal(out=st[:, 15:16], in_=st[:, 14:15]),
                    waits=[ve], inc=(pre + "_st", 1))
        rc = rs[:, 0:1]
        if use_expln:
            lt = self.scl(lambda e: e.activation(out=rc, in_=st[:, 13:14],
                                                 func=ACT.Ln, bias=self.c_eps,
                                                 scale=self.c_one),
                          waits=[v1, ("c_ini", 1)], inc=("qs_s", 1))
            r1 = self.scl(lambda e: e.activation(out=rc, in_=rc,
                                                 func=ACT.Exp, scale=self.c_mh),
                          waits=[lt], inc=(pre + "_rs", 1))
        else:
            r1 = self.scl(lambda e: e.activation(out=rc, in_=st[:, 15:16],
                                                 func=ACT.Sqrt),
                          waits=[v1, ("c_ini", 1)], inc=(pre + "_rs", 1))
        return r1

    def alloc_ffn_bufs(self, with_prep):
        AB, AF = self.AB, self.AF
        I, D, IH = self.I, self.D, self.IH
        b = {}
        b["wgqT"] = AB.alloc(self.DCH * I)
        b["wuqT"] = AB.alloc(self.DCH * I)
        b["wdqT"] = AB.alloc(self.ICH * D)
        b["txq"] = [AB.alloc(D) for _ in range(2)]
        b["hsg"] = AB.alloc(IH)
        b["qout"] = AB.alloc(IH)
        b["thq"] = [AB.alloc(IH), AB.alloc(IH)]
        b["xst"] = [AB.alloc(D) for _ in range(2)] if with_prep else None
        if with_prep:
            b["wst"] = [AF.alloc(IH) for _ in range(2)]
            b["wab"] = AB.alloc(D)
        q = {}
        q["axn"] = AB.alloc(IH)
        q["sg"] = [AB.alloc(IH), AB.alloc(IH)]
        q["t2"] = AB.alloc(IH)
        q["cb"] = AB.alloc(IH)
        _qs = AF.alloc(3 * (IH // 16))
        q["bm"] = _qs[:, 0:IH // 16]
        q["s32"] = _qs[:, IH // 16:2 * (IH // 16)]
        q["r32"] = _qs[:, 2 * (IH // 16):3 * (IH // 16)]
        q["in32"] = [AF.alloc(IH), AF.alloc(IH)]
        b["q"] = q
        b["xin"] = [AF.alloc(D) for _ in range(3)] if with_prep else None
        b["x2in"] = None if with_prep else [AB.alloc(D) for _ in range(3)]
        b["yst"] = None if with_prep else [AF.alloc(D) for _ in range(2)]
        stt_ = AF.alloc(128)
        b["stats"] = [stt_[:, 32 * i:32 * i + 16] for i in range(4)]
        b["rs"] = [stt_[:, 32 * i + 16:32 * i + 20] for i in range(4)]
        return b

    def emit_consts(self):
        self.v(lambda e: e.memset(self.c_eps, 1e-5))
        self.v(lambda e: e.memset(self.c_b20, -20.0))
        self.v(lambda e: e.memset(self.c_s32, 1.0 / 32.0))
        self.v(lambda e: e.memset(self.c_one, 1.0))
        self.v(lambda e: e.memset(self.c_mh, -0.5), inc=("c_ini", 1))
        self.sy(lambda e: e.dma_start(out=self.tri, in_=self.tri_in), inc=("tri_ld", 16))

    def emit_wchunk(self, kind, c, sub):
        s = self.sc
        fb = self.fb
        D, I, IH = self.D, self.I, self.IH
        j = self.wj % 2
        self.wj += 1
        wst = fb["wst"][j]
        wprev = self.qrel.get(("wst", j), [])
        if kind in ("g", "u"):
            N = D
            src_h = self.wg_in if kind == "g" else self.wu_in
            ld = self.gdma(lambda e: e.dma_start(out=wst[:, :N],
                                                 in_=src_h[c * 128:(c + 1) * 128, :]),
                           waits=wprev + [("w_ld", s.count("w_ld"))],
                           inc=("w_ld", 16))
        else:
            N = IH
            ld = self.gdma(lambda e: e.dma_start(
                out=wst[:, :N], in_=self.wd_in[c * 128:(c + 1) * 128, sub * IH:(sub + 1) * IH]),
                waits=wprev + [("w_ld", s.count("w_ld"))],
                inc=("w_ld", 16))
        wsb = fb["qout"]
        qt = self.emit_quant(wst[:, :N], N, [ld, ("w_t", s.count("w_t"))], wsb[:, :N],
                             rel_key=("wst", j))
        if kind in ("g", "u"):
            dst = fb["wgqT"] if kind == "g" else fb["wuqT"]
            out_ap = dst.rearrange("p (d i) -> p d i", i=I)[:, :, c * 128:(c + 1) * 128]
            in_ap = wsb[:, :N].rearrange("p (d r) -> p d r", r=128)
        else:
            dst = fb["wdqT"]
            out_ap = dst.rearrange("p (i d) -> p i d", d=D)[:, sub * 11:(sub + 1) * 11,
                                                           c * 128:(c + 1) * 128]
            in_ap = wsb[:, :N].rearrange("p (i r) -> p i r", r=128)
        tt = self.sy(lambda e: e.dma_start_transpose(out=out_ap, in_=in_ap),
                     waits=[qt, ("w_t", s.count("w_t"))], inc=("w_t", 16))
        self.qout_free = [tt]

    def emit_ffn_phase(self, first):
        s = self.sc
        NT = self.NT
        pre = "a" if first else "c"
        self.AB.mark()
        self.AF.mark()
        fb = self.alloc_ffn_bufs(with_prep=first)
        self.fb = fb
        self.qb = fb["q"]
        self.qpar = 0
        self.qinpar = 0
        self.sg_guard = [None, None]
        self.qrel = {}
        self.qout_free = []
        self.lnb = {"stats": fb["stats"], "rs": fb["rs"], "waits": []}
        if first:
            self.wj = 0
            order = ([("g", c, 0) for c in range(11)] + [("u", c, 0) for c in range(11)]
                     + [("g", c, 0) for c in range(11, 22)]
                     + [("u", c, 0) for c in range(11, 22)]
                     + [("d", c, sub) for c in range(8) for sub in range(2)])
            for (kind, c, sub) in order:
                self.emit_wchunk(kind, c, sub)
            self.w_thr = {"gu0": 16 * 22, "gu1": 16 * 44, "d": 16 * 60}
            wt_all = 16 * 60
            for (dst_h, src_a) in ((self.wg_hbm, fb["wgqT"]), (self.wu_hbm, fb["wuqT"]),
                                   (self.wd_hbm, fb["wdqT"])):
                self.gdma(lambda e, d=dst_h, a=src_a: e.dma_start(out=d, in_=a),
                          waits=[("w_t", wt_all)], inc=("w_sp", 16))
            D = self.D
            for (wi, src_h) in enumerate((self.wq_in, self.wk_in, self.wv_in, self.wfc_in)):
                for c in range(self.DCH):
                    j = self.wj % 2
                    self.wj += 1
                    wst = fb["wst"][j]
                    wprev = self.qrel.get(("wst", j), [])
                    ld = self.gdma(lambda e, sh=src_h, c=c, wst=wst: e.dma_start(
                        out=wst[:, :D], in_=sh[c * 128:(c + 1) * 128, :]),
                        waits=wprev + [("w_ld", s.count("w_ld"))], inc=("w_ld", 16))
                    cz = self.scl(lambda e, wst=wst: e.activation(
                        out=fb["wab"][:, :D], in_=wst[:, :D], func=ACT.Copy),
                        waits=[ld, ("wa_sp", s.count("wa_sp"))], inc=("w_ac", 1))
                    self.qrel[("wst", j)] = [cz]
                    self.gdma(lambda e, wi=wi, c=c: e.dma_start(
                        out=self.wa_hbm[:, (wi * self.DCH + c) * D:
                                        (wi * self.DCH + c + 1) * D],
                        in_=fb["wab"][:, :D]),
                        waits=[cz, ("wa_sp", s.count("wa_sp"))], inc=("wa_sp", 16))
        else:
            nld = 0
            early = [("b_qkmm", self.sc.count("b_qkmm")), ("b_vmm", self.sc.count("b_vmm")),
                     ("b_smm", self.sc.count("b_smm")), ("b_exp", self.sc.count("b_exp")),
                     ("b_att_done", self.sc.count("b_att_done")),
                     ("b_x2s", self.sc.count("b_x2s"))]
            for (src_h, dst_a) in ((self.wg_hbm, fb["wgqT"]), (self.wu_hbm, fb["wuqT"]),
                                   (self.wd_hbm, fb["wdqT"])):
                for piece in range(4):
                    n = src_h.shape[1] // 4
                    gate = early if nld < 3 else list(self.barrier_waits("gpsimd"))
                    self.gdma(lambda e, sh=src_h, da=dst_a, p=piece, n=n:
                              e.dma_start(out=da[:, p * n:(p + 1) * n],
                                          in_=sh[:, p * n:(p + 1) * n]),
                              waits=gate + [("w3_ld", self.sc.count("w3_ld"))],
                              inc=("w3_ld", 16))
                    nld += 1
            self.w_thr = {"gu0": 16 * nld, "gu1": 16 * nld, "d": 16 * nld}
            self.w_thr["sem"] = "w3_ld"

        self.emit_ffn_xpart(0, pre, first)
        self.emit_ffn_xrest(pre, first)
        for t in range(NT):
            self.emit_ffn_gu(t, 0, pre, "w_t" if first else "w3_ld")
            if t + 1 < NT:
                self.emit_ffn_xpart(t + 1, pre, first)
            self.emit_ffn_evhm(t, 0, pre)
            if t >= 1:
                self.emit_ffn_down(t - 1, pre, first, 0)
            if t + 1 < NT:
                self.emit_ffn_xrest(pre, first)
            self.emit_ffn_hqchain(pre)
            self.emit_ffn_gu(t, 1, pre, "w_t" if first else "w3_ld")
            self.emit_ffn_evhm(t, 1, pre)
            if t >= 1:
                self.emit_ffn_down(t - 1, pre, first, 1)
                self.emit_ffn_resid(t - 1, pre, first)
            self.emit_ffn_hqchain(pre)
        self.emit_ffn_down(NT - 1, pre, first, 0)
        self.emit_ffn_down(NT - 1, pre, first, 1)
        self.emit_ffn_resid(NT - 1, pre, first)
        for eng in Sched.ENGINES:
            bw = []
            if eng == "gpsimd":
                bw = [(pre + "_x1s", s.count(pre + "_x1s"))]
                if first:
                    bw.append(("w_sp", s.count("w_sp")))
            s.op(eng, lambda e: None, waits=bw,
                 inc=(("p1b_" if first else "p3b_") + eng, 1))
        self.AB.release()
        self.AF.release()

    def barrier_waits(self, eng, which="p2b_"):
        return [(which + e2, self.sc.count(which + e2)) for e2 in Sched.ENGINES]

    def emit_ffn_xpart(self, t, pre, first):
        s = self.sc
        fb = self.fb
        D = self.D
        if first:
            xsrc = fb["xin"][t % 3]
            ld = self.gdma(lambda e: e.dma_start(out=xsrc[:, :D],
                                                 in_=self.x_in[t * 128:(t + 1) * 128, :]),
                           waits=[(pre + "_x1", max(0, t - 2)),
                                  (pre + "_x0", s.count(pre + "_x0"))],
                           inc=(pre + "_x0", 16))
        else:
            xsrc = fb["x2in"][t % 3]
            ld = self.gdma(lambda e: e.dma_start(out=xsrc[:, :D],
                                                 in_=self.x2_hbm[t * 128:(t + 1) * 128, :]),
                           waits=[(pre + "_x1", max(0, t - 2)), ("b_x2s", 16 * (t + 1)),
                                  (pre + "_x0", s.count(pre + "_x0"))]
                           + (self.barrier_waits("gpsimd") if t < 3 else []),
                           inc=(pre + "_x0", 16))
        self.lnb["waits"] = [ld]
        r1 = self.emit_ln_stats(xsrc[:, :D], D, pre + "_ln", use_expln=False, rcol=t % 2)
        self.xp_state = (t, xsrc, r1)

    def emit_ffn_xrest(self, pre, first):
        s = self.sc
        fb = self.fb
        D = self.D
        (t, xsrc, r1) = self.xp_state
        st = fb["stats"][t % 2]
        qp = self.qinpar
        self.qinpar ^= 1
        qin = fb["q"]["in32"][qp]
        rcap = fb["rs"][t % 2][:, 0:1]
        ax = self.v(lambda e: e.tensor_scalar(out=qin[:, :D], in0=xsrc[:, :D],
                                              scalar1=st[:, 12:13], scalar2=rcap,
                                              op0=ALU.subtract, op1=ALU.mult),
                    waits=[r1] + self.qrel.get(("qin", qp), []),
                    inc=(pre + "_xn", 1))
        qx = self.emit_quant(qin[:, :D], D, [ax], fb["qout"][:, :D],
                             rel_key=("qin", qp))
        par = t % 2
        tt = self.sy(lambda e: e.dma_start_transpose(
            out=fb["txq"][par].rearrange("p (d r) -> p d r", r=128),
            in_=fb["qout"][:, :D].rearrange("p (d r) -> p d r", r=128)),
            waits=[qx, (pre + "_gu", max(0, 2 * (t - 1))),
                   (pre + "_txq", s.count(pre + "_txq"))], inc=(pre + "_txq", 16))
        self.qout_free = [tt]

    def emit_ffn_gu(self, t, h, pre, wsem):
        fb = self.fb
        IH = self.IH
        par = t % 2
        ncol = [512, 512, IH - 1024]
        wthr = self.w_thr["gu0"] if h == 0 else self.w_thr["gu1"]

        def gu(e):
            last = None
            for d in range(self.DCH):
                lhsT = fb["txq"][par][:, d * 128:(d + 1) * 128]
                for c in range(3):
                    w = ncol[c]
                    off = d * self.I + h * IH + c * 512
                    e.matmul(self.ps[c].ap()[:, :w], lhsT,
                             fb["wgqT"][:, off:off + w],
                             start=(d == 0), stop=(d == self.DCH - 1))
                    last = e.matmul(self.ps[3 + c].ap()[:, :w], lhsT,
                                    fb["wuqT"][:, off:off + w],
                                    start=(d == 0), stop=(d == self.DCH - 1))
            return last
        self.t(gu, waits=[(pre + "_txq", 16 * (t + 1)), (wsem, wthr),
                          (pre + "_ev", max(0, 2 * t + h)),
                          (pre + "_hm", max(0, 2 * t + h))], inc=(pre + "_gu", 1))

    def emit_ffn_evhm(self, t, h, pre):
        s = self.sc
        fb = self.fb
        IH = self.IH
        ncol = [512, 512, IH - 1024]

        def ev(e):
            last = None
            for c in range(3):
                w = ncol[c]
                last = e.activation(out=fb["hsg"][:, c * 512:c * 512 + w],
                                    in_=self.ps[c].ap()[:, :w], func=ACT.Silu)
            return last
        evt = self.sc.op("scalar", ev,
                         waits=[(pre + "_gu", 2 * t + h + 1), (pre + "_hm", max(0, 2 * t + h))],
                         inc=(pre + "_ev", 1))
        qp = self.qinpar
        self.qinpar ^= 1
        qin = fb["q"]["in32"][qp]

        def hmul(e):
            last = None
            for c in range(3):
                w = ncol[c]
                last = e.tensor_tensor(out=qin[:, c * 512:c * 512 + w],
                                       in0=fb["hsg"][:, c * 512:c * 512 + w],
                                       in1=self.ps[3 + c].ap()[:, :w], op=ALU.mult)
            return last
        hm = self.sc.op("vector", hmul,
                        waits=[evt] + self.qrel.get(("qin", qp), []),
                        inc=(pre + "_hm", 1))
        self.hq_state = (t, h, qp, qin, hm)

    def emit_ffn_hqchain(self, pre):
        s = self.sc
        fb = self.fb
        IH = self.IH
        (t, h, qp, qin, hm) = self.hq_state
        qh = self.emit_quant(qin[:, :IH], IH, [hm], fb["qout"][:, :IH],
                             rel_key=("qin", qp))
        tt = self.sy(lambda e: e.dma_start_transpose(
            out=fb["thq"][h].rearrange("p (i r) -> p i r", r=128),
            in_=fb["qout"][:, :IH].rearrange("p (i r) -> p i r", r=128)),
            waits=[qh, (pre + "_dn", max(0, t)),
                   (pre + "_thq", self.sc.count(pre + "_thq"))], inc=(pre + "_thq", 16))
        self.qout_free = [tt]

    def emit_ffn_down(self, t, pre, first, colh):
        fb = self.fb
        D = self.D
        wsem = "w_t" if first else "w3_ld"

        def dn(e):
            last = None
            for i in range(self.ICH):
                hh = 0 if i < 11 else 1
                ioff = (i - 11 * hh) * 128
                last = e.matmul(self.ps[6 + colh].ap()[:, :512],
                                fb["thq"][hh][:, ioff:ioff + 128],
                                fb["wdqT"][:, i * D + colh * 512: i * D + colh * 512 + 512],
                                start=(i == 0), stop=(i == self.ICH - 1))
            return last
        if colh == 0:
            self.t(dn, waits=[(pre + "_thq", 16 * (2 * t + 2)),
                              (wsem, self.w_thr["d"]),
                              (pre + "_x1", max(0, t))])
        else:
            self.t(dn, inc=(pre + "_dn", 1))

    def emit_ffn_resid(self, t, pre, first):
        s = self.sc
        fb = self.fb
        D = self.D
        par = t % 2
        xsrc = (fb["xin"] if first else fb["x2in"])[t % 3]
        dst = (fb["xst"] if first else fb["yst"])[par]
        out_hbm = self.x1_hbm if first else self.y_out
        self.v(lambda e: e.scalar_tensor_tensor(
            out=dst[:, 0:512], in0=self.ps[6].ap()[:, :512],
            scalar=0.5, in1=xsrc[:, 0:512], op0=ALU.mult, op1=ALU.add),
            waits=[(pre + "_dn", s.count(pre + "_dn")),
                   (pre + "_x1s", max(0, 16 * (t - 1)))])
        r2 = self.v(lambda e: e.scalar_tensor_tensor(
            out=dst[:, 512:1024], in0=self.ps[7].ap()[:, :512],
            scalar=0.5, in1=xsrc[:, 512:1024], op0=ALU.mult, op1=ALU.add),
            inc=(pre + "_x1", 1))
        self.gdma(lambda e: e.dma_start(out=out_hbm[t * 128:(t + 1) * 128, :],
                                        in_=dst[:, :D]),
                  waits=[r2, (pre + "_x1s", s.count(pre + "_x1s"))],
                  inc=(pre + "_x1s", 16))

    def finalize(self):
        self.gdma(lambda e: None, waits=[("c_x1s", self.sc.count("c_x1s"))])

    def alloc_attn_bufs(self):
        AB, AF = self.AB, self.AF
        D, QT = self.D, self.QT
        b = {}
        b["wq"] = AB.alloc(self.DCH * D)
        b["wk"] = AB.alloc(self.DCH * D)
        b["wv"] = AB.alloc(self.DCH * D)
        b["wfc"] = AB.alloc(self.DCH * D)
        b["xnT"] = AB.alloc(self.DCH * QT * 128)
        b["qT"] = AB.alloc(self.DCH * QT * 128)
        b["ktc"] = AB.alloc(self.DCH * QT * 128)
        b["vc"] = AB.alloc(QT * D)
        gkn = self.DCH * min(4, QT) * 128
        if 2 * gkn <= self.DCH * QT * 128:
            b["ktg"] = [b["xnT"][:, 0:gkn], b["xnT"][:, gkn:2 * gkn]]
        else:
            b["ktg"] = [AB.alloc(gkn) for _ in range(2)]
        b["vg"] = [AB.alloc(4 * D) for _ in range(2)]
        b["x1q"] = [AB.alloc(D) for _ in range(2)]
        b["x1r"] = [AB.alloc(D) for _ in range(2)]
        b["x2st"] = [AB.alloc(D) for _ in range(2)]
        b["xnb"] = [AB.alloc(D)]
        b["att"] = [AB.alloc(512) for _ in range(2)]
        b["attT"] = [AB.alloc(512) for _ in range(2)]
        b["lnr"] = AB.alloc(D)
        b["lnrT"] = [AB.alloc(D), AB.alloc(D)]
        b["gam"] = AF.alloc(D)
        b["bet"] = AF.alloc(D)
        b["bfc"] = AF.alloc(D)
        _st2 = AF.alloc(128)
        b["stats"] = [_st2[:, 32 * i:32 * i + 16] for i in range(4)]
        b["rs"] = [_st2[:, 32 * i + 16:32 * i + 20] for i in range(4)]
        b["wstg"] = [AF.alloc(D) for _ in range(2)]
        b["den"] = AF.alloc(2 * QT)
        b["dpart"] = AF.alloc(4)
        b["ov"] = AF.alloc(D)
        return b

    def emit_p2(self):
        s = self.sc
        NT, QT, NQ, D = self.NT, self.QT, self.NQ, self.D
        self.AB.mark()
        self.AF.mark()
        ab = self.alloc_attn_bufs()
        self.ab2 = ab
        self.qb = None
        self.lnb = {"stats": ab["stats"], "rs": ab["rs"], "waits": []}
        p1w = self.barrier_waits("gpsimd", "p1b_")

        for (wi, dst) in enumerate((ab["wq"], ab["wk"], ab["wv"], ab["wfc"])):
            self.gdma(lambda e, wi=wi, dst=dst: e.dma_start(
                out=dst, in_=self.wa_hbm[:, wi * self.DCH * self.D:
                                         (wi + 1) * self.DCH * self.D]),
                waits=p1w + [("wa_sp", 16 * 32), ("b_wld", s.count("b_wld"))],
                inc=("b_wld", 16))
        self.sc.op("scalar", lambda e: None,
                   waits=[("b_wld", s.count("b_wld"))], inc=("b_wc", 32))
        for (src_h, dst) in ((self.gam_in, ab["gam"]), (self.bet_in, ab["bet"]),
                             (self.bfc_in, ab["bfc"])):
            self.sy(lambda e, sh=src_h, dst=dst: e.dma_start(
                out=dst, in_=sh.broadcast_to([128, D])),
                waits=self.barrier_waits("sync", "p1b_") + [("b_cld", s.count("b_cld"))],
                inc=("b_cld", 16))

        self.pend_fin = []
        self.fc_pred = 0
        self.gr_last_av = {0: 0, 1: 0}
        for k in range(NQ):
            self.den_first = {tau: True for tau in range(self.QT)}
            self.o_first = {tau: [True, True] for tau in range(self.QT)}
            self.gr_free = {0: [], 1: []}
            self.emit_p2_qkv(k)
            self.emit_p2_attn(k)
            while self.pend_fin:
                self.emit_p2_finB()
        for eng in Sched.ENGINES:
            bw = []
            if eng == "gpsimd":
                bw = [(n, s.count(n)) for n in
                      ("b_ktsp", "b_vsp", "b_x2s", "b_gl", "b_x1l", "b_x1rl", "b_wld")]
            s.op(eng, lambda e: None, waits=bw, inc=("p2b_" + eng, 1))
        self.AB.release()
        self.AF.release()

    def emit_p2_qkv(self, k):
        s = self.sc
        ab = self.ab2
        D, QT = self.D, self.QT
        xnt_base = s.count("b_xnt")
        for tau in range(QT):
            t = k * QT + tau
            xb = ab["x1q"][tau % 2]
            ld = self.gdma(lambda e, t=t, xb=xb: e.dma_start(
                out=xb[:, :D], in_=self.x1_hbm[t * 128:(t + 1) * 128, :]),
                waits=[("a_x1s", 16 * self.NT), ("b_x1f", max(0, s.count("b_x1f") - 1)),
                       ("b_x1l", s.count("b_x1l"))],
                inc=("b_x1l", 16))
            self.lnb["waits"] = [ld]
            r1 = self.emit_ln_stats(xb[:, :D], D, "b_ln", use_expln=True, rcol=tau % 2)
            xn = ab["xnb"][0]
            st = ab["stats"][tau % 2]
            rcap = ab["rs"][tau % 2][:, 0:1]
            ax = self.v(lambda e, xb=xb, xn=xn, st=st, rcap=rcap: e.tensor_scalar(
                out=xn[:, :D], in0=xb[:, :D], scalar1=st[:, 12:13], scalar2=rcap,
                op0=ALU.subtract, op1=ALU.mult),
                waits=[r1, ("b_xnt", s.count("b_xnt"))],
                inc=("b_x1f", 1))
            self.sy(lambda e, xn=xn, tau=tau: e.dma_start_transpose(
                out=ab["xnT"].rearrange("p (d r) -> p d r", r=QT * 128)[:, :, tau * 128:(tau + 1) * 128],
                in_=xn[:, :D].rearrange("p (d r) -> p d r", r=128)),
                waits=[ax, ("b_xnt", s.count("b_xnt")),
                       ("b_qkmm", s.count("b_qkmm")), ("b_vmm", s.count("b_vmm")),
                       ("b_exp", s.count("b_exp")), ("b_gl", s.count("b_gl"))],
                inc=("b_xnt", 16))
        rhw = min(512, QT * 128)
        nrh = (QT * 128) // rhw
        qkctr = 0
        for rh in range(nrh):
            xnt_rh = xnt_base + 16 * (rh + 1) * (rhw // 128)
            for (wname, dsts, spill) in (("wq", "qT", False), ("wk", "ktc", True)):
                w = ab[wname]
                dst = ab[dsts]
                for dk in range(self.DCH):
                    pspar = qkctr % 2
                    qkctr += 1
                    def mm(e, w=w, dk=dk, rh=rh, pspar=pspar):
                        last = None
                        for d in range(self.DCH):
                            last = e.matmul(self.ps[pspar].ap()[:, :rhw],
                                            w[:, d * D + dk * 128: d * D + dk * 128 + 128],
                                            ab["xnT"][:, d * QT * 128 + rh * rhw:
                                                      d * QT * 128 + rh * rhw + rhw],
                                            start=(d == 0), stop=(d == self.DCH - 1))
                        return last
                    mt = self.t(mm, waits=[("b_xnt", xnt_rh), ("b_wc", 32),
                                           ("b_exp", s.count("b_exp")),
                                           ("b_qkev", max(0, s.count("b_qkev") - 1))],
                                inc=("b_qkmm", 1))
                    self.scl(lambda e, dst=dst, dk=dk, rh=rh, pspar=pspar: e.activation(
                        out=dst[:, dk * QT * 128 + rh * rhw: dk * QT * 128 + rh * rhw + rhw],
                        in_=self.ps[pspar].ap()[:, :rhw], func=ACT.Copy),
                        waits=[mt, ("b_att_done", s.count("b_att_done")),
                               ("b_ktsp", s.count("b_ktsp"))],
                        inc=("b_qkev", 1))
        for tau in range(QT):
            for vh in range(2):
                pspar = (tau * 2 + vh) % 2
                def mmv(e, tau=tau, vh=vh, pspar=pspar):
                    last = None
                    for d in range(self.DCH):
                        last = e.matmul(self.ps[2 + pspar].ap()[:, :512],
                                        ab["xnT"][:, d * QT * 128 + tau * 128:
                                                  d * QT * 128 + tau * 128 + 128],
                                        ab["wv"][:, d * D + vh * 512: d * D + vh * 512 + 512],
                                        start=(d == 0), stop=(d == self.DCH - 1))
                    return last
                mt = self.t(mmv, waits=[("b_xnt", xnt_base + 16 * (tau + 1)), ("b_wc", 32),
                                        ("b_oev", s.count("b_oev")),
                                        ("b_vev", max(0, s.count("b_vev") - 1))],
                            inc=("b_vmm", 1))
                self.scl(lambda e, tau=tau, vh=vh, pspar=pspar: e.activation(
                    out=ab["vc"][:, tau * D + vh * 512: tau * D + vh * 512 + 512],
                    in_=self.ps[2 + pspar].ap()[:, :512], func=ACT.Copy),
                    waits=[mt, ("b_att_done", s.count("b_att_done")),
                           ("b_vsp", s.count("b_vsp"))],
                    inc=("b_vev", 1))
        kev = s.count("b_qkev")
        vev = s.count("b_vev")
        self.gdma(lambda e, k=k: e.dma_start(out=self.kt_hbm[k], in_=ab["ktc"]),
                  waits=[("b_qkev", kev), ("b_ktsp", s.count("b_ktsp"))],
                  inc=("b_ktsp", 16))
        self.gdma(lambda e, k=k: e.dma_start(
            out=self.v_hbm[k * QT:(k + 1) * QT].rearrange("t p d -> p t d"),
            in_=ab["vc"].rearrange("p (t d) -> p t d", d=D)),
            waits=[("b_vev", vev), ("b_vsp", s.count("b_vsp"))], inc=("b_vsp", 16))

    def emit_p2_attn(self, k):
        s = self.sc
        ab = self.ab2
        D, QT = self.D, self.QT
        gsz = min(4, self.QT)
        ngran = (self.QT // gsz) * k
        for g in range(QT // 2):
            taus = (2 * g, 2 * g + 1)
            gtoks = []
            base_av = s.count("b_av")
            for gr in range(ngran):
                q, gi = gr // 2, gr % 2
                par = gr % 2
                wk = [("b_ktsp", 16 * (q + 1)), ("b_vsp", 16 * (q + 1))]
                if gr >= 2:
                    wk.append(("b_av", base_av + 2 * (gr - 2) + 2))
                else:
                    if self.gr_last_av[par]:
                        wk.append(("b_av", self.gr_last_av[par]))
                    wk.append(("b_qkmm", s.count("b_qkmm")))
                    wk.append(("b_vmm", s.count("b_vmm")))
                gw = gsz * 128
                l1 = self.gdma(lambda e, q=q, gi=gi, par=par, gw=gw: e.dma_start(
                    out=ab["ktg"][par][:, :self.DCH * gw].rearrange("p (d r) -> p d r", r=gw),
                    in_=self.kt_hbm[q].rearrange("p (d r) -> p d r", r=QT * 128)
                    [:, :, gi * gw:(gi + 1) * gw]),
                    waits=wk + [("b_gl", s.count("b_gl"))], inc=("b_gl", 16))
                l2 = self.gdma(lambda e, q=q, gi=gi, par=par: e.dma_start(
                    out=ab["vg"][par].rearrange("p (t d) -> p t d", d=D)[:, :gsz],
                    in_=self.v_hbm[q * QT + gi * gsz: q * QT + gi * gsz + gsz]
                    .rearrange("t p d -> p t d")),
                    waits=[("b_gl", s.count("b_gl"))], inc=("b_gl", 16))
                gtoks.append((l1, l2, par))
            chunks = []
            for ci in range(ngran):
                for tau in taus:
                    chunks.append((tau, "g", ci, gsz, False))
            loc = {tau: [] for tau in taus}
            for tau in taus:
                lo = 0
                while lo < tau:
                    wdt = min(4, tau - lo)
                    loc[tau].append((tau, "l", lo, wdt, False))
                    lo += wdt
                loc[tau].append((tau, "l", tau, 1, True))
            mx = max(len(loc[t]) for t in taus)
            for i in range(mx):
                for tau in taus:
                    if i < len(loc[tau]):
                        chunks.append(loc[tau][i])
            for a_ in range(ngran):
                self.gr_last_av[a_ % 2] = base_av + 2 * a_ + 2
            self.oev_gate = s.count("b_oev")
            self.av_base = s.count("b_av")
            pend = []
            for ci, ch in enumerate(chunks):
                info = self.emit_p2_scores(k, g, ci, ch, gtoks)
                pend.append(info)
                if ci in (1, 3):
                    self.emit_p2_finB()
                if len(pend) > 2:
                    self.emit_p2_av(*pend.pop(0))
            for p_ in pend:
                self.emit_p2_av(*p_)
            for tau in taus:
                self.emit_p2_fin(k, tau)
            if os.environ.get("MK_NODEFER") == "1":
                while self.pend_fin:
                    self.emit_p2_finB()

    def emit_p2_scores(self, k, g, ci, ch, gtoks):
        s = self.sc
        ab = self.ab2
        D, QT = self.D, self.QT
        (tau, kind, a, wt, diag) = ch
        w = wt * 128
        par = ci % 2
        gpar = gtoks[a][2] if kind == "g" else 0

        def mm(e):
            last = None
            for dk in range(self.DCH):
                lhsT = ab["qT"][:, dk * QT * 128 + tau * 128: dk * QT * 128 + tau * 128 + 128]
                if kind == "g":
                    gw = min(4, self.QT) * 128
                    rhs = ab["ktg"][gpar][:, dk * gw: dk * gw + w]
                else:
                    rhs = ab["ktc"][:, dk * QT * 128 + a * 128: dk * QT * 128 + a * 128 + w]
                last = e.matmul(self.ps[par].ap()[:, :w], lhsT, rhs,
                                start=(dk == 0), stop=(dk == self.DCH - 1))
            return last
        waits = [("b_exp", max(0, s.count("b_exp") - 1)),
                 ("b_qkev", s.count("b_qkev"))]
        if kind == "g":
            waits.append(gtoks[a][0])
        mt = self.t(mm, waits=waits, inc=("b_smm", 1))
        att = ab["att"][par]
        dguard = ("b_den", max(0, s.count("b_den") - 1))
        if not diag:
            ex = self.scl(lambda e: e.activation(
                out=att[:, :w], in_=self.ps[par].ap()[:, :w], func=ACT.Exp,
                bias=self.c_b20, scale=self.c_s32, accum_out=ab["dpart"][:, par:par + 1]),
                waits=[mt, ("c_ini", 1), dguard,
                       ("b_att_t", max(0, s.count("b_att_t") - 16))],
                inc=("b_exp", 1))
            self.v(self.den_update_op(tau, par),
                   waits=[ex, ("b_oev", self.oev_gate)], inc=("b_den", 1))
            mk_tok = ex
        else:
            ex = self.scl(lambda e: e.activation(
                out=att[:, :w], in_=self.ps[par].ap()[:, :w], func=ACT.Exp,
                bias=self.c_b20, scale=self.c_s32),
                waits=[mt, ("c_ini", 1), dguard,
                       ("b_att_t", max(0, s.count("b_att_t") - 16))],
                inc=("b_exp", 1))
            mk_tok = self.v(lambda e: e.tensor_tensor(out=att[:, :w], in0=att[:, :w],
                                                      in1=self.tri, op=ALU.mult),
                            waits=[ex, ("tri_ld", 16)], inc=("b_msk", 1))
            rr = self.v(lambda e: e.tensor_reduce(
                out=ab["dpart"][:, par:par + 1].rearrange("p (a b) -> p a b", b=1),
                in_=att[:, :w].rearrange("p (a b) -> p a b", b=w),
                axis=AX.X, op=ALU.add), waits=[mk_tok], inc=("qs_v", 1))
            self.v(self.den_update_op(tau, par),
                   waits=[rr, ("b_oev", self.oev_gate)], inc=("b_den", 1))
        tr = self.sy(lambda e: e.dma_start_transpose(
            out=ab["attT"][par][:, :w].rearrange("p (c r) -> p c r", r=128),
            in_=att[:, :w].rearrange("p (c r) -> p c r", r=128)),
            waits=[mk_tok, ("b_av", max(self.av_base, self.av_base + ci - 1)),
                   ("b_att_t", s.count("b_att_t"))],
            inc=("b_att_t", 16))
        vtok = gtoks[a][1] if kind == "g" else None
        return (ch, par, gpar, tr, vtok)

    def den_update_op(self, tau, par):
        first = self.den_first[tau]
        self.den_first[tau] = False
        dst = self.ab2["den"][:, tau:tau + 1]
        srcp = self.ab2["dpart"][:, par:par + 1]
        if first:
            return lambda e: e.tensor_copy(out=dst, in_=srcp)
        return lambda e: e.tensor_tensor(out=dst, in0=dst, in1=srcp, op=ALU.add)

    def emit_p2_av(self, ch, par, gpar, tr, vtok=None):
        s = self.sc
        ab = self.ab2
        D = self.D
        (tau, kind, a, wt, diag) = ch
        obank = 2 + 2 * (tau % 2)

        firsts = (self.o_first[tau][0], self.o_first[tau][1])
        self.o_first[tau][0] = False
        self.o_first[tau][1] = False

        def mm(e):
            last = None
            for vh in range(2):
                for mt_ in range(wt):
                    lhsT = ab["attT"][par][:, mt_ * 128: mt_ * 128 + 128]
                    if kind == "g":
                        rhs = ab["vg"][gpar][:, mt_ * D + vh * 512: mt_ * D + vh * 512 + 512]
                    else:
                        rhs = ab["vc"][:, (a + mt_) * D + vh * 512: (a + mt_) * D + vh * 512 + 512]
                    last = e.matmul(self.ps[obank + vh].ap()[:, :512], lhsT, rhs,
                                    start=(firsts[vh] and mt_ == 0),
                                    stop=(diag and mt_ == wt - 1))
            return last
        w_ = [tr, ("b_oev", self.oev_gate)]
        if vtok is not None:
            w_.append(vtok)
        tok = self.t(mm, waits=w_, inc=("b_av", 1))
        if kind == "g":
            self.gr_free[gpar] = [tok]

    def emit_p2_fin(self, k, tau):
        s = self.sc
        ab = self.ab2
        D, QT = self.D, self.QT
        t = k * QT + tau
        obank = 2 + 2 * (tau % 2)
        den_n = s.count("b_den")
        par = tau % 2
        x1r = ab["x1r"][par]
        ld = self.gdma(lambda e: e.dma_start(out=x1r[:, :D],
                                             in_=self.x1_hbm[t * 128:(t + 1) * 128, :]),
                       waits=[("b_x1rl", s.count("b_x1rl")),
                              ("b_x2w", max(0, s.count("b_x2w") - 1))],
                       inc=("b_x1rl", 16))
        rd = self.v(lambda e: e.reciprocal(out=ab["den"][:, QT + tau:QT + tau + 1],
                                           in_=ab["den"][:, tau:tau + 1]),
                    waits=[("b_den", den_n), ("b_av", s.count("b_av"))],
                    inc=("qs_v", 1))
        rden = ab["den"][:, QT + tau:QT + tau + 1]
        e1 = self.v(lambda e: e.tensor_scalar(out=ab["ov"][:, 0:512],
                                              in0=self.ps[obank].ap()[:, :512],
                                              scalar1=rden, scalar2=None, op0=ALU.mult),
                    waits=[rd], inc=("b_oev", 1))
        e2 = self.v(lambda e: e.tensor_scalar(out=ab["ov"][:, 512:1024],
                                              in0=self.ps[obank + 1].ap()[:, :512],
                                              scalar1=rden, scalar2=None, op0=ALU.mult),
                    waits=[e1], inc=("b_oev", 1))
        av = self.v(lambda e: e.tensor_tensor(
            out=ab["ov"][:, :D], in0=ab["ov"][:, :D],
            in1=ab["vc"][:, tau * D:(tau + 1) * D], op=ALU.add),
            waits=[e2], inc=("qs_v", 1))
        self.lnb["waits"] = [av]
        r1 = self.emit_ln_stats(ab["ov"][:, :D], D, "b_ln2", use_expln=True,
                                rcol=2 + tau % 2)
        stt_ = ab["stats"][2 + tau % 2]
        rcap = ab["rs"][2 + tau % 2][:, 0:1]
        n1 = self.v(lambda e, stt_=stt_, rcap=rcap: e.tensor_scalar(
            out=ab["ov"][:, :D], in0=ab["ov"][:, :D],
            scalar1=stt_[:, 12:13], scalar2=rcap,
            op0=ALU.subtract, op1=ALU.mult),
                    waits=[r1], inc=("qs_v", 1))
        n2 = self.v(lambda e: e.tensor_tensor(out=ab["ov"][:, :D], in0=ab["ov"][:, :D],
                                              in1=ab["gam"][:, :D], op=ALU.mult),
                    waits=[n1, ("b_cld", 48)], inc=("qs_v", 1))
        n3 = self.v(lambda e: e.tensor_tensor(out=ab["lnr"][:, :D], in0=ab["ov"][:, :D],
                                              in1=ab["bet"][:, :D], op=ALU.add),
                    waits=[n2, ("b_lnt", s.count("b_lnt"))], inc=("qs_v", 1))
        lt = ab["lnrT"][par]
        tr = self.sy(lambda e: e.dma_start_transpose(
            out=lt.rearrange("p (d r) -> p d r", r=128),
            in_=ab["lnr"][:, :D].rearrange("p (d r) -> p d r", r=128)),
            waits=[n3, ("b_lnt", s.count("b_lnt")),
                   ("b_fc", self.fc_pred)],
            inc=("b_lnt", 16))
        self.fc_pred += 1
        self.pend_fin.append((k, tau, t, tr, ld))

    def emit_p2_finB(self):
        if not self.pend_fin:
            return
        (k, tau, t, tr, ld) = self.pend_fin.pop(0)
        s = self.sc
        ab = self.ab2
        D, QT = self.D, self.QT
        par = tau % 2
        lt = ab["lnrT"][par]

        def fc(e):
            last = None
            for oh in range(2):
                for d in range(self.DCH):
                    last = e.matmul(self.ps[6 + oh].ap()[:, :512],
                                    lt[:, d * 128: d * 128 + 128],
                                    ab["wfc"][:, d * D + oh * 512: d * D + oh * 512 + 512],
                                    start=(d == 0), stop=(d == self.DCH - 1))
            return last
        ft = self.t(fc, waits=[tr, ("b_wc", 32), ("b_x2w", s.count("b_x2w"))],
                    inc=("b_fc", 1))
        x2 = ab["x2st"][par]
        w1 = self.v(lambda e: e.scalar_tensor_tensor(
            out=x2[:, 0:512], in0=self.ps[6].ap()[:, :512], scalar=1.0,
            in1=ab["bfc"][:, 0:512], op0=ALU.mult, op1=ALU.add),
            waits=[ft, ld, ("b_x2s", max(0, 16 * (t - 1)))], inc=("qs_v", 1))
        w2 = self.v(lambda e: e.scalar_tensor_tensor(
            out=x2[:, 512:1024], in0=self.ps[7].ap()[:, :512], scalar=1.0,
            in1=ab["bfc"][:, 512:1024], op0=ALU.mult, op1=ALU.add),
            waits=[w1], inc=("qs_v", 1))
        w3 = self.v(lambda e: e.tensor_tensor(out=x2[:, :D], in0=x2[:, :D],
                                              in1=ab["x1r"][par][:, :D], op=ALU.add),
                    waits=[w2], inc=("b_x2w", 1))
        self.gdma(lambda e: e.dma_start(out=self.x2_hbm[t * 128:(t + 1) * 128, :],
                                        in_=x2[:, :D]),
                  waits=[w3, ("b_x2s", s.count("b_x2s"))], inc=("b_x2s", 16))
        self.v(lambda e: None, inc=("b_att_done", 1))

import ml_dtypes
from concourse.bass_utils import run_bass_kernel_spmd

_NT, _QT, _D, _I = 32, 8, 1024, 2816


def _build():
    nc = bass.Bass("TRN2", target_bir_lowering=False)
    mk = MK(nc, NT=_NT, QT=_QT, D=_D, I=_I)
    mk.emit_consts()
    mk.emit_ffn_phase(first=True)
    mk.emit_p2()
    mk.emit_ffn_phase(first=False)
    mk.sc.finalize_waits()
    with nc.Block() as blk:
        @blk.sync
        def _(e):
            mk.sc.emit_stream("sync", e)

        @blk.scalar
        def _(e):
            mk.sc.emit_stream("scalar", e)

        @blk.vector
        def _(e):
            mk.sc.emit_stream("vector", e)

        @blk.tensor
        def _(e):
            mk.sc.emit_stream("tensor", e)

        @blk.gpsimd
        def _(e):
            mk.sc.emit_stream("gpsimd", e)
    return nc


def kernel(L0_attn_x, L0_attn_Wq, L0_attn_Wk, L0_attn_Wv, L0_attn_Wfc,
           L0_attn_bfc, L0_attn_gamma, L0_attn_beta,
           L0_ffn_w_gate, L0_ffn_w_up, L0_ffn_w_down):
    nc = _build()
    x = np.ascontiguousarray(L0_attn_x)
    tri = np.tril(np.ones((128, 128), np.float32)).astype(ml_dtypes.bfloat16)
    base = {
        "Wq": np.ascontiguousarray(L0_attn_Wq),
        "Wk": np.ascontiguousarray(L0_attn_Wk),
        "Wv": np.ascontiguousarray(L0_attn_Wv),
        "Wfc": np.ascontiguousarray(L0_attn_Wfc),
        "bfc": np.ascontiguousarray(L0_attn_bfc).reshape(1, _D),
        "gamma": np.ascontiguousarray(L0_attn_gamma).reshape(1, _D),
        "beta": np.ascontiguousarray(L0_attn_beta).reshape(1, _D),
        "w_gate": np.ascontiguousarray(L0_ffn_w_gate),
        "w_up": np.ascontiguousarray(L0_ffn_w_up),
        "w_down": np.ascontiguousarray(L0_ffn_w_down),
        "tri": tri,
    }
    in_maps = [dict(base, x=np.ascontiguousarray(x[b])) for b in range(8)]
    res = run_bass_kernel_spmd(nc, in_maps, core_ids=list(range(8)))
    out = np.stack([res.results[b]["y"] for b in range(8)], axis=0)
    return out.astype(np.float32)


# version 21
# speedup vs baseline: 1.0151x; 1.0151x over previous
import os
import numpy as np
import concourse.bass as bass
import concourse.mybir as mybir

SIM_DRAIN = False
SCL_DRAIN = False

F32 = mybir.dt.float32
BF16 = mybir.dt.bfloat16
AX = mybir.AxisListType
ALU = mybir.AluOpType
ACT = mybir.ActivationFunctionType


class Sched:
    ENGINES = ("sync", "scalar", "vector", "tensor", "gpsimd")

    def __init__(self, nc):
        self.nc = nc
        self.streams = {e: [] for e in self.ENGINES}
        self.sem_handles = {}
        self.sem_counts = {}
        self._sem_ctx = []

    def sem(self, name):
        if name not in self.sem_handles:
            cm = self.nc.semaphore(name)
            h = cm.__enter__()
            self._sem_ctx.append(cm)
            self.sem_handles[name] = h
            self.sem_counts[name] = 0
        return self.sem_handles[name]

    def op(self, eng, emit, waits=(), inc=None, tag=""):
        n_after = None
        if inc is not None:
            name, amt = inc
            self.sem(name)
            self.sem_counts[name] += amt
            n_after = self.sem_counts[name]
        for (s, v) in waits:
            self.sem(s)
        self.streams[eng].append((emit, tuple(waits), inc, tag))
        return (inc[0], n_after) if inc is not None else None

    def count(self, name):
        self.sem(name)
        return self.sem_counts[name]

    def emit_stream(self, eng_name, e):
        for (emit, waits, inc, tag) in self.streams[eng_name]:
            for (s, v) in waits:
                if v > 0:
                    e.wait_ge(self.sem_handles[s], v)
            ins = emit(e)
            if inc is not None:
                name, amt = inc
                if ins is None:
                    e.sem_inc(self.sem_handles[name], amt)
                else:
                    ins.then_inc(self.sem_handles[name], amt)

    def n_ops(self):
        return {k: len(v) for k, v in self.streams.items()}

    def finalize_waits(self, eng="gpsimd"):
        names = [n for n, c in self.sem_counts.items() if c > 0]
        def w(e):
            for n in names:
                e.wait_ge(self.sem_handles[n], self.sem_counts[n])
            return None
        self.streams[eng].append((w, (), None, "final_waits"))


class Arena:
    def __init__(self, nc, name, n_elems, dtype):
        self.t = nc.alloc_sbuf_tensor(name, [128, n_elems], dtype)
        self.n = n_elems
        self.off = 0
        self.marks = []
        self.peak = 0

    def alloc(self, n, align=32):
        o = (self.off + align - 1) // align * align
        assert o + n <= self.n, f"arena {self.t.name} overflow: {o}+{n} > {self.n}"
        self.off = o + n
        self.peak = max(self.peak, self.off)
        return self.t.ap()[:, o:o + n]

    def mark(self):
        self.marks.append(self.off)

    def release(self):
        self.off = self.marks.pop()


class MK:
    def __init__(self, nc, NT=32, QT=8, D=1024, I=2816, expose=()):
        self.nc = nc
        self.NT = NT
        self.QT = QT
        self.NQ = NT // QT
        self.D = D
        self.DCH = D // 128
        self.I = I
        self.ICH = I // 128
        self.IH = I // 2
        self.sc = Sched(nc)
        self.expose = set(expose)

        def dram(name, shape, dt, kind="Internal"):
            if kind == "Internal" and name in self.expose:
                kind = "ExternalOutput"
            return nc.dram_tensor(name, shape, dt, kind=kind)
        self.x_in = dram("x", [NT * 128, D], F32, kind="ExternalInput").ap()
        self.wq_in = dram("Wq", [D, D], F32, kind="ExternalInput").ap()
        self.wk_in = dram("Wk", [D, D], F32, kind="ExternalInput").ap()
        self.wv_in = dram("Wv", [D, D], F32, kind="ExternalInput").ap()
        self.wfc_in = dram("Wfc", [D, D], F32, kind="ExternalInput").ap()
        self.bfc_in = dram("bfc", [1, D], F32, kind="ExternalInput").ap()
        self.gam_in = dram("gamma", [1, D], F32, kind="ExternalInput").ap()
        self.bet_in = dram("beta", [1, D], F32, kind="ExternalInput").ap()
        self.wg_in = dram("w_gate", [I, D], F32, kind="ExternalInput").ap()
        self.wu_in = dram("w_up", [I, D], F32, kind="ExternalInput").ap()
        self.wd_in = dram("w_down", [D, I], F32, kind="ExternalInput").ap()
        self.tri_in = dram("tri", [128, 128], BF16, kind="ExternalInput").ap()
        self.y_out = dram("y", [NT * 128, D], F32, kind="ExternalOutput").ap()
        self.x1_hbm = dram("x1s", [NT * 128, D], BF16).ap()
        self.x2_hbm = dram("x2s", [NT * 128, D], BF16).ap()
        self.kt_hbm = dram("kts", [self.NQ, 128, self.DCH * QT * 128], BF16).ap()
        self.v_hbm = dram("vs", [NT, 128, D], BF16).ap()
        self.wg_hbm = dram("wgs", [128, self.DCH * I], BF16).ap()
        self.wu_hbm = dram("wus", [128, self.DCH * I], BF16).ap()
        self.wd_hbm = dram("wds", [128, self.ICH * D], BF16).ap()
        self.wa_hbm = dram("was", [128, 4 * self.DCH * D], BF16).ap()

        self.AB = Arena(nc, "AB", 86784, BF16)
        self.AF = Arena(nc, "AF", 9504, F32)
        self.c_eps = nc.alloc_sbuf_tensor("c_eps", [128, 1], F32).ap()
        self.c_b20 = nc.alloc_sbuf_tensor("c_b20", [128, 1], F32).ap()
        self.c_s32 = nc.alloc_sbuf_tensor("c_s32", [128, 1], F32).ap()
        self.c_one = nc.alloc_sbuf_tensor("c_one", [128, 1], F32).ap()
        self.c_mh = nc.alloc_sbuf_tensor("c_mh", [128, 1], F32).ap()
        self.tri = nc.alloc_sbuf_tensor("tri_sb", [128, 128], BF16).ap()
        self.ps = [nc.alloc_psum_tensor(f"ps{i}", [128, 512], F32) for i in range(8)]

    def v(self, emit, waits=(), inc=None, tag=""):
        return self.sc.op("vector", emit, waits, inc, tag)

    def scl(self, emit, waits=(), inc=None, tag=""):
        return self.sc.op("scalar", emit, waits, inc, tag)

    def gdma(self, emit, waits=(), inc=None, tag=""):
        return self.sc.op("gpsimd", emit, waits, inc, tag)

    def sy(self, emit, waits=(), inc=None, tag=""):
        return self.sc.op("sync", emit, waits, inc, tag)

    def t(self, emit, waits=(), inc=None, tag=""):
        return self.sc.op("tensor", emit, waits, inc, tag)

    def emit_quant(self, qin32, N, waits_src, out_q, rel_key=None, cmp_bf16=False):
        s = self.sc
        q = self.qb
        NB = N // 16
        NH = N // 2
        NBH = NB // 2
        pre = "qs"
        w_src = list(waits_src)
        H = ((0, NH, 0, NBH), (NH, N, NBH, NB))
        sp = self.qpar
        self.qpar ^= 1
        sgbuf = q["sg"][sp]
        sgg = self.sg_guard[sp] or []

        gtok = []
        for (lo, hi, _, _) in H:
            gtok.append(self.scl(lambda e, lo=lo, hi=hi: e.activation(
                out=sgbuf[:, lo:hi], in_=qin32[:, lo:hi], func=ACT.Sign),
                waits=w_src + sgg, inc=(pre + "_as", 1)))

        def VH(emitfn, dep=(), extra=()):
            toks = []
            for i, (lo, hi, blo, bhi) in enumerate(H):
                w = list(extra)
                if dep:
                    w.append(dep[i])
                toks.append(self.v(lambda e, lo=lo, hi=hi, blo=blo, bhi=bhi:
                                   emitfn(e, lo, hi, blo, bhi),
                                   waits=w, inc=(pre + "_v", 1)))
            return toks

        ab_ = []
        for i, (lo, hi, _, _) in enumerate(H):
            ab_.append(self.scl(lambda e, lo=lo, hi=hi: e.activation(
                out=qin32[:, lo:hi], in_=qin32[:, lo:hi], func=ACT.Abs),
                waits=[gtok[i]], inc=(pre + "_as", 1)))
        bm = VH(lambda e, lo, hi, blo, bhi: e.tensor_reduce(
            out=q["bm"][:, blo:bhi].rearrange("p (b i) -> p b i", i=1),
            in_=qin32[:, lo:hi].rearrange("p (b i) -> p b i", i=16),
            axis=AX.X, op=ALU.max), dep=ab_)
        s32 = VH(lambda e, lo, hi, blo, bhi: e.tensor_scalar(
            out=q["s32"][:, blo:bhi], in0=q["bm"][:, blo:bhi],
            scalar1=1.0 / 6.0, scalar2=1e-12, op0=ALU.mult, op1=ALU.max), dep=bm)
        r32 = VH(lambda e, lo, hi, blo, bhi: e.reciprocal(
            out=q["r32"][:, blo:bhi], in_=q["s32"][:, blo:bhi]), dep=s32)
        adst = q["axn"] if cmp_bf16 else qin32
        t1 = VH(lambda e, lo, hi, blo, bhi: e.tensor_tensor(
            out=adst[:, lo:hi].rearrange("p (b i) -> p b i", i=16),
            in0=qin32[:, lo:hi].rearrange("p (b i) -> p b i", i=16),
            in1=q["r32"][:, blo:bhi].rearrange("p (b i) -> p b i", i=1)
                .broadcast_to([128, NBH, 16]),
            op=ALU.mult), dep=r32)
        csrc = adst
        t2 = VH(lambda e, lo, hi, blo, bhi: e.tensor_single_scalar(
            out=q["t2"][:, lo:hi], in_=csrc[:, lo:hi], scalar=0.25, op=ALU.is_gt),
            dep=t1)
        cbp = t1
        for th in (0.75, 1.25, 1.75):
            cb = VH(lambda e, lo, hi, blo, bhi, th=th: e.tensor_single_scalar(
                out=q["cb"][:, lo:hi], in_=csrc[:, lo:hi], scalar=th, op=ALU.is_gt),
                dep=t1 if th == 0.75 else cbp)
            t2 = VH(lambda e, lo, hi, blo, bhi: e.tensor_add(
                out=q["t2"][:, lo:hi], in0=q["t2"][:, lo:hi], in1=q["cb"][:, lo:hi]),
                dep=[cb[0], cb[1]], extra=[t2[0], t2[1]])
            cbp = t2
        b1 = VH(lambda e, lo, hi, blo, bhi: e.tensor_single_scalar(
            out=q["cb"][:, lo:hi], in_=csrc[:, lo:hi], scalar=2.5, op=ALU.is_gt),
            dep=t2)
        b3 = VH(lambda e, lo, hi, blo, bhi: e.scalar_tensor_tensor(
            out=q["cb"][:, lo:hi], in0=csrc[:, lo:hi], scalar=3.5, in1=q["cb"][:, lo:hi],
            op0=ALU.is_gt, op1=ALU.add), dep=b1)
        eh = VH(lambda e, lo, hi, blo, bhi: e.tensor_single_scalar(
            out=q["t2"][:, lo:hi], in_=q["t2"][:, lo:hi], scalar=0.5, op=ALU.mult),
            dep=t2)
        e1 = VH(lambda e, lo, hi, blo, bhi: e.tensor_add(
            out=q["t2"][:, lo:hi], in0=q["t2"][:, lo:hi], in1=q["cb"][:, lo:hi]),
            dep=[b3[0], b3[1]], extra=[eh[0], eh[1]])
        c7 = VH(lambda e, lo, hi, blo, bhi: e.tensor_scalar(
            out=q["cb"][:, lo:hi], in0=csrc[:, lo:hi], scalar1=5.0, scalar2=2.0,
            op0=ALU.is_gt, op1=ALU.mult), dep=e1)
        if rel_key is not None:
            self.qrel[rel_key] = [c7[0], c7[1]]
        e2 = VH(lambda e, lo, hi, blo, bhi: e.tensor_add(
            out=q["t2"][:, lo:hi], in0=q["t2"][:, lo:hi], in1=q["cb"][:, lo:hi]),
            dep=[c7[0], c7[1]], extra=[e1[0], e1[1]])
        ss = VH(lambda e, lo, hi, blo, bhi: e.tensor_tensor(
            out=q["axn"][:, lo:hi].rearrange("p (b i) -> p b i", i=16),
            in0=sgbuf[:, lo:hi].rearrange("p (b i) -> p b i", i=16),
            in1=q["s32"][:, blo:bhi].rearrange("p (b i) -> p b i", i=1)
                .broadcast_to([128, NBH, 16]),
            op=ALU.mult), dep=e2, extra=[s32[0], s32[1]])
        self.sg_guard[sp] = [ss[0], ss[1]]
        (lo0, hi0, _, _), (lo1, hi1, _, _) = H
        tok0 = self.v(lambda e: e.tensor_tensor(
            out=out_q[:, lo0:hi0], in0=q["t2"][:, lo0:hi0], in1=q["axn"][:, lo0:hi0],
            op=ALU.mult), waits=[ss[0]] + list(self.qout_free), inc=(pre + "_v", 1))
        return self.v(lambda e: e.tensor_tensor(
            out=out_q[:, lo1:hi1], in0=q["t2"][:, lo1:hi1], in1=q["axn"][:, lo1:hi1],
            op=ALU.mult), waits=[ss[1], tok0], inc=(pre + "_q", 1))

    def emit_ln_stats(self, src, N, pre, use_expln, rcol=0):
        st = self.lnb["stats"][rcol % 4]
        rs = self.lnb["rs"][rcol % 4]
        s = self.sc
        nch = (N + 511) // 512
        prev_rs = max(0, s.count(pre + "_rs") - 1)

        w0 = list(self.lnb["waits"]) + ([(pre + "_rs", prev_rs)] if prev_rs else [])
        btok = []
        for c in range(nch):
            lo = c * 512
            hi = min(N, lo + 512)
            btok.append(self.v(lambda e, c=c, lo=lo, hi=hi: e.bn_stats(
                out=st[:, 6 * c:6 * c + 6], in_=src[:, lo:hi]), waits=w0,
                inc=("qs_v", 1)))
        ag = self.v(lambda e: e.bn_aggr(out=st[:, 12:14], in_=st[:, 0:6 * nch]),
                    waits=btok, inc=("qs_v", 1))
        ve = self.v(lambda e: e.tensor_scalar(out=st[:, 14:15], in0=st[:, 13:14],
                                              scalar1=1e-5, scalar2=None, op0=ALU.add),
                    waits=[ag], inc=("qs_v", 1))
        v1 = self.v(lambda e: e.reciprocal(out=st[:, 15:16], in_=st[:, 14:15]),
                    waits=[ve], inc=(pre + "_st", 1))
        rc = rs[:, 0:1]
        if use_expln:
            lt = self.scl(lambda e: e.activation(out=rc, in_=st[:, 13:14],
                                                 func=ACT.Ln, bias=self.c_eps,
                                                 scale=self.c_one),
                          waits=[v1, ("c_ini", 1)], inc=("qs_s", 1))
            r1 = self.scl(lambda e: e.activation(out=rc, in_=rc,
                                                 func=ACT.Exp, scale=self.c_mh),
                          waits=[lt], inc=(pre + "_rs", 1))
        else:
            r1 = self.scl(lambda e: e.activation(out=rc, in_=st[:, 15:16],
                                                 func=ACT.Sqrt),
                          waits=[v1, ("c_ini", 1)], inc=(pre + "_rs", 1))
        return r1

    def alloc_ffn_bufs(self, with_prep):
        AB, AF = self.AB, self.AF
        I, D, IH = self.I, self.D, self.IH
        b = {}
        b["wgqT"] = AB.alloc(self.DCH * I)
        b["wuqT"] = AB.alloc(self.DCH * I)
        b["wdqT"] = AB.alloc(self.ICH * D)
        b["txq"] = [AB.alloc(D) for _ in range(2)]
        b["hsg"] = AB.alloc(IH)
        b["qout"] = AB.alloc(IH)
        b["thq"] = [AB.alloc(IH), AB.alloc(IH)]
        b["xst"] = [AB.alloc(D) for _ in range(2)] if with_prep else None
        if with_prep:
            b["wst"] = [AF.alloc(IH) for _ in range(2)]
            b["wab"] = AB.alloc(D)
        q = {}
        q["axn"] = AB.alloc(IH)
        q["sg"] = [AB.alloc(IH), AB.alloc(IH)]
        q["t2"] = AB.alloc(IH)
        q["cb"] = AB.alloc(IH)
        _qs = AF.alloc(3 * (IH // 16))
        q["bm"] = _qs[:, 0:IH // 16]
        q["s32"] = _qs[:, IH // 16:2 * (IH // 16)]
        q["r32"] = _qs[:, 2 * (IH // 16):3 * (IH // 16)]
        q["in32"] = [AF.alloc(IH), AF.alloc(IH)]
        b["q"] = q
        b["xin"] = [AF.alloc(D) for _ in range(3)] if with_prep else None
        b["x2in"] = None if with_prep else [AB.alloc(D) for _ in range(3)]
        b["yst"] = None if with_prep else [AF.alloc(D) for _ in range(2)]
        stt_ = AF.alloc(128)
        b["stats"] = [stt_[:, 32 * i:32 * i + 16] for i in range(4)]
        b["rs"] = [stt_[:, 32 * i + 16:32 * i + 20] for i in range(4)]
        return b

    def emit_consts(self):
        self.v(lambda e: e.memset(self.c_eps, 1e-5))
        self.v(lambda e: e.memset(self.c_b20, -20.0))
        self.v(lambda e: e.memset(self.c_s32, 1.0 / 32.0))
        self.v(lambda e: e.memset(self.c_one, 1.0))
        self.v(lambda e: e.memset(self.c_mh, -0.5), inc=("c_ini", 1))
        self.sy(lambda e: e.dma_start(out=self.tri, in_=self.tri_in), inc=("tri_ld", 16))

    def emit_wchunk(self, kind, c, sub):
        s = self.sc
        fb = self.fb
        D, I, IH = self.D, self.I, self.IH
        j = self.wj % 2
        self.wj += 1
        wst = fb["wst"][j]
        wprev = self.qrel.get(("wst", j), [])
        if kind in ("g", "u"):
            N = D
            src_h = self.wg_in if kind == "g" else self.wu_in
            ld = self.gdma(lambda e: e.dma_start(out=wst[:, :N],
                                                 in_=src_h[c * 128:(c + 1) * 128, :]),
                           waits=wprev + [("w_ld", s.count("w_ld"))],
                           inc=("w_ld", 16))
        else:
            N = IH
            ld = self.gdma(lambda e: e.dma_start(
                out=wst[:, :N], in_=self.wd_in[c * 128:(c + 1) * 128, sub * IH:(sub + 1) * IH]),
                waits=wprev + [("w_ld", s.count("w_ld"))],
                inc=("w_ld", 16))
        wsb = fb["qout"]
        qt = self.emit_quant(wst[:, :N], N, [ld, ("w_t", s.count("w_t"))], wsb[:, :N],
                             rel_key=("wst", j))
        if kind in ("g", "u"):
            dst = fb["wgqT"] if kind == "g" else fb["wuqT"]
            out_ap = dst.rearrange("p (d i) -> p d i", i=I)[:, :, c * 128:(c + 1) * 128]
            in_ap = wsb[:, :N].rearrange("p (d r) -> p d r", r=128)
        else:
            dst = fb["wdqT"]
            out_ap = dst.rearrange("p (i d) -> p i d", d=D)[:, sub * 11:(sub + 1) * 11,
                                                           c * 128:(c + 1) * 128]
            in_ap = wsb[:, :N].rearrange("p (i r) -> p i r", r=128)
        tt = self.sy(lambda e: e.dma_start_transpose(out=out_ap, in_=in_ap),
                     waits=[qt, ("w_t", s.count("w_t"))], inc=("w_t", 16))
        self.qout_free = [tt]

    def emit_ffn_phase(self, first):
        s = self.sc
        NT = self.NT
        pre = "a" if first else "c"
        self.AB.mark()
        self.AF.mark()
        fb = self.alloc_ffn_bufs(with_prep=first)
        self.fb = fb
        self.qb = fb["q"]
        self.qpar = 0
        self.qinpar = 0
        self.sg_guard = [None, None]
        self.qrel = {}
        self.qout_free = []
        self.lnb = {"stats": fb["stats"], "rs": fb["rs"], "waits": []}
        if first:
            self.wj = 0
            order = ([("g", c, 0) for c in range(11)] + [("u", c, 0) for c in range(11)]
                     + [("g", c, 0) for c in range(11, 22)]
                     + [("u", c, 0) for c in range(11, 22)]
                     + [("d", c, sub) for c in range(8) for sub in range(2)])
            for (kind, c, sub) in order:
                self.emit_wchunk(kind, c, sub)
            self.w_thr = {"gu0": 16 * 22, "gu1": 16 * 44, "d": 16 * 60}
            wt_all = 16 * 60
            for (dst_h, src_a) in ((self.wg_hbm, fb["wgqT"]), (self.wu_hbm, fb["wuqT"]),
                                   (self.wd_hbm, fb["wdqT"])):
                self.gdma(lambda e, d=dst_h, a=src_a: e.dma_start(out=d, in_=a),
                          waits=[("w_t", wt_all)], inc=("w_sp", 16))
            D = self.D
            for (wi, src_h) in enumerate((self.wq_in, self.wk_in, self.wv_in, self.wfc_in)):
                for c in range(self.DCH):
                    j = self.wj % 2
                    self.wj += 1
                    wst = fb["wst"][j]
                    wprev = self.qrel.get(("wst", j), [])
                    ld = self.gdma(lambda e, sh=src_h, c=c, wst=wst: e.dma_start(
                        out=wst[:, :D], in_=sh[c * 128:(c + 1) * 128, :]),
                        waits=wprev + [("w_ld", s.count("w_ld"))], inc=("w_ld", 16))
                    cz = self.scl(lambda e, wst=wst: e.activation(
                        out=fb["wab"][:, :D], in_=wst[:, :D], func=ACT.Copy),
                        waits=[ld, ("wa_sp", s.count("wa_sp"))], inc=("w_ac", 1))
                    self.qrel[("wst", j)] = [cz]
                    self.gdma(lambda e, wi=wi, c=c: e.dma_start(
                        out=self.wa_hbm[:, (wi * self.DCH + c) * D:
                                        (wi * self.DCH + c + 1) * D],
                        in_=fb["wab"][:, :D]),
                        waits=[cz, ("wa_sp", s.count("wa_sp"))], inc=("wa_sp", 16))
        else:
            nld = 0
            early = [("b_qkmm", self.sc.count("b_qkmm")), ("b_vmm", self.sc.count("b_vmm")),
                     ("b_smm", self.sc.count("b_smm")), ("b_exp", self.sc.count("b_exp")),
                     ("b_att_done", self.sc.count("b_att_done")),
                     ("b_x2s", self.sc.count("b_x2s"))]
            for (src_h, dst_a) in ((self.wg_hbm, fb["wgqT"]), (self.wu_hbm, fb["wuqT"]),
                                   (self.wd_hbm, fb["wdqT"])):
                for piece in range(4):
                    n = src_h.shape[1] // 4
                    gate = early if nld < 3 else list(self.barrier_waits("gpsimd"))
                    self.gdma(lambda e, sh=src_h, da=dst_a, p=piece, n=n:
                              e.dma_start(out=da[:, p * n:(p + 1) * n],
                                          in_=sh[:, p * n:(p + 1) * n]),
                              waits=gate + [("w3_ld", self.sc.count("w3_ld"))],
                              inc=("w3_ld", 16))
                    nld += 1
            self.w_thr = {"gu0": 16 * nld, "gu1": 16 * nld, "d": 16 * nld}
            self.w_thr["sem"] = "w3_ld"

        self.emit_ffn_xpart(0, pre, first)
        self.emit_ffn_xrest(pre, first)
        for t in range(NT):
            self.emit_ffn_gu(t, 0, pre, "w_t" if first else "w3_ld")
            if t + 1 < NT:
                self.emit_ffn_xpart(t + 1, pre, first)
            self.emit_ffn_evhm(t, 0, pre)
            if t >= 1:
                self.emit_ffn_down(t - 1, pre, first, 0)
            if t + 1 < NT:
                self.emit_ffn_xrest(pre, first)
            self.emit_ffn_hqchain(pre)
            self.emit_ffn_gu(t, 1, pre, "w_t" if first else "w3_ld")
            self.emit_ffn_evhm(t, 1, pre)
            if t >= 1:
                self.emit_ffn_down(t - 1, pre, first, 1)
                self.emit_ffn_resid(t - 1, pre, first)
            self.emit_ffn_hqchain(pre)
        self.emit_ffn_down(NT - 1, pre, first, 0)
        self.emit_ffn_down(NT - 1, pre, first, 1)
        self.emit_ffn_resid(NT - 1, pre, first)
        for eng in Sched.ENGINES:
            bw = []
            if eng == "gpsimd":
                bw = [(pre + "_x1s", s.count(pre + "_x1s"))]
                if first:
                    bw.append(("w_sp", s.count("w_sp")))
            s.op(eng, lambda e: None, waits=bw,
                 inc=(("p1b_" if first else "p3b_") + eng, 1))
        self.AB.release()
        self.AF.release()

    def barrier_waits(self, eng, which="p2b_"):
        return [(which + e2, self.sc.count(which + e2)) for e2 in Sched.ENGINES]

    def emit_ffn_xpart(self, t, pre, first):
        s = self.sc
        fb = self.fb
        D = self.D
        if first:
            xsrc = fb["xin"][t % 3]
            ld = self.gdma(lambda e: e.dma_start(out=xsrc[:, :D],
                                                 in_=self.x_in[t * 128:(t + 1) * 128, :]),
                           waits=[(pre + "_x1", max(0, t - 2)),
                                  (pre + "_x0", s.count(pre + "_x0"))],
                           inc=(pre + "_x0", 16))
        else:
            xsrc = fb["x2in"][t % 3]
            ld = self.gdma(lambda e: e.dma_start(out=xsrc[:, :D],
                                                 in_=self.x2_hbm[t * 128:(t + 1) * 128, :]),
                           waits=[(pre + "_x1", max(0, t - 2)), ("b_x2s", 16 * (t + 1)),
                                  (pre + "_x0", s.count(pre + "_x0"))]
                           + (self.barrier_waits("gpsimd") if t < 3 else []),
                           inc=(pre + "_x0", 16))
        self.lnb["waits"] = [ld]
        r1 = self.emit_ln_stats(xsrc[:, :D], D, pre + "_ln", use_expln=False, rcol=t % 2)
        self.xp_state = (t, xsrc, r1)

    def emit_ffn_xrest(self, pre, first):
        s = self.sc
        fb = self.fb
        D = self.D
        (t, xsrc, r1) = self.xp_state
        st = fb["stats"][t % 2]
        qp = self.qinpar
        self.qinpar ^= 1
        qin = fb["q"]["in32"][qp]
        rcap = fb["rs"][t % 2][:, 0:1]
        ax = self.v(lambda e: e.tensor_scalar(out=qin[:, :D], in0=xsrc[:, :D],
                                              scalar1=st[:, 12:13], scalar2=rcap,
                                              op0=ALU.subtract, op1=ALU.mult),
                    waits=[r1] + self.qrel.get(("qin", qp), []),
                    inc=(pre + "_xn", 1))
        qx = self.emit_quant(qin[:, :D], D, [ax], fb["qout"][:, :D],
                             rel_key=("qin", qp))
        par = t % 2
        tt = self.sy(lambda e: e.dma_start_transpose(
            out=fb["txq"][par].rearrange("p (d r) -> p d r", r=128),
            in_=fb["qout"][:, :D].rearrange("p (d r) -> p d r", r=128)),
            waits=[qx, (pre + "_gu", max(0, 2 * (t - 1))),
                   (pre + "_txq", s.count(pre + "_txq"))], inc=(pre + "_txq", 16))
        self.qout_free = [tt]

    def emit_ffn_gu(self, t, h, pre, wsem):
        fb = self.fb
        IH = self.IH
        par = t % 2
        ncol = [512, 512, IH - 1024]
        wthr = self.w_thr["gu0"] if h == 0 else self.w_thr["gu1"]

        def gu(e):
            last = None
            for d in range(self.DCH):
                lhsT = fb["txq"][par][:, d * 128:(d + 1) * 128]
                for c in range(3):
                    w = ncol[c]
                    off = d * self.I + h * IH + c * 512
                    e.matmul(self.ps[c].ap()[:, :w], lhsT,
                             fb["wgqT"][:, off:off + w],
                             start=(d == 0), stop=(d == self.DCH - 1))
                    last = e.matmul(self.ps[3 + c].ap()[:, :w], lhsT,
                                    fb["wuqT"][:, off:off + w],
                                    start=(d == 0), stop=(d == self.DCH - 1))
            return last
        self.t(gu, waits=[(pre + "_txq", 16 * (t + 1)), (wsem, wthr),
                          (pre + "_ev", max(0, 2 * t + h)),
                          (pre + "_hm", max(0, 2 * t + h))], inc=(pre + "_gu", 1))

    def emit_ffn_evhm(self, t, h, pre):
        s = self.sc
        fb = self.fb
        IH = self.IH
        ncol = [512, 512, IH - 1024]

        def ev(e):
            last = None
            for c in range(3):
                w = ncol[c]
                last = e.activation(out=fb["hsg"][:, c * 512:c * 512 + w],
                                    in_=self.ps[c].ap()[:, :w], func=ACT.Silu)
            return last
        evt = self.sc.op("scalar", ev,
                         waits=[(pre + "_gu", 2 * t + h + 1), (pre + "_hm", max(0, 2 * t + h))],
                         inc=(pre + "_ev", 1))
        qp = self.qinpar
        self.qinpar ^= 1
        qin = fb["q"]["in32"][qp]

        def hmul(e):
            last = None
            for c in range(3):
                w = ncol[c]
                last = e.tensor_tensor(out=qin[:, c * 512:c * 512 + w],
                                       in0=fb["hsg"][:, c * 512:c * 512 + w],
                                       in1=self.ps[3 + c].ap()[:, :w], op=ALU.mult)
            return last
        hm = self.sc.op("vector", hmul,
                        waits=[evt] + self.qrel.get(("qin", qp), []),
                        inc=(pre + "_hm", 1))
        self.hq_state = (t, h, qp, qin, hm)

    def emit_ffn_hqchain(self, pre):
        s = self.sc
        fb = self.fb
        IH = self.IH
        (t, h, qp, qin, hm) = self.hq_state
        qh = self.emit_quant(qin[:, :IH], IH, [hm], fb["qout"][:, :IH],
                             rel_key=("qin", qp))
        tt = self.sy(lambda e: e.dma_start_transpose(
            out=fb["thq"][h].rearrange("p (i r) -> p i r", r=128),
            in_=fb["qout"][:, :IH].rearrange("p (i r) -> p i r", r=128)),
            waits=[qh, (pre + "_dn", max(0, t)),
                   (pre + "_thq", self.sc.count(pre + "_thq"))], inc=(pre + "_thq", 16))
        self.qout_free = [tt]

    def emit_ffn_down(self, t, pre, first, colh):
        fb = self.fb
        D = self.D
        wsem = "w_t" if first else "w3_ld"

        def dn(e):
            last = None
            for i in range(self.ICH):
                hh = 0 if i < 11 else 1
                ioff = (i - 11 * hh) * 128
                last = e.matmul(self.ps[6 + colh].ap()[:, :512],
                                fb["thq"][hh][:, ioff:ioff + 128],
                                fb["wdqT"][:, i * D + colh * 512: i * D + colh * 512 + 512],
                                start=(i == 0), stop=(i == self.ICH - 1))
            return last
        if colh == 0:
            self.t(dn, waits=[(pre + "_thq", 16 * (2 * t + 2)),
                              (wsem, self.w_thr["d"]),
                              (pre + "_x1", max(0, t))])
        else:
            self.t(dn, inc=(pre + "_dn", 1))

    def emit_ffn_resid(self, t, pre, first):
        s = self.sc
        fb = self.fb
        D = self.D
        par = t % 2
        xsrc = (fb["xin"] if first else fb["x2in"])[t % 3]
        dst = (fb["xst"] if first else fb["yst"])[par]
        out_hbm = self.x1_hbm if first else self.y_out
        self.v(lambda e: e.scalar_tensor_tensor(
            out=dst[:, 0:512], in0=self.ps[6].ap()[:, :512],
            scalar=0.5, in1=xsrc[:, 0:512], op0=ALU.mult, op1=ALU.add),
            waits=[(pre + "_dn", s.count(pre + "_dn")),
                   (pre + "_x1s", max(0, 16 * (t - 1)))])
        r2 = self.v(lambda e: e.scalar_tensor_tensor(
            out=dst[:, 512:1024], in0=self.ps[7].ap()[:, :512],
            scalar=0.5, in1=xsrc[:, 512:1024], op0=ALU.mult, op1=ALU.add),
            inc=(pre + "_x1", 1))
        self.gdma(lambda e: e.dma_start(out=out_hbm[t * 128:(t + 1) * 128, :],
                                        in_=dst[:, :D]),
                  waits=[r2, (pre + "_x1s", s.count(pre + "_x1s"))],
                  inc=(pre + "_x1s", 16))

    def finalize(self):
        self.gdma(lambda e: None, waits=[("c_x1s", self.sc.count("c_x1s"))])

    def alloc_attn_bufs(self):
        AB, AF = self.AB, self.AF
        D, QT = self.D, self.QT
        b = {}
        b["wq"] = AB.alloc(self.DCH * D)
        b["wk"] = AB.alloc(self.DCH * D)
        b["wv"] = AB.alloc(self.DCH * D)
        b["wfc"] = AB.alloc(self.DCH * D)
        b["xnT"] = AB.alloc(self.DCH * QT * 128)
        b["qT"] = AB.alloc(self.DCH * QT * 128)
        b["ktc"] = AB.alloc(self.DCH * QT * 128)
        b["vc"] = AB.alloc(QT * D)
        gkn = self.DCH * min(4, QT) * 128
        if 2 * gkn <= self.DCH * QT * 128:
            b["ktg"] = [b["xnT"][:, 0:gkn], b["xnT"][:, gkn:2 * gkn]]
        else:
            b["ktg"] = [AB.alloc(gkn) for _ in range(2)]
        b["vg"] = [AB.alloc(4 * D) for _ in range(2)]
        b["x1q"] = [AB.alloc(D) for _ in range(2)]
        b["x1r"] = [AB.alloc(D) for _ in range(2)]
        b["x2st"] = [AB.alloc(D) for _ in range(2)]
        b["att"] = [AB.alloc(512) for _ in range(3)]
        b["attT"] = [AB.alloc(512) for _ in range(3)]
        b["lnr"] = AB.alloc(D)
        b["xnb"] = [b["lnr"]]
        b["lnrT"] = [AB.alloc(D), AB.alloc(D)]
        b["gam"] = AF.alloc(D)
        b["bet"] = AF.alloc(D)
        b["bfc"] = AF.alloc(D)
        _st2 = AF.alloc(128)
        b["stats"] = [_st2[:, 32 * i:32 * i + 16] for i in range(4)]
        b["rs"] = [_st2[:, 32 * i + 16:32 * i + 20] for i in range(4)]
        b["wstg"] = [AF.alloc(D) for _ in range(2)]
        b["den"] = AF.alloc(2 * QT)
        b["dpart"] = AF.alloc(4)
        b["ov"] = AF.alloc(D)
        return b

    def emit_p2(self):
        s = self.sc
        NT, QT, NQ, D = self.NT, self.QT, self.NQ, self.D
        self.AB.mark()
        self.AF.mark()
        ab = self.alloc_attn_bufs()
        self.ab2 = ab
        self.qb = None
        self.lnb = {"stats": ab["stats"], "rs": ab["rs"], "waits": []}
        p1w = self.barrier_waits("gpsimd", "p1b_")

        for (wi, dst) in enumerate((ab["wq"], ab["wk"], ab["wv"], ab["wfc"])):
            self.gdma(lambda e, wi=wi, dst=dst: e.dma_start(
                out=dst, in_=self.wa_hbm[:, wi * self.DCH * self.D:
                                         (wi + 1) * self.DCH * self.D]),
                waits=p1w + [("wa_sp", 16 * 32), ("b_wld", s.count("b_wld"))],
                inc=("b_wld", 16))
        self.sc.op("scalar", lambda e: None,
                   waits=[("b_wld", s.count("b_wld"))], inc=("b_wc", 32))
        for (src_h, dst) in ((self.gam_in, ab["gam"]), (self.bet_in, ab["bet"]),
                             (self.bfc_in, ab["bfc"])):
            self.sy(lambda e, sh=src_h, dst=dst: e.dma_start(
                out=dst, in_=sh.broadcast_to([128, D])),
                waits=self.barrier_waits("sync", "p1b_") + [("b_cld", s.count("b_cld"))],
                inc=("b_cld", 16))

        self.pend_fin = []
        self.fc_pred = 0
        self.gr_last_av = {0: 0, 1: 0}
        for k in range(NQ):
            self.den_first = {tau: True for tau in range(self.QT)}
            self.o_first = {tau: [True, True] for tau in range(self.QT)}
            self.gr_free = {0: [], 1: []}
            self.emit_p2_qkv(k)
            self.emit_p2_attn(k)
            while self.pend_fin:
                self.emit_p2_finB()
        for eng in Sched.ENGINES:
            bw = []
            if eng == "gpsimd":
                bw = [(n, s.count(n)) for n in
                      ("b_ktsp", "b_vsp", "b_x2s", "b_gl", "b_x1l", "b_x1rl", "b_wld")]
            s.op(eng, lambda e: None, waits=bw, inc=("p2b_" + eng, 1))
        self.AB.release()
        self.AF.release()

    def emit_p2_qkv(self, k):
        s = self.sc
        ab = self.ab2
        D, QT = self.D, self.QT
        xnt_base = s.count("b_xnt")
        for tau in range(QT):
            t = k * QT + tau
            xb = ab["x1q"][tau % 2]
            ld = self.gdma(lambda e, t=t, xb=xb: e.dma_start(
                out=xb[:, :D], in_=self.x1_hbm[t * 128:(t + 1) * 128, :]),
                waits=[("a_x1s", 16 * self.NT), ("b_x1f", max(0, s.count("b_x1f") - 1)),
                       ("b_x1l", s.count("b_x1l"))],
                inc=("b_x1l", 16))
            self.lnb["waits"] = [ld]
            r1 = self.emit_ln_stats(xb[:, :D], D, "b_ln", use_expln=True, rcol=tau % 2)
            xn = ab["xnb"][0]
            st = ab["stats"][tau % 2]
            rcap = ab["rs"][tau % 2][:, 0:1]
            ax = self.v(lambda e, xb=xb, xn=xn, st=st, rcap=rcap: e.tensor_scalar(
                out=xn[:, :D], in0=xb[:, :D], scalar1=st[:, 12:13], scalar2=rcap,
                op0=ALU.subtract, op1=ALU.mult),
                waits=[r1, ("b_xnt", s.count("b_xnt")),
                       ("b_lnt", s.count("b_lnt"))],
                inc=("b_x1f", 1))
            self.sy(lambda e, xn=xn, tau=tau: e.dma_start_transpose(
                out=ab["xnT"].rearrange("p (d r) -> p d r", r=QT * 128)[:, :, tau * 128:(tau + 1) * 128],
                in_=xn[:, :D].rearrange("p (d r) -> p d r", r=128)),
                waits=[ax, ("b_xnt", s.count("b_xnt")),
                       ("b_qkmm", s.count("b_qkmm")), ("b_vmm", s.count("b_vmm")),
                       ("b_exp", s.count("b_exp")), ("b_gl", s.count("b_gl"))],
                inc=("b_xnt", 16))
        rhw = min(512, QT * 128)
        nrh = (QT * 128) // rhw
        qkctr = 0
        for rh in range(nrh):
            xnt_rh = xnt_base + 16 * (rh + 1) * (rhw // 128)
            for (wname, dsts, spill) in (("wq", "qT", False), ("wk", "ktc", True)):
                w = ab[wname]
                dst = ab[dsts]
                for dk in range(self.DCH):
                    pspar = qkctr % 2
                    qkctr += 1
                    def mm(e, w=w, dk=dk, rh=rh, pspar=pspar):
                        last = None
                        for d in range(self.DCH):
                            last = e.matmul(self.ps[pspar].ap()[:, :rhw],
                                            w[:, d * D + dk * 128: d * D + dk * 128 + 128],
                                            ab["xnT"][:, d * QT * 128 + rh * rhw:
                                                      d * QT * 128 + rh * rhw + rhw],
                                            start=(d == 0), stop=(d == self.DCH - 1))
                        return last
                    mt = self.t(mm, waits=[("b_xnt", xnt_rh), ("b_wc", 32),
                                           ("b_exp", s.count("b_exp")),
                                           ("b_qkev", max(0, s.count("b_qkev") - 1))],
                                inc=("b_qkmm", 1))
                    self.scl(lambda e, dst=dst, dk=dk, rh=rh, pspar=pspar: e.activation(
                        out=dst[:, dk * QT * 128 + rh * rhw: dk * QT * 128 + rh * rhw + rhw],
                        in_=self.ps[pspar].ap()[:, :rhw], func=ACT.Copy),
                        waits=[mt, ("b_att_done", s.count("b_att_done")),
                               ("b_ktsp", s.count("b_ktsp"))],
                        inc=("b_qkev", 1))
        for tau in range(QT):
            for vh in range(2):
                pspar = (tau * 2 + vh) % 2
                def mmv(e, tau=tau, vh=vh, pspar=pspar):
                    last = None
                    for d in range(self.DCH):
                        last = e.matmul(self.ps[2 + pspar].ap()[:, :512],
                                        ab["xnT"][:, d * QT * 128 + tau * 128:
                                                  d * QT * 128 + tau * 128 + 128],
                                        ab["wv"][:, d * D + vh * 512: d * D + vh * 512 + 512],
                                        start=(d == 0), stop=(d == self.DCH - 1))
                    return last
                mt = self.t(mmv, waits=[("b_xnt", xnt_base + 16 * (tau + 1)), ("b_wc", 32),
                                        ("b_oev", s.count("b_oev")),
                                        ("b_vev", max(0, s.count("b_vev") - 1))],
                            inc=("b_vmm", 1))
                self.scl(lambda e, tau=tau, vh=vh, pspar=pspar: e.activation(
                    out=ab["vc"][:, tau * D + vh * 512: tau * D + vh * 512 + 512],
                    in_=self.ps[2 + pspar].ap()[:, :512], func=ACT.Copy),
                    waits=[mt, ("b_att_done", s.count("b_att_done")),
                           ("b_vsp", s.count("b_vsp"))],
                    inc=("b_vev", 1))
        kev = s.count("b_qkev")
        vev = s.count("b_vev")
        self.gdma(lambda e, k=k: e.dma_start(out=self.kt_hbm[k], in_=ab["ktc"]),
                  waits=[("b_qkev", kev), ("b_ktsp", s.count("b_ktsp"))],
                  inc=("b_ktsp", 16))
        self.gdma(lambda e, k=k: e.dma_start(
            out=self.v_hbm[k * QT:(k + 1) * QT].rearrange("t p d -> p t d"),
            in_=ab["vc"].rearrange("p (t d) -> p t d", d=D)),
            waits=[("b_vev", vev), ("b_vsp", s.count("b_vsp"))], inc=("b_vsp", 16))

    def emit_p2_attn(self, k):
        s = self.sc
        ab = self.ab2
        D, QT = self.D, self.QT
        gsz = min(4, self.QT)
        ngran = (self.QT // gsz) * k
        for g in range(QT // 2):
            taus = (2 * g, 2 * g + 1)
            gtoks = []
            base_av = s.count("b_av")
            for gr in range(ngran):
                q, gi = gr // 2, gr % 2
                par = gr % 2
                wk = [("b_ktsp", 16 * (q + 1)), ("b_vsp", 16 * (q + 1))]
                if gr >= 2:
                    wk.append(("b_av", base_av + 2 * (gr - 2) + 2))
                else:
                    if self.gr_last_av[par]:
                        wk.append(("b_av", self.gr_last_av[par]))
                    wk.append(("b_qkmm", s.count("b_qkmm")))
                    wk.append(("b_vmm", s.count("b_vmm")))
                gw = gsz * 128
                l1 = self.gdma(lambda e, q=q, gi=gi, par=par, gw=gw: e.dma_start(
                    out=ab["ktg"][par][:, :self.DCH * gw].rearrange("p (d r) -> p d r", r=gw),
                    in_=self.kt_hbm[q].rearrange("p (d r) -> p d r", r=QT * 128)
                    [:, :, gi * gw:(gi + 1) * gw]),
                    waits=wk + [("b_gl", s.count("b_gl"))], inc=("b_gl", 16))
                l2 = self.gdma(lambda e, q=q, gi=gi, par=par: e.dma_start(
                    out=ab["vg"][par].rearrange("p (t d) -> p t d", d=D)[:, :gsz],
                    in_=self.v_hbm[q * QT + gi * gsz: q * QT + gi * gsz + gsz]
                    .rearrange("t p d -> p t d")),
                    waits=[("b_gl", s.count("b_gl"))], inc=("b_gl", 16))
                gtoks.append((l1, l2, par))
            chunks = []
            for ci in range(ngran):
                for tau in taus:
                    chunks.append((tau, "g", ci, gsz, False))
            loc = {tau: [] for tau in taus}
            for tau in taus:
                lo = 0
                while lo < tau:
                    wdt = min(4, tau - lo)
                    loc[tau].append((tau, "l", lo, wdt, False))
                    lo += wdt
                loc[tau].append((tau, "l", tau, 1, True))
            mx = max(len(loc[t]) for t in taus)
            for i in range(mx):
                for tau in taus:
                    if i < len(loc[tau]):
                        chunks.append(loc[tau][i])
            for a_ in range(ngran):
                self.gr_last_av[a_ % 2] = base_av + 2 * a_ + 2
            self.oev_gate = s.count("b_oev")
            self.av_base = s.count("b_av")
            pend = []
            seen_gr = set()
            for ci, ch in enumerate(chunks):
                if ch[1] == "g" and ch[2] not in seen_gr:
                    seen_gr.add(ch[2])
                    if ch[2] >= 2:
                        while len(pend) > 2:
                            self.emit_p2_av(*pend.pop(0))
                info = self.emit_p2_scores(k, g, ci, ch, gtoks)
                pend.append(info)
                if ci in (1, 3):
                    self.emit_p2_finB()
                if len(pend) > 3:
                    self.emit_p2_av(*pend.pop(0))
            for p_ in pend:
                self.emit_p2_av(*p_)
            for tau in taus:
                self.emit_p2_fin(k, tau)
            if os.environ.get("MK_NODEFER") == "1":
                while self.pend_fin:
                    self.emit_p2_finB()

    def emit_p2_scores(self, k, g, ci, ch, gtoks):
        s = self.sc
        ab = self.ab2
        D, QT = self.D, self.QT
        (tau, kind, a, wt, diag) = ch
        w = wt * 128
        par = ci % 2
        apar = ci % 3
        gpar = gtoks[a][2] if kind == "g" else 0

        def mm(e):
            last = None
            for dk in range(self.DCH):
                lhsT = ab["qT"][:, dk * QT * 128 + tau * 128: dk * QT * 128 + tau * 128 + 128]
                if kind == "g":
                    gw = min(4, self.QT) * 128
                    rhs = ab["ktg"][gpar][:, dk * gw: dk * gw + w]
                else:
                    rhs = ab["ktc"][:, dk * QT * 128 + a * 128: dk * QT * 128 + a * 128 + w]
                last = e.matmul(self.ps[par].ap()[:, :w], lhsT, rhs,
                                start=(dk == 0), stop=(dk == self.DCH - 1))
            return last
        waits = [("b_exp", max(0, s.count("b_exp") - 1)),
                 ("b_qkev", s.count("b_qkev"))]
        if kind == "g":
            waits.append(gtoks[a][0])
        mt = self.t(mm, waits=waits, inc=("b_smm", 1))
        att = ab["att"][apar]
        dguard = ("b_den", max(0, s.count("b_den") - 1))
        if not diag:
            ex = self.scl(lambda e: e.activation(
                out=att[:, :w], in_=self.ps[par].ap()[:, :w], func=ACT.Exp,
                bias=self.c_b20, scale=self.c_s32, accum_out=ab["dpart"][:, par:par + 1]),
                waits=[mt, ("c_ini", 1), dguard,
                       ("b_att_t", max(0, s.count("b_att_t") - 32))],
                inc=("b_exp", 1))
            self.v(self.den_update_op(tau, par),
                   waits=[ex, ("b_oev", self.oev_gate)], inc=("b_den", 1))
            mk_tok = ex
        else:
            ex = self.scl(lambda e: e.activation(
                out=att[:, :w], in_=self.ps[par].ap()[:, :w], func=ACT.Exp,
                bias=self.c_b20, scale=self.c_s32),
                waits=[mt, ("c_ini", 1), dguard,
                       ("b_att_t", max(0, s.count("b_att_t") - 32))],
                inc=("b_exp", 1))
            mk_tok = self.v(lambda e: e.tensor_tensor(out=att[:, :w], in0=att[:, :w],
                                                      in1=self.tri, op=ALU.mult),
                            waits=[ex, ("tri_ld", 16)], inc=("b_msk", 1))
            rr = self.v(lambda e: e.tensor_reduce(
                out=ab["dpart"][:, par:par + 1].rearrange("p (a b) -> p a b", b=1),
                in_=att[:, :w].rearrange("p (a b) -> p a b", b=w),
                axis=AX.X, op=ALU.add), waits=[mk_tok], inc=("qs_v", 1))
            self.v(self.den_update_op(tau, par),
                   waits=[rr, ("b_oev", self.oev_gate)], inc=("b_den", 1))
        tr = self.sy(lambda e: e.dma_start_transpose(
            out=ab["attT"][apar][:, :w].rearrange("p (c r) -> p c r", r=128),
            in_=att[:, :w].rearrange("p (c r) -> p c r", r=128)),
            waits=[mk_tok, ("b_av", max(self.av_base, self.av_base + ci - 2)),
                   ("b_att_t", s.count("b_att_t"))],
            inc=("b_att_t", 16))
        vtok = gtoks[a][1] if kind == "g" else None
        return (ch, apar, gpar, tr, vtok)

    def den_update_op(self, tau, par):
        first = self.den_first[tau]
        self.den_first[tau] = False
        dst = self.ab2["den"][:, tau:tau + 1]
        srcp = self.ab2["dpart"][:, par:par + 1]
        if first:
            return lambda e: e.tensor_copy(out=dst, in_=srcp)
        return lambda e: e.tensor_tensor(out=dst, in0=dst, in1=srcp, op=ALU.add)

    def emit_p2_av(self, ch, par, gpar, tr, vtok=None):
        s = self.sc
        ab = self.ab2
        D = self.D
        (tau, kind, a, wt, diag) = ch
        obank = 2 + 2 * (tau % 2)

        firsts = (self.o_first[tau][0], self.o_first[tau][1])
        self.o_first[tau][0] = False
        self.o_first[tau][1] = False

        def mm(e):
            last = None
            for vh in range(2):
                for mt_ in range(wt):
                    lhsT = ab["attT"][par][:, mt_ * 128: mt_ * 128 + 128]
                    if kind == "g":
                        rhs = ab["vg"][gpar][:, mt_ * D + vh * 512: mt_ * D + vh * 512 + 512]
                    else:
                        rhs = ab["vc"][:, (a + mt_) * D + vh * 512: (a + mt_) * D + vh * 512 + 512]
                    last = e.matmul(self.ps[obank + vh].ap()[:, :512], lhsT, rhs,
                                    start=(firsts[vh] and mt_ == 0),
                                    stop=(diag and mt_ == wt - 1))
            return last
        w_ = [tr, ("b_oev", self.oev_gate)]
        if vtok is not None:
            w_.append(vtok)
        tok = self.t(mm, waits=w_, inc=("b_av", 1))
        if kind == "g":
            self.gr_free[gpar] = [tok]

    def emit_p2_fin(self, k, tau):
        s = self.sc
        ab = self.ab2
        D, QT = self.D, self.QT
        t = k * QT + tau
        obank = 2 + 2 * (tau % 2)
        den_n = s.count("b_den")
        par = tau % 2
        x1r = ab["x1r"][par]
        ld = self.gdma(lambda e: e.dma_start(out=x1r[:, :D],
                                             in_=self.x1_hbm[t * 128:(t + 1) * 128, :]),
                       waits=[("b_x1rl", s.count("b_x1rl")),
                              ("b_x2w", max(0, s.count("b_x2w") - 1))],
                       inc=("b_x1rl", 16))
        rd = self.v(lambda e: e.reciprocal(out=ab["den"][:, QT + tau:QT + tau + 1],
                                           in_=ab["den"][:, tau:tau + 1]),
                    waits=[("b_den", den_n), ("b_av", s.count("b_av"))],
                    inc=("qs_v", 1))
        rden = ab["den"][:, QT + tau:QT + tau + 1]
        e1 = self.v(lambda e: e.tensor_scalar(out=ab["ov"][:, 0:512],
                                              in0=self.ps[obank].ap()[:, :512],
                                              scalar1=rden, scalar2=None, op0=ALU.mult),
                    waits=[rd], inc=("b_oev", 1))
        e2 = self.v(lambda e: e.tensor_scalar(out=ab["ov"][:, 512:1024],
                                              in0=self.ps[obank + 1].ap()[:, :512],
                                              scalar1=rden, scalar2=None, op0=ALU.mult),
                    waits=[e1], inc=("b_oev", 1))
        av = self.v(lambda e: e.tensor_tensor(
            out=ab["ov"][:, :D], in0=ab["ov"][:, :D],
            in1=ab["vc"][:, tau * D:(tau + 1) * D], op=ALU.add),
            waits=[e2], inc=("qs_v", 1))
        self.lnb["waits"] = [av]
        r1 = self.emit_ln_stats(ab["ov"][:, :D], D, "b_ln2", use_expln=True,
                                rcol=2 + tau % 2)
        stt_ = ab["stats"][2 + tau % 2]
        rcap = ab["rs"][2 + tau % 2][:, 0:1]
        n1 = self.v(lambda e, stt_=stt_, rcap=rcap: e.tensor_scalar(
            out=ab["ov"][:, :D], in0=ab["ov"][:, :D],
            scalar1=stt_[:, 12:13], scalar2=rcap,
            op0=ALU.subtract, op1=ALU.mult),
                    waits=[r1], inc=("qs_v", 1))
        n2 = self.v(lambda e: e.tensor_tensor(out=ab["ov"][:, :D], in0=ab["ov"][:, :D],
                                              in1=ab["gam"][:, :D], op=ALU.mult),
                    waits=[n1, ("b_cld", 48)], inc=("qs_v", 1))
        n3 = self.v(lambda e: e.tensor_tensor(out=ab["lnr"][:, :D], in0=ab["ov"][:, :D],
                                              in1=ab["bet"][:, :D], op=ALU.add),
                    waits=[n2, ("b_lnt", s.count("b_lnt")),
                           ("b_xnt", s.count("b_xnt"))], inc=("qs_v", 1))
        lt = ab["lnrT"][par]
        tr = self.sy(lambda e: e.dma_start_transpose(
            out=lt.rearrange("p (d r) -> p d r", r=128),
            in_=ab["lnr"][:, :D].rearrange("p (d r) -> p d r", r=128)),
            waits=[n3, ("b_lnt", s.count("b_lnt")),
                   ("b_fc", self.fc_pred)],
            inc=("b_lnt", 16))
        self.fc_pred += 1
        self.pend_fin.append((k, tau, t, tr, ld))

    def emit_p2_finB(self):
        if not self.pend_fin:
            return
        (k, tau, t, tr, ld) = self.pend_fin.pop(0)
        s = self.sc
        ab = self.ab2
        D, QT = self.D, self.QT
        par = tau % 2
        lt = ab["lnrT"][par]

        def fc(e):
            last = None
            for oh in range(2):
                for d in range(self.DCH):
                    last = e.matmul(self.ps[6 + oh].ap()[:, :512],
                                    lt[:, d * 128: d * 128 + 128],
                                    ab["wfc"][:, d * D + oh * 512: d * D + oh * 512 + 512],
                                    start=(d == 0), stop=(d == self.DCH - 1))
            return last
        ft = self.t(fc, waits=[tr, ("b_wc", 32), ("b_x2w", s.count("b_x2w"))],
                    inc=("b_fc", 1))
        x2 = ab["x2st"][par]
        w1 = self.v(lambda e: e.scalar_tensor_tensor(
            out=x2[:, 0:512], in0=self.ps[6].ap()[:, :512], scalar=1.0,
            in1=ab["bfc"][:, 0:512], op0=ALU.mult, op1=ALU.add),
            waits=[ft, ld, ("b_x2s", max(0, 16 * (t - 1)))], inc=("qs_v", 1))
        w2 = self.v(lambda e: e.scalar_tensor_tensor(
            out=x2[:, 512:1024], in0=self.ps[7].ap()[:, :512], scalar=1.0,
            in1=ab["bfc"][:, 512:1024], op0=ALU.mult, op1=ALU.add),
            waits=[w1], inc=("qs_v", 1))
        w3 = self.v(lambda e: e.tensor_tensor(out=x2[:, :D], in0=x2[:, :D],
                                              in1=ab["x1r"][par][:, :D], op=ALU.add),
                    waits=[w2], inc=("b_x2w", 1))
        self.gdma(lambda e: e.dma_start(out=self.x2_hbm[t * 128:(t + 1) * 128, :],
                                        in_=x2[:, :D]),
                  waits=[w3, ("b_x2s", s.count("b_x2s"))], inc=("b_x2s", 16))
        self.v(lambda e: None, inc=("b_att_done", 1))

import ml_dtypes
from concourse.bass_utils import run_bass_kernel_spmd

_NT, _QT, _D, _I = 32, 8, 1024, 2816


def _build():
    nc = bass.Bass("TRN2", target_bir_lowering=False)
    mk = MK(nc, NT=_NT, QT=_QT, D=_D, I=_I)
    mk.emit_consts()
    mk.emit_ffn_phase(first=True)
    mk.emit_p2()
    mk.emit_ffn_phase(first=False)
    mk.sc.finalize_waits()
    with nc.Block() as blk:
        @blk.sync
        def _(e):
            mk.sc.emit_stream("sync", e)

        @blk.scalar
        def _(e):
            mk.sc.emit_stream("scalar", e)

        @blk.vector
        def _(e):
            mk.sc.emit_stream("vector", e)

        @blk.tensor
        def _(e):
            mk.sc.emit_stream("tensor", e)

        @blk.gpsimd
        def _(e):
            mk.sc.emit_stream("gpsimd", e)
    return nc


def kernel(L0_attn_x, L0_attn_Wq, L0_attn_Wk, L0_attn_Wv, L0_attn_Wfc,
           L0_attn_bfc, L0_attn_gamma, L0_attn_beta,
           L0_ffn_w_gate, L0_ffn_w_up, L0_ffn_w_down):
    nc = _build()
    x = np.ascontiguousarray(L0_attn_x)
    tri = np.tril(np.ones((128, 128), np.float32)).astype(ml_dtypes.bfloat16)
    base = {
        "Wq": np.ascontiguousarray(L0_attn_Wq),
        "Wk": np.ascontiguousarray(L0_attn_Wk),
        "Wv": np.ascontiguousarray(L0_attn_Wv),
        "Wfc": np.ascontiguousarray(L0_attn_Wfc),
        "bfc": np.ascontiguousarray(L0_attn_bfc).reshape(1, _D),
        "gamma": np.ascontiguousarray(L0_attn_gamma).reshape(1, _D),
        "beta": np.ascontiguousarray(L0_attn_beta).reshape(1, _D),
        "w_gate": np.ascontiguousarray(L0_ffn_w_gate),
        "w_up": np.ascontiguousarray(L0_ffn_w_up),
        "w_down": np.ascontiguousarray(L0_ffn_w_down),
        "tri": tri,
    }
    in_maps = [dict(base, x=np.ascontiguousarray(x[b])) for b in range(8)]
    res = run_bass_kernel_spmd(nc, in_maps, core_ids=list(range(8)))
    out = np.stack([res.results[b]["y"] for b in range(8)], axis=0)
    return out.astype(np.float32)


# version 22
# speedup vs baseline: 1.0157x; 1.0006x over previous
import os
import numpy as np
import concourse.bass as bass
import concourse.mybir as mybir

SIM_DRAIN = False
SCL_DRAIN = False

F32 = mybir.dt.float32
BF16 = mybir.dt.bfloat16
AX = mybir.AxisListType
ALU = mybir.AluOpType
ACT = mybir.ActivationFunctionType


class Sched:
    ENGINES = ("sync", "scalar", "vector", "tensor", "gpsimd")

    def __init__(self, nc):
        self.nc = nc
        self.streams = {e: [] for e in self.ENGINES}
        self.sem_handles = {}
        self.sem_counts = {}
        self._sem_ctx = []

    def sem(self, name):
        if name not in self.sem_handles:
            cm = self.nc.semaphore(name)
            h = cm.__enter__()
            self._sem_ctx.append(cm)
            self.sem_handles[name] = h
            self.sem_counts[name] = 0
        return self.sem_handles[name]

    def op(self, eng, emit, waits=(), inc=None, tag=""):
        n_after = None
        if inc is not None:
            name, amt = inc
            self.sem(name)
            self.sem_counts[name] += amt
            n_after = self.sem_counts[name]
        for (s, v) in waits:
            self.sem(s)
        self.streams[eng].append((emit, tuple(waits), inc, tag))
        return (inc[0], n_after) if inc is not None else None

    def count(self, name):
        self.sem(name)
        return self.sem_counts[name]

    def emit_stream(self, eng_name, e):
        for (emit, waits, inc, tag) in self.streams[eng_name]:
            for (s, v) in waits:
                if v > 0:
                    e.wait_ge(self.sem_handles[s], v)
            ins = emit(e)
            if inc is not None:
                name, amt = inc
                if ins is None:
                    e.sem_inc(self.sem_handles[name], amt)
                else:
                    ins.then_inc(self.sem_handles[name], amt)

    def n_ops(self):
        return {k: len(v) for k, v in self.streams.items()}

    def finalize_waits(self, eng="gpsimd"):
        names = [n for n, c in self.sem_counts.items() if c > 0]
        def w(e):
            for n in names:
                e.wait_ge(self.sem_handles[n], self.sem_counts[n])
            return None
        self.streams[eng].append((w, (), None, "final_waits"))


class Arena:
    def __init__(self, nc, name, n_elems, dtype):
        self.t = nc.alloc_sbuf_tensor(name, [128, n_elems], dtype)
        self.n = n_elems
        self.off = 0
        self.marks = []
        self.peak = 0

    def alloc(self, n, align=32):
        o = (self.off + align - 1) // align * align
        assert o + n <= self.n, f"arena {self.t.name} overflow: {o}+{n} > {self.n}"
        self.off = o + n
        self.peak = max(self.peak, self.off)
        return self.t.ap()[:, o:o + n]

    def mark(self):
        self.marks.append(self.off)

    def release(self):
        self.off = self.marks.pop()


class MK:
    def __init__(self, nc, NT=32, QT=8, D=1024, I=2816, expose=()):
        self.nc = nc
        self.NT = NT
        self.QT = QT
        self.NQ = NT // QT
        self.D = D
        self.DCH = D // 128
        self.I = I
        self.ICH = I // 128
        self.IH = I // 2
        self.sc = Sched(nc)
        self.expose = set(expose)

        def dram(name, shape, dt, kind="Internal"):
            if kind == "Internal" and name in self.expose:
                kind = "ExternalOutput"
            return nc.dram_tensor(name, shape, dt, kind=kind)
        self.x_in = dram("x", [NT * 128, D], F32, kind="ExternalInput").ap()
        self.wq_in = dram("Wq", [D, D], F32, kind="ExternalInput").ap()
        self.wk_in = dram("Wk", [D, D], F32, kind="ExternalInput").ap()
        self.wv_in = dram("Wv", [D, D], F32, kind="ExternalInput").ap()
        self.wfc_in = dram("Wfc", [D, D], F32, kind="ExternalInput").ap()
        self.bfc_in = dram("bfc", [1, D], F32, kind="ExternalInput").ap()
        self.gam_in = dram("gamma", [1, D], F32, kind="ExternalInput").ap()
        self.bet_in = dram("beta", [1, D], F32, kind="ExternalInput").ap()
        self.wg_in = dram("w_gate", [I, D], F32, kind="ExternalInput").ap()
        self.wu_in = dram("w_up", [I, D], F32, kind="ExternalInput").ap()
        self.wd_in = dram("w_down", [D, I], F32, kind="ExternalInput").ap()
        self.tri_in = dram("tri", [128, 128], BF16, kind="ExternalInput").ap()
        self.y_out = dram("y", [NT * 128, D], F32, kind="ExternalOutput").ap()
        self.x1_hbm = dram("x1s", [NT * 128, D], BF16).ap()
        self.x2_hbm = dram("x2s", [NT * 128, D], BF16).ap()
        self.kt_hbm = dram("kts", [self.NQ, 128, self.DCH * QT * 128], BF16).ap()
        self.v_hbm = dram("vs", [NT, 128, D], BF16).ap()
        self.wg_hbm = dram("wgs", [128, self.DCH * I], BF16).ap()
        self.wu_hbm = dram("wus", [128, self.DCH * I], BF16).ap()
        self.wd_hbm = dram("wds", [128, self.ICH * D], BF16).ap()
        self.wa_hbm = dram("was", [128, 4 * self.DCH * D], BF16).ap()

        self.AB = Arena(nc, "AB", 86784, BF16)
        self.AF = Arena(nc, "AF", 9504, F32)
        self.c_eps = nc.alloc_sbuf_tensor("c_eps", [128, 1], F32).ap()
        self.c_b20 = nc.alloc_sbuf_tensor("c_b20", [128, 1], F32).ap()
        self.c_s32 = nc.alloc_sbuf_tensor("c_s32", [128, 1], F32).ap()
        self.c_one = nc.alloc_sbuf_tensor("c_one", [128, 1], F32).ap()
        self.c_mh = nc.alloc_sbuf_tensor("c_mh", [128, 1], F32).ap()
        self.tri = nc.alloc_sbuf_tensor("tri_sb", [128, 128], BF16).ap()
        self.ps = [nc.alloc_psum_tensor(f"ps{i}", [128, 512], F32) for i in range(8)]

    def v(self, emit, waits=(), inc=None, tag=""):
        return self.sc.op("vector", emit, waits, inc, tag)

    def scl(self, emit, waits=(), inc=None, tag=""):
        return self.sc.op("scalar", emit, waits, inc, tag)

    def gdma(self, emit, waits=(), inc=None, tag=""):
        return self.sc.op("gpsimd", emit, waits, inc, tag)

    def sy(self, emit, waits=(), inc=None, tag=""):
        return self.sc.op("sync", emit, waits, inc, tag)

    def t(self, emit, waits=(), inc=None, tag=""):
        return self.sc.op("tensor", emit, waits, inc, tag)

    def emit_quant(self, qin32, N, waits_src, out_q, rel_key=None, cmp_bf16=False):
        s = self.sc
        q = self.qb
        NB = N // 16
        NH = N // 2
        NBH = NB // 2
        pre = "qs"
        w_src = list(waits_src)
        H = ((0, NH, 0, NBH), (NH, N, NBH, NB))
        sp = self.qpar
        self.qpar ^= 1
        sgbuf = q["sg"][sp]
        sgg = self.sg_guard[sp] or []

        gtok = []
        for (lo, hi, _, _) in H:
            gtok.append(self.scl(lambda e, lo=lo, hi=hi: e.activation(
                out=sgbuf[:, lo:hi], in_=qin32[:, lo:hi], func=ACT.Sign),
                waits=w_src + sgg, inc=(pre + "_as", 1)))

        def VH(emitfn, dep=(), extra=()):
            toks = []
            for i, (lo, hi, blo, bhi) in enumerate(H):
                w = list(extra)
                if dep:
                    w.append(dep[i])
                toks.append(self.v(lambda e, lo=lo, hi=hi, blo=blo, bhi=bhi:
                                   emitfn(e, lo, hi, blo, bhi),
                                   waits=w, inc=(pre + "_v", 1)))
            return toks

        ab_ = []
        for i, (lo, hi, _, _) in enumerate(H):
            ab_.append(self.scl(lambda e, lo=lo, hi=hi: e.activation(
                out=qin32[:, lo:hi], in_=qin32[:, lo:hi], func=ACT.Abs),
                waits=[gtok[i]], inc=(pre + "_as", 1)))
        bm = VH(lambda e, lo, hi, blo, bhi: e.tensor_reduce(
            out=q["bm"][:, blo:bhi].rearrange("p (b i) -> p b i", i=1),
            in_=qin32[:, lo:hi].rearrange("p (b i) -> p b i", i=16),
            axis=AX.X, op=ALU.max), dep=ab_)
        s32 = VH(lambda e, lo, hi, blo, bhi: e.tensor_scalar(
            out=q["s32"][:, blo:bhi], in0=q["bm"][:, blo:bhi],
            scalar1=1.0 / 6.0, scalar2=1e-12, op0=ALU.mult, op1=ALU.max), dep=bm)
        r32 = VH(lambda e, lo, hi, blo, bhi: e.reciprocal(
            out=q["r32"][:, blo:bhi], in_=q["s32"][:, blo:bhi]), dep=s32)
        adst = q["axn"] if cmp_bf16 else qin32
        t1 = VH(lambda e, lo, hi, blo, bhi: e.tensor_tensor(
            out=adst[:, lo:hi].rearrange("p (b i) -> p b i", i=16),
            in0=qin32[:, lo:hi].rearrange("p (b i) -> p b i", i=16),
            in1=q["r32"][:, blo:bhi].rearrange("p (b i) -> p b i", i=1)
                .broadcast_to([128, NBH, 16]),
            op=ALU.mult), dep=r32)
        csrc = adst
        t2 = VH(lambda e, lo, hi, blo, bhi: e.tensor_single_scalar(
            out=q["t2"][:, lo:hi], in_=csrc[:, lo:hi], scalar=0.25, op=ALU.is_gt),
            dep=t1)
        cbp = t1
        for th in (0.75, 1.25, 1.75):
            cb = VH(lambda e, lo, hi, blo, bhi, th=th: e.tensor_single_scalar(
                out=q["cb"][:, lo:hi], in_=csrc[:, lo:hi], scalar=th, op=ALU.is_gt),
                dep=t1 if th == 0.75 else cbp)
            t2 = VH(lambda e, lo, hi, blo, bhi: e.tensor_add(
                out=q["t2"][:, lo:hi], in0=q["t2"][:, lo:hi], in1=q["cb"][:, lo:hi]),
                dep=[cb[0], cb[1]], extra=[t2[0], t2[1]])
            cbp = t2
        b1 = VH(lambda e, lo, hi, blo, bhi: e.tensor_single_scalar(
            out=q["cb"][:, lo:hi], in_=csrc[:, lo:hi], scalar=2.5, op=ALU.is_gt),
            dep=t2)
        b3 = VH(lambda e, lo, hi, blo, bhi: e.scalar_tensor_tensor(
            out=q["cb"][:, lo:hi], in0=csrc[:, lo:hi], scalar=3.5, in1=q["cb"][:, lo:hi],
            op0=ALU.is_gt, op1=ALU.add), dep=b1)
        eh = VH(lambda e, lo, hi, blo, bhi: e.tensor_single_scalar(
            out=q["t2"][:, lo:hi], in_=q["t2"][:, lo:hi], scalar=0.5, op=ALU.mult),
            dep=t2)
        e1 = VH(lambda e, lo, hi, blo, bhi: e.tensor_add(
            out=q["t2"][:, lo:hi], in0=q["t2"][:, lo:hi], in1=q["cb"][:, lo:hi]),
            dep=[b3[0], b3[1]], extra=[eh[0], eh[1]])
        c7 = VH(lambda e, lo, hi, blo, bhi: e.tensor_scalar(
            out=q["cb"][:, lo:hi], in0=csrc[:, lo:hi], scalar1=5.0, scalar2=2.0,
            op0=ALU.is_gt, op1=ALU.mult), dep=e1)
        if rel_key is not None:
            self.qrel[rel_key] = [c7[0], c7[1]]
        e2 = VH(lambda e, lo, hi, blo, bhi: e.tensor_add(
            out=q["t2"][:, lo:hi], in0=q["t2"][:, lo:hi], in1=q["cb"][:, lo:hi]),
            dep=[c7[0], c7[1]], extra=[e1[0], e1[1]])
        ss = VH(lambda e, lo, hi, blo, bhi: e.tensor_tensor(
            out=q["axn"][:, lo:hi].rearrange("p (b i) -> p b i", i=16),
            in0=sgbuf[:, lo:hi].rearrange("p (b i) -> p b i", i=16),
            in1=q["s32"][:, blo:bhi].rearrange("p (b i) -> p b i", i=1)
                .broadcast_to([128, NBH, 16]),
            op=ALU.mult), dep=e2, extra=[s32[0], s32[1]])
        self.sg_guard[sp] = [ss[0], ss[1]]
        (lo0, hi0, _, _), (lo1, hi1, _, _) = H
        tok0 = self.v(lambda e: e.tensor_tensor(
            out=out_q[:, lo0:hi0], in0=q["t2"][:, lo0:hi0], in1=q["axn"][:, lo0:hi0],
            op=ALU.mult), waits=[ss[0]] + list(self.qout_free), inc=(pre + "_v", 1))
        return self.v(lambda e: e.tensor_tensor(
            out=out_q[:, lo1:hi1], in0=q["t2"][:, lo1:hi1], in1=q["axn"][:, lo1:hi1],
            op=ALU.mult), waits=[ss[1], tok0], inc=(pre + "_q", 1))

    def emit_ln_stats(self, src, N, pre, use_expln, rcol=0):
        st = self.lnb["stats"][rcol % 4]
        rs = self.lnb["rs"][rcol % 4]
        s = self.sc
        nch = (N + 511) // 512
        prev_rs = max(0, s.count(pre + "_rs") - 1)

        w0 = list(self.lnb["waits"]) + ([(pre + "_rs", prev_rs)] if prev_rs else [])
        btok = []
        for c in range(nch):
            lo = c * 512
            hi = min(N, lo + 512)
            btok.append(self.v(lambda e, c=c, lo=lo, hi=hi: e.bn_stats(
                out=st[:, 6 * c:6 * c + 6], in_=src[:, lo:hi]), waits=w0,
                inc=("qs_v", 1)))
        ag = self.v(lambda e: e.bn_aggr(out=st[:, 12:14], in_=st[:, 0:6 * nch]),
                    waits=btok, inc=("qs_v", 1))
        ve = self.v(lambda e: e.tensor_scalar(out=st[:, 14:15], in0=st[:, 13:14],
                                              scalar1=1e-5, scalar2=None, op0=ALU.add),
                    waits=[ag], inc=("qs_v", 1))
        v1 = self.v(lambda e: e.reciprocal(out=st[:, 15:16], in_=st[:, 14:15]),
                    waits=[ve], inc=(pre + "_st", 1))
        rc = rs[:, 0:1]
        if use_expln:
            lt = self.scl(lambda e: e.activation(out=rc, in_=st[:, 13:14],
                                                 func=ACT.Ln, bias=self.c_eps,
                                                 scale=self.c_one),
                          waits=[v1, ("c_ini", 1)], inc=("qs_s", 1))
            r1 = self.scl(lambda e: e.activation(out=rc, in_=rc,
                                                 func=ACT.Exp, scale=self.c_mh),
                          waits=[lt], inc=(pre + "_rs", 1))
        else:
            r1 = self.scl(lambda e: e.activation(out=rc, in_=st[:, 15:16],
                                                 func=ACT.Sqrt),
                          waits=[v1, ("c_ini", 1)], inc=(pre + "_rs", 1))
        return r1

    def alloc_ffn_bufs(self, with_prep):
        AB, AF = self.AB, self.AF
        I, D, IH = self.I, self.D, self.IH
        b = {}
        b["wgqT"] = AB.alloc(self.DCH * I)
        b["wuqT"] = AB.alloc(self.DCH * I)
        b["wdqT"] = AB.alloc(self.ICH * D)
        b["txq"] = [AB.alloc(D) for _ in range(2)]
        b["hsg"] = AB.alloc(IH)
        b["qout"] = AB.alloc(IH)
        b["thq"] = [AB.alloc(IH), AB.alloc(IH)]
        b["xst"] = [AB.alloc(D) for _ in range(2)] if with_prep else None
        if with_prep:
            b["wst"] = [AF.alloc(IH) for _ in range(2)]
            b["wab"] = AB.alloc(D)
        q = {}
        q["axn"] = AB.alloc(IH)
        q["sg"] = [AB.alloc(IH), AB.alloc(IH)]
        q["t2"] = AB.alloc(IH)
        q["cb"] = AB.alloc(IH)
        _qs = AF.alloc(3 * (IH // 16))
        q["bm"] = _qs[:, 0:IH // 16]
        q["s32"] = _qs[:, IH // 16:2 * (IH // 16)]
        q["r32"] = _qs[:, 2 * (IH // 16):3 * (IH // 16)]
        q["in32"] = [AF.alloc(IH), AF.alloc(IH)]
        b["q"] = q
        b["xin"] = [AF.alloc(D) for _ in range(3)] if with_prep else None
        b["x2in"] = None if with_prep else [AB.alloc(D) for _ in range(3)]
        b["yst"] = None if with_prep else [AF.alloc(D) for _ in range(2)]
        stt_ = AF.alloc(128)
        b["stats"] = [stt_[:, 32 * i:32 * i + 16] for i in range(4)]
        b["rs"] = [stt_[:, 32 * i + 16:32 * i + 20] for i in range(4)]
        return b

    def emit_consts(self):
        self.v(lambda e: e.memset(self.c_eps, 1e-5))
        self.v(lambda e: e.memset(self.c_b20, -20.0))
        self.v(lambda e: e.memset(self.c_s32, 1.0 / 32.0))
        self.v(lambda e: e.memset(self.c_one, 1.0))
        self.v(lambda e: e.memset(self.c_mh, -0.5), inc=("c_ini", 1))
        self.sy(lambda e: e.dma_start(out=self.tri, in_=self.tri_in), inc=("tri_ld", 16))

    def emit_wchunk(self, kind, c, sub):
        s = self.sc
        fb = self.fb
        D, I, IH = self.D, self.I, self.IH
        j = self.wj % 2
        self.wj += 1
        wst = fb["wst"][j]
        wprev = self.qrel.get(("wst", j), [])
        if kind in ("g", "u"):
            N = D
            src_h = self.wg_in if kind == "g" else self.wu_in
            ld = self.gdma(lambda e: e.dma_start(out=wst[:, :N],
                                                 in_=src_h[c * 128:(c + 1) * 128, :]),
                           waits=wprev + [("w_ld", s.count("w_ld"))],
                           inc=("w_ld", 16))
        else:
            N = IH
            ld = self.gdma(lambda e: e.dma_start(
                out=wst[:, :N], in_=self.wd_in[c * 128:(c + 1) * 128, sub * IH:(sub + 1) * IH]),
                waits=wprev + [("w_ld", s.count("w_ld"))],
                inc=("w_ld", 16))
        wsb = fb["qout"]
        qt = self.emit_quant(wst[:, :N], N, [ld, ("w_t", s.count("w_t"))], wsb[:, :N],
                             rel_key=("wst", j))
        if kind in ("g", "u"):
            dst = fb["wgqT"] if kind == "g" else fb["wuqT"]
            out_ap = dst.rearrange("p (d i) -> p d i", i=I)[:, :, c * 128:(c + 1) * 128]
            in_ap = wsb[:, :N].rearrange("p (d r) -> p d r", r=128)
        else:
            dst = fb["wdqT"]
            out_ap = dst.rearrange("p (i d) -> p i d", d=D)[:, sub * 11:(sub + 1) * 11,
                                                           c * 128:(c + 1) * 128]
            in_ap = wsb[:, :N].rearrange("p (i r) -> p i r", r=128)
        tt = self.sy(lambda e: e.dma_start_transpose(out=out_ap, in_=in_ap),
                     waits=[qt, ("w_t", s.count("w_t"))], inc=("w_t", 16))
        self.qout_free = [tt]

    def emit_ffn_phase(self, first):
        s = self.sc
        NT = self.NT
        pre = "a" if first else "c"
        self.AB.mark()
        self.AF.mark()
        fb = self.alloc_ffn_bufs(with_prep=first)
        self.fb = fb
        self.qb = fb["q"]
        self.qpar = 0
        self.qinpar = 0
        self.sg_guard = [None, None]
        self.qrel = {}
        self.qout_free = []
        self.lnb = {"stats": fb["stats"], "rs": fb["rs"], "waits": []}
        if first:
            self.wj = 0
            order = ([("g", c, 0) for c in range(11)] + [("u", c, 0) for c in range(11)]
                     + [("g", c, 0) for c in range(11, 22)]
                     + [("u", c, 0) for c in range(11, 22)]
                     + [("d", c, sub) for c in range(8) for sub in range(2)])
            for (kind, c, sub) in order:
                self.emit_wchunk(kind, c, sub)
            self.w_thr = {"gu0": 16 * 22, "gu1": 16 * 44, "d": 16 * 60}
            wt_all = 16 * 60
            for (dst_h, src_a) in ((self.wg_hbm, fb["wgqT"]), (self.wu_hbm, fb["wuqT"]),
                                   (self.wd_hbm, fb["wdqT"])):
                self.gdma(lambda e, d=dst_h, a=src_a: e.dma_start(out=d, in_=a),
                          waits=[("w_t", wt_all)], inc=("w_sp", 16))
            self.emit_ffn_xpart(0, pre, first)
            self.emit_ffn_xrest(pre, first)
            self.emit_ffn_xpart(1, pre, first)
            xpre = 2
            D = self.D
            for (wi, src_h) in enumerate((self.wq_in, self.wk_in, self.wv_in, self.wfc_in)):
                for c in range(self.DCH):
                    j = self.wj % 2
                    self.wj += 1
                    wst = fb["wst"][j]
                    wprev = self.qrel.get(("wst", j), [])
                    ld = self.gdma(lambda e, sh=src_h, c=c, wst=wst: e.dma_start(
                        out=wst[:, :D], in_=sh[c * 128:(c + 1) * 128, :]),
                        waits=wprev + [("w_ld", s.count("w_ld"))], inc=("w_ld", 16))
                    cz = self.scl(lambda e, wst=wst: e.activation(
                        out=fb["wab"][:, :D], in_=wst[:, :D], func=ACT.Copy),
                        waits=[ld, ("wa_sp", s.count("wa_sp"))], inc=("w_ac", 1))
                    self.qrel[("wst", j)] = [cz]
                    self.gdma(lambda e, wi=wi, c=c: e.dma_start(
                        out=self.wa_hbm[:, (wi * self.DCH + c) * D:
                                        (wi * self.DCH + c + 1) * D],
                        in_=fb["wab"][:, :D]),
                        waits=[cz, ("wa_sp", s.count("wa_sp"))], inc=("wa_sp", 16))
        else:
            xpre = 0
            nld = 0
            early = [("b_qkmm", self.sc.count("b_qkmm")), ("b_vmm", self.sc.count("b_vmm")),
                     ("b_smm", self.sc.count("b_smm")), ("b_exp", self.sc.count("b_exp")),
                     ("b_att_done", self.sc.count("b_att_done")),
                     ("b_x2s", self.sc.count("b_x2s"))]
            for (src_h, dst_a) in ((self.wg_hbm, fb["wgqT"]), (self.wu_hbm, fb["wuqT"]),
                                   (self.wd_hbm, fb["wdqT"])):
                for piece in range(4):
                    n = src_h.shape[1] // 4
                    gate = early if nld < 3 else list(self.barrier_waits("gpsimd"))
                    self.gdma(lambda e, sh=src_h, da=dst_a, p=piece, n=n:
                              e.dma_start(out=da[:, p * n:(p + 1) * n],
                                          in_=sh[:, p * n:(p + 1) * n]),
                              waits=gate + [("w3_ld", self.sc.count("w3_ld"))],
                              inc=("w3_ld", 16))
                    nld += 1
            self.w_thr = {"gu0": 16 * nld, "gu1": 16 * nld, "d": 16 * nld}
            self.w_thr["sem"] = "w3_ld"

        if xpre == 0:
            self.emit_ffn_xpart(0, pre, first)
            self.emit_ffn_xrest(pre, first)
        for t in range(NT):
            self.emit_ffn_gu(t, 0, pre, "w_t" if first else "w3_ld")
            if t + 1 < NT and t + 1 >= xpre:
                self.emit_ffn_xpart(t + 1, pre, first)
            self.emit_ffn_evhm(t, 0, pre)
            if t >= 1:
                self.emit_ffn_down(t - 1, pre, first, 0)
            if t + 1 < NT:
                self.emit_ffn_xrest(pre, first)
            self.emit_ffn_hqchain(pre)
            self.emit_ffn_gu(t, 1, pre, "w_t" if first else "w3_ld")
            self.emit_ffn_evhm(t, 1, pre)
            if t >= 1:
                self.emit_ffn_down(t - 1, pre, first, 1)
                self.emit_ffn_resid(t - 1, pre, first)
            self.emit_ffn_hqchain(pre)
        self.emit_ffn_down(NT - 1, pre, first, 0)
        self.emit_ffn_down(NT - 1, pre, first, 1)
        self.emit_ffn_resid(NT - 1, pre, first)
        for eng in Sched.ENGINES:
            bw = []
            if eng == "gpsimd":
                bw = [(pre + "_x1s", s.count(pre + "_x1s"))]
                if first:
                    bw.append(("w_sp", s.count("w_sp")))
            s.op(eng, lambda e: None, waits=bw,
                 inc=(("p1b_" if first else "p3b_") + eng, 1))
        self.AB.release()
        self.AF.release()

    def barrier_waits(self, eng, which="p2b_"):
        return [(which + e2, self.sc.count(which + e2)) for e2 in Sched.ENGINES]

    def emit_ffn_xpart(self, t, pre, first):
        s = self.sc
        fb = self.fb
        D = self.D
        if first:
            xsrc = fb["xin"][t % 3]
            ld = self.gdma(lambda e: e.dma_start(out=xsrc[:, :D],
                                                 in_=self.x_in[t * 128:(t + 1) * 128, :]),
                           waits=[(pre + "_x1", max(0, t - 2)),
                                  (pre + "_x0", s.count(pre + "_x0"))],
                           inc=(pre + "_x0", 16))
        else:
            xsrc = fb["x2in"][t % 3]
            ld = self.gdma(lambda e: e.dma_start(out=xsrc[:, :D],
                                                 in_=self.x2_hbm[t * 128:(t + 1) * 128, :]),
                           waits=[(pre + "_x1", max(0, t - 2)), ("b_x2s", 16 * (t + 1)),
                                  (pre + "_x0", s.count(pre + "_x0"))]
                           + (self.barrier_waits("gpsimd") if t < 3 else []),
                           inc=(pre + "_x0", 16))
        self.lnb["waits"] = [ld]
        r1 = self.emit_ln_stats(xsrc[:, :D], D, pre + "_ln", use_expln=False, rcol=t % 2)
        self.xp_state = (t, xsrc, r1)

    def emit_ffn_xrest(self, pre, first):
        s = self.sc
        fb = self.fb
        D = self.D
        (t, xsrc, r1) = self.xp_state
        st = fb["stats"][t % 2]
        qp = self.qinpar
        self.qinpar ^= 1
        qin = fb["q"]["in32"][qp]
        rcap = fb["rs"][t % 2][:, 0:1]
        ax = self.v(lambda e: e.tensor_scalar(out=qin[:, :D], in0=xsrc[:, :D],
                                              scalar1=st[:, 12:13], scalar2=rcap,
                                              op0=ALU.subtract, op1=ALU.mult),
                    waits=[r1] + self.qrel.get(("qin", qp), []),
                    inc=(pre + "_xn", 1))
        qx = self.emit_quant(qin[:, :D], D, [ax], fb["qout"][:, :D],
                             rel_key=("qin", qp))
        par = t % 2
        tt = self.sy(lambda e: e.dma_start_transpose(
            out=fb["txq"][par].rearrange("p (d r) -> p d r", r=128),
            in_=fb["qout"][:, :D].rearrange("p (d r) -> p d r", r=128)),
            waits=[qx, (pre + "_gu", max(0, 2 * (t - 1))),
                   (pre + "_txq", s.count(pre + "_txq"))], inc=(pre + "_txq", 16))
        self.qout_free = [tt]

    def emit_ffn_gu(self, t, h, pre, wsem):
        fb = self.fb
        IH = self.IH
        par = t % 2
        ncol = [512, 512, IH - 1024]
        wthr = self.w_thr["gu0"] if h == 0 else self.w_thr["gu1"]

        def gu(e):
            last = None
            for d in range(self.DCH):
                lhsT = fb["txq"][par][:, d * 128:(d + 1) * 128]
                for c in range(3):
                    w = ncol[c]
                    off = d * self.I + h * IH + c * 512
                    e.matmul(self.ps[c].ap()[:, :w], lhsT,
                             fb["wgqT"][:, off:off + w],
                             start=(d == 0), stop=(d == self.DCH - 1))
                    last = e.matmul(self.ps[3 + c].ap()[:, :w], lhsT,
                                    fb["wuqT"][:, off:off + w],
                                    start=(d == 0), stop=(d == self.DCH - 1))
            return last
        self.t(gu, waits=[(pre + "_txq", 16 * (t + 1)), (wsem, wthr),
                          (pre + "_ev", max(0, 2 * t + h)),
                          (pre + "_hm", max(0, 2 * t + h))], inc=(pre + "_gu", 1))

    def emit_ffn_evhm(self, t, h, pre):
        s = self.sc
        fb = self.fb
        IH = self.IH
        ncol = [512, 512, IH - 1024]

        def ev(e):
            last = None
            for c in range(3):
                w = ncol[c]
                last = e.activation(out=fb["hsg"][:, c * 512:c * 512 + w],
                                    in_=self.ps[c].ap()[:, :w], func=ACT.Silu)
            return last
        evt = self.sc.op("scalar", ev,
                         waits=[(pre + "_gu", 2 * t + h + 1), (pre + "_hm", max(0, 2 * t + h))],
                         inc=(pre + "_ev", 1))
        qp = self.qinpar
        self.qinpar ^= 1
        qin = fb["q"]["in32"][qp]

        def hmul(e):
            last = None
            for c in range(3):
                w = ncol[c]
                last = e.tensor_tensor(out=qin[:, c * 512:c * 512 + w],
                                       in0=fb["hsg"][:, c * 512:c * 512 + w],
                                       in1=self.ps[3 + c].ap()[:, :w], op=ALU.mult)
            return last
        hm = self.sc.op("vector", hmul,
                        waits=[evt] + self.qrel.get(("qin", qp), []),
                        inc=(pre + "_hm", 1))
        self.hq_state = (t, h, qp, qin, hm)

    def emit_ffn_hqchain(self, pre):
        s = self.sc
        fb = self.fb
        IH = self.IH
        (t, h, qp, qin, hm) = self.hq_state
        qh = self.emit_quant(qin[:, :IH], IH, [hm], fb["qout"][:, :IH],
                             rel_key=("qin", qp))
        tt = self.sy(lambda e: e.dma_start_transpose(
            out=fb["thq"][h].rearrange("p (i r) -> p i r", r=128),
            in_=fb["qout"][:, :IH].rearrange("p (i r) -> p i r", r=128)),
            waits=[qh, (pre + "_dn", max(0, t)),
                   (pre + "_thq", self.sc.count(pre + "_thq"))], inc=(pre + "_thq", 16))
        self.qout_free = [tt]

    def emit_ffn_down(self, t, pre, first, colh):
        fb = self.fb
        D = self.D
        wsem = "w_t" if first else "w3_ld"

        def dn(e):
            last = None
            for i in range(self.ICH):
                hh = 0 if i < 11 else 1
                ioff = (i - 11 * hh) * 128
                last = e.matmul(self.ps[6 + colh].ap()[:, :512],
                                fb["thq"][hh][:, ioff:ioff + 128],
                                fb["wdqT"][:, i * D + colh * 512: i * D + colh * 512 + 512],
                                start=(i == 0), stop=(i == self.ICH - 1))
            return last
        if colh == 0:
            self.t(dn, waits=[(pre + "_thq", 16 * (2 * t + 2)),
                              (wsem, self.w_thr["d"]),
                              (pre + "_x1", max(0, t))])
        else:
            self.t(dn, inc=(pre + "_dn", 1))

    def emit_ffn_resid(self, t, pre, first):
        s = self.sc
        fb = self.fb
        D = self.D
        par = t % 2
        xsrc = (fb["xin"] if first else fb["x2in"])[t % 3]
        dst = (fb["xst"] if first else fb["yst"])[par]
        out_hbm = self.x1_hbm if first else self.y_out
        self.v(lambda e: e.scalar_tensor_tensor(
            out=dst[:, 0:512], in0=self.ps[6].ap()[:, :512],
            scalar=0.5, in1=xsrc[:, 0:512], op0=ALU.mult, op1=ALU.add),
            waits=[(pre + "_dn", s.count(pre + "_dn")),
                   (pre + "_x1s", max(0, 16 * (t - 1)))])
        r2 = self.v(lambda e: e.scalar_tensor_tensor(
            out=dst[:, 512:1024], in0=self.ps[7].ap()[:, :512],
            scalar=0.5, in1=xsrc[:, 512:1024], op0=ALU.mult, op1=ALU.add),
            inc=(pre + "_x1", 1))
        self.gdma(lambda e: e.dma_start(out=out_hbm[t * 128:(t + 1) * 128, :],
                                        in_=dst[:, :D]),
                  waits=[r2, (pre + "_x1s", s.count(pre + "_x1s"))],
                  inc=(pre + "_x1s", 16))

    def finalize(self):
        self.gdma(lambda e: None, waits=[("c_x1s", self.sc.count("c_x1s"))])

    def alloc_attn_bufs(self):
        AB, AF = self.AB, self.AF
        D, QT = self.D, self.QT
        b = {}
        b["wq"] = AB.alloc(self.DCH * D)
        b["wk"] = AB.alloc(self.DCH * D)
        b["wv"] = AB.alloc(self.DCH * D)
        b["wfc"] = AB.alloc(self.DCH * D)
        b["xnT"] = AB.alloc(self.DCH * QT * 128)
        b["qT"] = AB.alloc(self.DCH * QT * 128)
        b["ktc"] = AB.alloc(self.DCH * QT * 128)
        b["vc"] = AB.alloc(QT * D)
        gkn = self.DCH * min(4, QT) * 128
        if 2 * gkn <= self.DCH * QT * 128:
            b["ktg"] = [b["xnT"][:, 0:gkn], b["xnT"][:, gkn:2 * gkn]]
        else:
            b["ktg"] = [AB.alloc(gkn) for _ in range(2)]
        b["vg"] = [AB.alloc(4 * D) for _ in range(2)]
        b["x1q"] = [AB.alloc(D) for _ in range(2)]
        b["x1r"] = [AB.alloc(D) for _ in range(2)]
        b["x2st"] = [AB.alloc(D) for _ in range(2)]
        b["att"] = [AB.alloc(512) for _ in range(3)]
        b["attT"] = [AB.alloc(512) for _ in range(3)]
        b["lnr"] = AB.alloc(D)
        b["xnb"] = [b["lnr"]]
        b["lnrT"] = [AB.alloc(D), AB.alloc(D)]
        b["gam"] = AF.alloc(D)
        b["bet"] = AF.alloc(D)
        b["bfc"] = AF.alloc(D)
        _st2 = AF.alloc(128)
        b["stats"] = [_st2[:, 32 * i:32 * i + 16] for i in range(4)]
        b["rs"] = [_st2[:, 32 * i + 16:32 * i + 20] for i in range(4)]
        b["wstg"] = [AF.alloc(D) for _ in range(2)]
        b["den"] = AF.alloc(2 * QT)
        b["dpart"] = AF.alloc(4)
        b["ov"] = AF.alloc(D)
        return b

    def emit_p2(self):
        s = self.sc
        NT, QT, NQ, D = self.NT, self.QT, self.NQ, self.D
        self.AB.mark()
        self.AF.mark()
        ab = self.alloc_attn_bufs()
        self.ab2 = ab
        self.qb = None
        self.lnb = {"stats": ab["stats"], "rs": ab["rs"], "waits": []}
        p1w = self.barrier_waits("gpsimd", "p1b_")

        for (wi, dst) in enumerate((ab["wq"], ab["wk"], ab["wv"], ab["wfc"])):
            self.gdma(lambda e, wi=wi, dst=dst: e.dma_start(
                out=dst, in_=self.wa_hbm[:, wi * self.DCH * self.D:
                                         (wi + 1) * self.DCH * self.D]),
                waits=p1w + [("wa_sp", 16 * 32), ("b_wld", s.count("b_wld"))],
                inc=("b_wld", 16))
        self.sc.op("scalar", lambda e: None,
                   waits=[("b_wld", s.count("b_wld"))], inc=("b_wc", 32))
        for (src_h, dst) in ((self.gam_in, ab["gam"]), (self.bet_in, ab["bet"]),
                             (self.bfc_in, ab["bfc"])):
            self.sy(lambda e, sh=src_h, dst=dst: e.dma_start(
                out=dst, in_=sh.broadcast_to([128, D])),
                waits=self.barrier_waits("sync", "p1b_") + [("b_cld", s.count("b_cld"))],
                inc=("b_cld", 16))

        self.pend_fin = []
        self.fc_pred = 0
        self.gr_last_av = {0: 0, 1: 0}
        for k in range(NQ):
            self.den_first = {tau: True for tau in range(self.QT)}
            self.o_first = {tau: [True, True] for tau in range(self.QT)}
            self.gr_free = {0: [], 1: []}
            self.emit_p2_qkv(k)
            self.emit_p2_attn(k)
            while self.pend_fin:
                self.emit_p2_finB()
        for eng in Sched.ENGINES:
            bw = []
            if eng == "gpsimd":
                bw = [(n, s.count(n)) for n in
                      ("b_ktsp", "b_vsp", "b_x2s", "b_gl", "b_x1l", "b_x1rl", "b_wld")]
            s.op(eng, lambda e: None, waits=bw, inc=("p2b_" + eng, 1))
        self.AB.release()
        self.AF.release()

    def emit_p2_qkv(self, k):
        s = self.sc
        ab = self.ab2
        D, QT = self.D, self.QT
        xnt_base = s.count("b_xnt")
        for tau in range(QT):
            t = k * QT + tau
            xb = ab["x1q"][tau % 2]
            ld = self.gdma(lambda e, t=t, xb=xb: e.dma_start(
                out=xb[:, :D], in_=self.x1_hbm[t * 128:(t + 1) * 128, :]),
                waits=[("a_x1s", 16 * self.NT), ("b_x1f", max(0, s.count("b_x1f") - 1)),
                       ("b_x1l", s.count("b_x1l"))],
                inc=("b_x1l", 16))
            self.lnb["waits"] = [ld]
            r1 = self.emit_ln_stats(xb[:, :D], D, "b_ln", use_expln=True, rcol=tau % 2)
            xn = ab["xnb"][0]
            st = ab["stats"][tau % 2]
            rcap = ab["rs"][tau % 2][:, 0:1]
            ax = self.v(lambda e, xb=xb, xn=xn, st=st, rcap=rcap: e.tensor_scalar(
                out=xn[:, :D], in0=xb[:, :D], scalar1=st[:, 12:13], scalar2=rcap,
                op0=ALU.subtract, op1=ALU.mult),
                waits=[r1, ("b_xnt", s.count("b_xnt")),
                       ("b_lnt", s.count("b_lnt"))],
                inc=("b_x1f", 1))
            self.sy(lambda e, xn=xn, tau=tau: e.dma_start_transpose(
                out=ab["xnT"].rearrange("p (d r) -> p d r", r=QT * 128)[:, :, tau * 128:(tau + 1) * 128],
                in_=xn[:, :D].rearrange("p (d r) -> p d r", r=128)),
                waits=[ax, ("b_xnt", s.count("b_xnt")),
                       ("b_qkmm", s.count("b_qkmm")), ("b_vmm", s.count("b_vmm")),
                       ("b_exp", s.count("b_exp")), ("b_gl", s.count("b_gl"))],
                inc=("b_xnt", 16))
        rhw = min(512, QT * 128)
        nrh = (QT * 128) // rhw
        qkctr = 0
        for rh in range(nrh):
            xnt_rh = xnt_base + 16 * (rh + 1) * (rhw // 128)
            for (wname, dsts, spill) in (("wq", "qT", False), ("wk", "ktc", True)):
                w = ab[wname]
                dst = ab[dsts]
                for dk in range(self.DCH):
                    pspar = qkctr % 2
                    qkctr += 1
                    def mm(e, w=w, dk=dk, rh=rh, pspar=pspar):
                        last = None
                        for d in range(self.DCH):
                            last = e.matmul(self.ps[pspar].ap()[:, :rhw],
                                            w[:, d * D + dk * 128: d * D + dk * 128 + 128],
                                            ab["xnT"][:, d * QT * 128 + rh * rhw:
                                                      d * QT * 128 + rh * rhw + rhw],
                                            start=(d == 0), stop=(d == self.DCH - 1))
                        return last
                    mt = self.t(mm, waits=[("b_xnt", xnt_rh), ("b_wc", 32),
                                           ("b_exp", s.count("b_exp")),
                                           ("b_qkev", max(0, s.count("b_qkev") - 1))],
                                inc=("b_qkmm", 1))
                    self.scl(lambda e, dst=dst, dk=dk, rh=rh, pspar=pspar: e.activation(
                        out=dst[:, dk * QT * 128 + rh * rhw: dk * QT * 128 + rh * rhw + rhw],
                        in_=self.ps[pspar].ap()[:, :rhw], func=ACT.Copy),
                        waits=[mt, ("b_att_done", s.count("b_att_done")),
                               ("b_ktsp", s.count("b_ktsp"))],
                        inc=("b_qkev", 1))
        for tau in range(QT):
            for vh in range(2):
                pspar = (tau * 2 + vh) % 2
                def mmv(e, tau=tau, vh=vh, pspar=pspar):
                    last = None
                    for d in range(self.DCH):
                        last = e.matmul(self.ps[2 + pspar].ap()[:, :512],
                                        ab["xnT"][:, d * QT * 128 + tau * 128:
                                                  d * QT * 128 + tau * 128 + 128],
                                        ab["wv"][:, d * D + vh * 512: d * D + vh * 512 + 512],
                                        start=(d == 0), stop=(d == self.DCH - 1))
                    return last
                mt = self.t(mmv, waits=[("b_xnt", xnt_base + 16 * (tau + 1)), ("b_wc", 32),
                                        ("b_oev", s.count("b_oev")),
                                        ("b_vev", max(0, s.count("b_vev") - 1))],
                            inc=("b_vmm", 1))
                self.scl(lambda e, tau=tau, vh=vh, pspar=pspar: e.activation(
                    out=ab["vc"][:, tau * D + vh * 512: tau * D + vh * 512 + 512],
                    in_=self.ps[2 + pspar].ap()[:, :512], func=ACT.Copy),
                    waits=[mt, ("b_att_done", s.count("b_att_done")),
                           ("b_vsp", s.count("b_vsp"))],
                    inc=("b_vev", 1))
        kev = s.count("b_qkev")
        vev = s.count("b_vev")
        self.gdma(lambda e, k=k: e.dma_start(out=self.kt_hbm[k], in_=ab["ktc"]),
                  waits=[("b_qkev", kev), ("b_ktsp", s.count("b_ktsp"))],
                  inc=("b_ktsp", 16))
        self.gdma(lambda e, k=k: e.dma_start(
            out=self.v_hbm[k * QT:(k + 1) * QT].rearrange("t p d -> p t d"),
            in_=ab["vc"].rearrange("p (t d) -> p t d", d=D)),
            waits=[("b_vev", vev), ("b_vsp", s.count("b_vsp"))], inc=("b_vsp", 16))

    def emit_p2_attn(self, k):
        s = self.sc
        ab = self.ab2
        D, QT = self.D, self.QT
        gsz = min(4, self.QT)
        ngran = (self.QT // gsz) * k
        for g in range(QT // 2):
            taus = (2 * g, 2 * g + 1)
            gtoks = []
            base_av = s.count("b_av")
            for gr in range(ngran):
                q, gi = gr // 2, gr % 2
                par = gr % 2
                wk = [("b_ktsp", 16 * (q + 1)), ("b_vsp", 16 * (q + 1))]
                if gr >= 2:
                    wk.append(("b_av", base_av + 2 * (gr - 2) + 2))
                else:
                    if self.gr_last_av[par]:
                        wk.append(("b_av", self.gr_last_av[par]))
                    wk.append(("b_qkmm", s.count("b_qkmm")))
                    wk.append(("b_vmm", s.count("b_vmm")))
                gw = gsz * 128
                l1 = self.gdma(lambda e, q=q, gi=gi, par=par, gw=gw: e.dma_start(
                    out=ab["ktg"][par][:, :self.DCH * gw].rearrange("p (d r) -> p d r", r=gw),
                    in_=self.kt_hbm[q].rearrange("p (d r) -> p d r", r=QT * 128)
                    [:, :, gi * gw:(gi + 1) * gw]),
                    waits=wk + [("b_gl", s.count("b_gl"))], inc=("b_gl", 16))
                l2 = self.gdma(lambda e, q=q, gi=gi, par=par: e.dma_start(
                    out=ab["vg"][par].rearrange("p (t d) -> p t d", d=D)[:, :gsz],
                    in_=self.v_hbm[q * QT + gi * gsz: q * QT + gi * gsz + gsz]
                    .rearrange("t p d -> p t d")),
                    waits=[("b_gl", s.count("b_gl"))], inc=("b_gl", 16))
                gtoks.append((l1, l2, par))
            chunks = []
            for ci in range(ngran):
                for tau in taus:
                    chunks.append((tau, "g", ci, gsz, False))
            loc = {tau: [] for tau in taus}
            for tau in taus:
                lo = 0
                while lo < tau:
                    wdt = min(4, tau - lo)
                    loc[tau].append((tau, "l", lo, wdt, False))
                    lo += wdt
                loc[tau].append((tau, "l", tau, 1, True))
            mx = max(len(loc[t]) for t in taus)
            for i in range(mx):
                for tau in taus:
                    if i < len(loc[tau]):
                        chunks.append(loc[tau][i])
            for a_ in range(ngran):
                self.gr_last_av[a_ % 2] = base_av + 2 * a_ + 2
            self.oev_gate = s.count("b_oev")
            self.av_base = s.count("b_av")
            pend = []
            seen_gr = set()
            for ci, ch in enumerate(chunks):
                if ch[1] == "g" and ch[2] not in seen_gr:
                    seen_gr.add(ch[2])
                    if ch[2] >= 2:
                        while len(pend) > 2:
                            self.emit_p2_av(*pend.pop(0))
                info = self.emit_p2_scores(k, g, ci, ch, gtoks)
                pend.append(info)
                if ci in (1, 3):
                    self.emit_p2_finB()
                if len(pend) > 3:
                    self.emit_p2_av(*pend.pop(0))
            for p_ in pend:
                self.emit_p2_av(*p_)
            for tau in taus:
                self.emit_p2_fin(k, tau)
            if os.environ.get("MK_NODEFER") == "1":
                while self.pend_fin:
                    self.emit_p2_finB()

    def emit_p2_scores(self, k, g, ci, ch, gtoks):
        s = self.sc
        ab = self.ab2
        D, QT = self.D, self.QT
        (tau, kind, a, wt, diag) = ch
        w = wt * 128
        par = ci % 2
        apar = ci % 3
        gpar = gtoks[a][2] if kind == "g" else 0

        def mm(e):
            last = None
            for dk in range(self.DCH):
                lhsT = ab["qT"][:, dk * QT * 128 + tau * 128: dk * QT * 128 + tau * 128 + 128]
                if kind == "g":
                    gw = min(4, self.QT) * 128
                    rhs = ab["ktg"][gpar][:, dk * gw: dk * gw + w]
                else:
                    rhs = ab["ktc"][:, dk * QT * 128 + a * 128: dk * QT * 128 + a * 128 + w]
                last = e.matmul(self.ps[par].ap()[:, :w], lhsT, rhs,
                                start=(dk == 0), stop=(dk == self.DCH - 1))
            return last
        waits = [("b_exp", max(0, s.count("b_exp") - 1)),
                 ("b_qkev", s.count("b_qkev"))]
        if kind == "g":
            waits.append(gtoks[a][0])
        mt = self.t(mm, waits=waits, inc=("b_smm", 1))
        att = ab["att"][apar]
        dguard = ("b_den", max(0, s.count("b_den") - 1))
        if not diag:
            ex = self.scl(lambda e: e.activation(
                out=att[:, :w], in_=self.ps[par].ap()[:, :w], func=ACT.Exp,
                bias=self.c_b20, scale=self.c_s32, accum_out=ab["dpart"][:, par:par + 1]),
                waits=[mt, ("c_ini", 1), dguard,
                       ("b_att_t", max(0, s.count("b_att_t") - 32))],
                inc=("b_exp", 1))
            self.v(self.den_update_op(tau, par),
                   waits=[ex, ("b_oev", self.oev_gate)], inc=("b_den", 1))
            mk_tok = ex
        else:
            ex = self.scl(lambda e: e.activation(
                out=att[:, :w], in_=self.ps[par].ap()[:, :w], func=ACT.Exp,
                bias=self.c_b20, scale=self.c_s32),
                waits=[mt, ("c_ini", 1), dguard,
                       ("b_att_t", max(0, s.count("b_att_t") - 32))],
                inc=("b_exp", 1))
            mk_tok = self.v(lambda e: e.tensor_tensor(out=att[:, :w], in0=att[:, :w],
                                                      in1=self.tri, op=ALU.mult),
                            waits=[ex, ("tri_ld", 16)], inc=("b_msk", 1))
            rr = self.v(lambda e: e.tensor_reduce(
                out=ab["dpart"][:, par:par + 1].rearrange("p (a b) -> p a b", b=1),
                in_=att[:, :w].rearrange("p (a b) -> p a b", b=w),
                axis=AX.X, op=ALU.add), waits=[mk_tok], inc=("qs_v", 1))
            self.v(self.den_update_op(tau, par),
                   waits=[rr, ("b_oev", self.oev_gate)], inc=("b_den", 1))
        tr = self.sy(lambda e: e.dma_start_transpose(
            out=ab["attT"][apar][:, :w].rearrange("p (c r) -> p c r", r=128),
            in_=att[:, :w].rearrange("p (c r) -> p c r", r=128)),
            waits=[mk_tok, ("b_av", max(self.av_base, self.av_base + ci - 2)),
                   ("b_att_t", s.count("b_att_t"))],
            inc=("b_att_t", 16))
        vtok = gtoks[a][1] if kind == "g" else None
        return (ch, apar, gpar, tr, vtok)

    def den_update_op(self, tau, par):
        first = self.den_first[tau]
        self.den_first[tau] = False
        dst = self.ab2["den"][:, tau:tau + 1]
        srcp = self.ab2["dpart"][:, par:par + 1]
        if first:
            return lambda e: e.tensor_copy(out=dst, in_=srcp)
        return lambda e: e.tensor_tensor(out=dst, in0=dst, in1=srcp, op=ALU.add)

    def emit_p2_av(self, ch, par, gpar, tr, vtok=None):
        s = self.sc
        ab = self.ab2
        D = self.D
        (tau, kind, a, wt, diag) = ch
        obank = 2 + 2 * (tau % 2)

        firsts = (self.o_first[tau][0], self.o_first[tau][1])
        self.o_first[tau][0] = False
        self.o_first[tau][1] = False

        def mm(e):
            last = None
            for vh in range(2):
                for mt_ in range(wt):
                    lhsT = ab["attT"][par][:, mt_ * 128: mt_ * 128 + 128]
                    if kind == "g":
                        rhs = ab["vg"][gpar][:, mt_ * D + vh * 512: mt_ * D + vh * 512 + 512]
                    else:
                        rhs = ab["vc"][:, (a + mt_) * D + vh * 512: (a + mt_) * D + vh * 512 + 512]
                    last = e.matmul(self.ps[obank + vh].ap()[:, :512], lhsT, rhs,
                                    start=(firsts[vh] and mt_ == 0),
                                    stop=(diag and mt_ == wt - 1))
            return last
        w_ = [tr, ("b_oev", self.oev_gate)]
        if vtok is not None:
            w_.append(vtok)
        tok = self.t(mm, waits=w_, inc=("b_av", 1))
        if kind == "g":
            self.gr_free[gpar] = [tok]

    def emit_p2_fin(self, k, tau):
        s = self.sc
        ab = self.ab2
        D, QT = self.D, self.QT
        t = k * QT + tau
        obank = 2 + 2 * (tau % 2)
        den_n = s.count("b_den")
        par = tau % 2
        x1r = ab["x1r"][par]
        ld = self.gdma(lambda e: e.dma_start(out=x1r[:, :D],
                                             in_=self.x1_hbm[t * 128:(t + 1) * 128, :]),
                       waits=[("b_x1rl", s.count("b_x1rl")),
                              ("b_x2w", max(0, s.count("b_x2w") - 1))],
                       inc=("b_x1rl", 16))
        rd = self.v(lambda e: e.reciprocal(out=ab["den"][:, QT + tau:QT + tau + 1],
                                           in_=ab["den"][:, tau:tau + 1]),
                    waits=[("b_den", den_n), ("b_av", s.count("b_av"))],
                    inc=("qs_v", 1))
        rden = ab["den"][:, QT + tau:QT + tau + 1]
        e1 = self.v(lambda e: e.tensor_scalar(out=ab["ov"][:, 0:512],
                                              in0=self.ps[obank].ap()[:, :512],
                                              scalar1=rden, scalar2=None, op0=ALU.mult),
                    waits=[rd], inc=("b_oev", 1))
        e2 = self.v(lambda e: e.tensor_scalar(out=ab["ov"][:, 512:1024],
                                              in0=self.ps[obank + 1].ap()[:, :512],
                                              scalar1=rden, scalar2=None, op0=ALU.mult),
                    waits=[e1], inc=("b_oev", 1))
        av = self.v(lambda e: e.tensor_tensor(
            out=ab["ov"][:, :D], in0=ab["ov"][:, :D],
            in1=ab["vc"][:, tau * D:(tau + 1) * D], op=ALU.add),
            waits=[e2], inc=("qs_v", 1))
        self.lnb["waits"] = [av]
        r1 = self.emit_ln_stats(ab["ov"][:, :D], D, "b_ln2", use_expln=True,
                                rcol=2 + tau % 2)
        stt_ = ab["stats"][2 + tau % 2]
        rcap = ab["rs"][2 + tau % 2][:, 0:1]
        n1 = self.v(lambda e, stt_=stt_, rcap=rcap: e.tensor_scalar(
            out=ab["ov"][:, :D], in0=ab["ov"][:, :D],
            scalar1=stt_[:, 12:13], scalar2=rcap,
            op0=ALU.subtract, op1=ALU.mult),
                    waits=[r1], inc=("qs_v", 1))
        n2 = self.v(lambda e: e.tensor_tensor(out=ab["ov"][:, :D], in0=ab["ov"][:, :D],
                                              in1=ab["gam"][:, :D], op=ALU.mult),
                    waits=[n1, ("b_cld", 48)], inc=("qs_v", 1))
        n3 = self.v(lambda e: e.tensor_tensor(out=ab["lnr"][:, :D], in0=ab["ov"][:, :D],
                                              in1=ab["bet"][:, :D], op=ALU.add),
                    waits=[n2, ("b_lnt", s.count("b_lnt")),
                           ("b_xnt", s.count("b_xnt"))], inc=("qs_v", 1))
        lt = ab["lnrT"][par]
        tr = self.sy(lambda e: e.dma_start_transpose(
            out=lt.rearrange("p (d r) -> p d r", r=128),
            in_=ab["lnr"][:, :D].rearrange("p (d r) -> p d r", r=128)),
            waits=[n3, ("b_lnt", s.count("b_lnt")),
                   ("b_fc", self.fc_pred)],
            inc=("b_lnt", 16))
        self.fc_pred += 1
        self.pend_fin.append((k, tau, t, tr, ld))

    def emit_p2_finB(self):
        if not self.pend_fin:
            return
        (k, tau, t, tr, ld) = self.pend_fin.pop(0)
        s = self.sc
        ab = self.ab2
        D, QT = self.D, self.QT
        par = tau % 2
        lt = ab["lnrT"][par]

        def fc(e):
            last = None
            for oh in range(2):
                for d in range(self.DCH):
                    last = e.matmul(self.ps[6 + oh].ap()[:, :512],
                                    lt[:, d * 128: d * 128 + 128],
                                    ab["wfc"][:, d * D + oh * 512: d * D + oh * 512 + 512],
                                    start=(d == 0), stop=(d == self.DCH - 1))
            return last
        ft = self.t(fc, waits=[tr, ("b_wc", 32), ("b_x2w", s.count("b_x2w"))],
                    inc=("b_fc", 1))
        x2 = ab["x2st"][par]
        w1 = self.v(lambda e: e.scalar_tensor_tensor(
            out=x2[:, 0:512], in0=self.ps[6].ap()[:, :512], scalar=1.0,
            in1=ab["bfc"][:, 0:512], op0=ALU.mult, op1=ALU.add),
            waits=[ft, ld, ("b_x2s", max(0, 16 * (t - 1)))], inc=("qs_v", 1))
        w2 = self.v(lambda e: e.scalar_tensor_tensor(
            out=x2[:, 512:1024], in0=self.ps[7].ap()[:, :512], scalar=1.0,
            in1=ab["bfc"][:, 512:1024], op0=ALU.mult, op1=ALU.add),
            waits=[w1], inc=("qs_v", 1))
        w3 = self.v(lambda e: e.tensor_tensor(out=x2[:, :D], in0=x2[:, :D],
                                              in1=ab["x1r"][par][:, :D], op=ALU.add),
                    waits=[w2], inc=("b_x2w", 1))
        self.gdma(lambda e: e.dma_start(out=self.x2_hbm[t * 128:(t + 1) * 128, :],
                                        in_=x2[:, :D]),
                  waits=[w3, ("b_x2s", s.count("b_x2s"))], inc=("b_x2s", 16))
        self.v(lambda e: None, inc=("b_att_done", 1))

import ml_dtypes
from concourse.bass_utils import run_bass_kernel_spmd

_NT, _QT, _D, _I = 32, 8, 1024, 2816


def _build():
    nc = bass.Bass("TRN2", target_bir_lowering=False)
    mk = MK(nc, NT=_NT, QT=_QT, D=_D, I=_I)
    mk.emit_consts()
    mk.emit_ffn_phase(first=True)
    mk.emit_p2()
    mk.emit_ffn_phase(first=False)
    mk.sc.finalize_waits()
    with nc.Block() as blk:
        @blk.sync
        def _(e):
            mk.sc.emit_stream("sync", e)

        @blk.scalar
        def _(e):
            mk.sc.emit_stream("scalar", e)

        @blk.vector
        def _(e):
            mk.sc.emit_stream("vector", e)

        @blk.tensor
        def _(e):
            mk.sc.emit_stream("tensor", e)

        @blk.gpsimd
        def _(e):
            mk.sc.emit_stream("gpsimd", e)
    return nc


def kernel(L0_attn_x, L0_attn_Wq, L0_attn_Wk, L0_attn_Wv, L0_attn_Wfc,
           L0_attn_bfc, L0_attn_gamma, L0_attn_beta,
           L0_ffn_w_gate, L0_ffn_w_up, L0_ffn_w_down):
    nc = _build()
    x = np.ascontiguousarray(L0_attn_x)
    tri = np.tril(np.ones((128, 128), np.float32)).astype(ml_dtypes.bfloat16)
    base = {
        "Wq": np.ascontiguousarray(L0_attn_Wq),
        "Wk": np.ascontiguousarray(L0_attn_Wk),
        "Wv": np.ascontiguousarray(L0_attn_Wv),
        "Wfc": np.ascontiguousarray(L0_attn_Wfc),
        "bfc": np.ascontiguousarray(L0_attn_bfc).reshape(1, _D),
        "gamma": np.ascontiguousarray(L0_attn_gamma).reshape(1, _D),
        "beta": np.ascontiguousarray(L0_attn_beta).reshape(1, _D),
        "w_gate": np.ascontiguousarray(L0_ffn_w_gate),
        "w_up": np.ascontiguousarray(L0_ffn_w_up),
        "w_down": np.ascontiguousarray(L0_ffn_w_down),
        "tri": tri,
    }
    in_maps = [dict(base, x=np.ascontiguousarray(x[b])) for b in range(8)]
    res = run_bass_kernel_spmd(nc, in_maps, core_ids=list(range(8)))
    out = np.stack([res.results[b]["y"] for b in range(8)], axis=0)
    return out.astype(np.float32)


# version 24
# speedup vs baseline: 1.0383x; 1.0222x over previous
import os
import numpy as np
import concourse.bass as bass
import concourse.mybir as mybir

SIM_DRAIN = False
SCL_DRAIN = False

F32 = mybir.dt.float32
BF16 = mybir.dt.bfloat16
AX = mybir.AxisListType
ALU = mybir.AluOpType
ACT = mybir.ActivationFunctionType


class Sched:
    ENGINES = ("sync", "scalar", "vector", "tensor", "gpsimd")

    def __init__(self, nc):
        self.nc = nc
        self.streams = {e: [] for e in self.ENGINES}
        self.sem_handles = {}
        self.sem_counts = {}
        self._sem_ctx = []

    def sem(self, name):
        if name not in self.sem_handles:
            cm = self.nc.semaphore(name)
            h = cm.__enter__()
            self._sem_ctx.append(cm)
            self.sem_handles[name] = h
            self.sem_counts[name] = 0
        return self.sem_handles[name]

    def op(self, eng, emit, waits=(), inc=None, tag=""):
        n_after = None
        if inc is not None:
            name, amt = inc
            self.sem(name)
            self.sem_counts[name] += amt
            n_after = self.sem_counts[name]
        for (s, v) in waits:
            self.sem(s)
        self.streams[eng].append((emit, tuple(waits), inc, tag))
        return (inc[0], n_after) if inc is not None else None

    def count(self, name):
        self.sem(name)
        return self.sem_counts[name]

    def emit_stream(self, eng_name, e):
        for (emit, waits, inc, tag) in self.streams[eng_name]:
            for (s, v) in waits:
                if v > 0:
                    e.wait_ge(self.sem_handles[s], v)
            ins = emit(e)
            if inc is not None:
                name, amt = inc
                if ins is None:
                    e.sem_inc(self.sem_handles[name], amt)
                else:
                    ins.then_inc(self.sem_handles[name], amt)

    def n_ops(self):
        return {k: len(v) for k, v in self.streams.items()}

    def finalize_waits(self, eng="gpsimd"):
        names = [n for n, c in self.sem_counts.items() if c > 0]
        def w(e):
            for n in names:
                e.wait_ge(self.sem_handles[n], self.sem_counts[n])
            return None
        self.streams[eng].append((w, (), None, "final_waits"))


class Arena:
    def __init__(self, nc, name, n_elems, dtype):
        self.t = nc.alloc_sbuf_tensor(name, [128, n_elems], dtype)
        self.n = n_elems
        self.off = 0
        self.marks = []
        self.peak = 0

    def alloc(self, n, align=32):
        o = (self.off + align - 1) // align * align
        assert o + n <= self.n, f"arena {self.t.name} overflow: {o}+{n} > {self.n}"
        self.off = o + n
        self.peak = max(self.peak, self.off)
        return self.t.ap()[:, o:o + n]

    def mark(self):
        self.marks.append(self.off)

    def release(self):
        self.off = self.marks.pop()


class MK:
    def __init__(self, nc, NT=32, QT=8, D=1024, I=2816, expose=()):
        self.nc = nc
        self.NT = NT
        self.QT = QT
        self.NQ = NT // QT
        self.D = D
        self.DCH = D // 128
        self.I = I
        self.ICH = I // 128
        self.IH = I // 2
        self.sc = Sched(nc)
        self.expose = set(expose)

        def dram(name, shape, dt, kind="Internal"):
            if kind == "Internal" and name in self.expose:
                kind = "ExternalOutput"
            return nc.dram_tensor(name, shape, dt, kind=kind)
        self.x_in = dram("x", [NT * 128, D], F32, kind="ExternalInput").ap()
        self.wq_in = dram("Wq", [D, D], F32, kind="ExternalInput").ap()
        self.wk_in = dram("Wk", [D, D], F32, kind="ExternalInput").ap()
        self.wv_in = dram("Wv", [D, D], F32, kind="ExternalInput").ap()
        self.wfc_in = dram("Wfc", [D, D], F32, kind="ExternalInput").ap()
        self.bfc_in = dram("bfc", [1, D], F32, kind="ExternalInput").ap()
        self.gam_in = dram("gamma", [1, D], F32, kind="ExternalInput").ap()
        self.bet_in = dram("beta", [1, D], F32, kind="ExternalInput").ap()
        self.wg_in = dram("w_gate", [I, D], F32, kind="ExternalInput").ap()
        self.wu_in = dram("w_up", [I, D], F32, kind="ExternalInput").ap()
        self.wd_in = dram("w_down", [D, I], F32, kind="ExternalInput").ap()
        self.tri_in = dram("tri", [128, 128], BF16, kind="ExternalInput").ap()
        self.y_out = dram("y", [NT * 128, D], F32, kind="ExternalOutput").ap()
        self.x1_hbm = dram("x1s", [NT * 128, D], BF16).ap()
        self.x2_hbm = dram("x2s", [NT * 128, D], BF16).ap()
        self.kt_hbm = dram("kts", [self.NQ, 128, self.DCH * QT * 128], BF16).ap()
        self.v_hbm = dram("vs", [NT, 128, D], BF16).ap()
        self.wg_hbm = dram("wgs", [128, self.DCH * I], BF16).ap()
        self.wu_hbm = dram("wus", [128, self.DCH * I], BF16).ap()
        self.wd_hbm = dram("wds", [128, self.ICH * D], BF16).ap()
        self.wa_hbm = dram("was", [128, 4 * self.DCH * D], BF16).ap()

        self.AB = Arena(nc, "AB", 86784, BF16)
        self.AF = Arena(nc, "AF", 9504, F32)
        self.c_eps = nc.alloc_sbuf_tensor("c_eps", [128, 1], F32).ap()
        self.c_b20 = nc.alloc_sbuf_tensor("c_b20", [128, 1], F32).ap()
        self.c_s32 = nc.alloc_sbuf_tensor("c_s32", [128, 1], F32).ap()
        self.c_one = nc.alloc_sbuf_tensor("c_one", [128, 1], F32).ap()
        self.c_mh = nc.alloc_sbuf_tensor("c_mh", [128, 1], F32).ap()
        self.tri = nc.alloc_sbuf_tensor("tri_sb", [128, 128], BF16).ap()
        self.ps = [nc.alloc_psum_tensor(f"ps{i}", [128, 512], F32) for i in range(8)]

    def v(self, emit, waits=(), inc=None, tag=""):
        return self.sc.op("vector", emit, waits, inc, tag)

    def scl(self, emit, waits=(), inc=None, tag=""):
        return self.sc.op("scalar", emit, waits, inc, tag)

    def gdma(self, emit, waits=(), inc=None, tag=""):
        return self.sc.op("gpsimd", emit, waits, inc, tag)

    def sy(self, emit, waits=(), inc=None, tag=""):
        return self.sc.op("sync", emit, waits, inc, tag)

    def t(self, emit, waits=(), inc=None, tag=""):
        return self.sc.op("tensor", emit, waits, inc, tag)

    def emit_quant(self, qin32, N, waits_src, out_q, rel_key=None, cmp_bf16=False):
        s = self.sc
        q = self.qb
        NB = N // 16
        NH = N // 2
        NBH = NB // 2
        pre = "qs"
        w_src = list(waits_src)
        H = ((0, NH, 0, NBH), (NH, N, NBH, NB))
        sp = self.qpar
        self.qpar ^= 1
        sgbuf = q["sg"][sp]
        sgg = self.sg_guard[sp] or []

        gtok = []
        for (lo, hi, _, _) in H:
            gtok.append(self.scl(lambda e, lo=lo, hi=hi: e.activation(
                out=sgbuf[:, lo:hi], in_=qin32[:, lo:hi], func=ACT.Sign),
                waits=w_src + sgg, inc=(pre + "_as", 1)))

        def VH(emitfn, dep=(), extra=()):
            toks = []
            for i, (lo, hi, blo, bhi) in enumerate(H):
                w = list(extra)
                if dep:
                    w.append(dep[i])
                toks.append(self.v(lambda e, lo=lo, hi=hi, blo=blo, bhi=bhi:
                                   emitfn(e, lo, hi, blo, bhi),
                                   waits=w, inc=(pre + "_v", 1)))
            return toks

        ab_ = []
        for i, (lo, hi, _, _) in enumerate(H):
            ab_.append(self.scl(lambda e, lo=lo, hi=hi: e.activation(
                out=qin32[:, lo:hi], in_=qin32[:, lo:hi], func=ACT.Abs),
                waits=[gtok[i]], inc=(pre + "_as", 1)))
        bm = VH(lambda e, lo, hi, blo, bhi: e.tensor_reduce(
            out=q["bm"][:, blo:bhi].rearrange("p (b i) -> p b i", i=1),
            in_=qin32[:, lo:hi].rearrange("p (b i) -> p b i", i=16),
            axis=AX.X, op=ALU.max), dep=ab_)
        s32 = VH(lambda e, lo, hi, blo, bhi: e.tensor_scalar(
            out=q["s32"][:, blo:bhi], in0=q["bm"][:, blo:bhi],
            scalar1=1.0 / 6.0, scalar2=1e-12, op0=ALU.mult, op1=ALU.max), dep=bm)
        r32 = VH(lambda e, lo, hi, blo, bhi: e.reciprocal(
            out=q["r32"][:, blo:bhi], in_=q["s32"][:, blo:bhi]), dep=s32)
        adst = q["axn"] if cmp_bf16 else qin32
        t1 = VH(lambda e, lo, hi, blo, bhi: e.tensor_tensor(
            out=adst[:, lo:hi].rearrange("p (b i) -> p b i", i=16),
            in0=qin32[:, lo:hi].rearrange("p (b i) -> p b i", i=16),
            in1=q["r32"][:, blo:bhi].rearrange("p (b i) -> p b i", i=1)
                .broadcast_to([128, NBH, 16]),
            op=ALU.mult), dep=r32)
        csrc = adst
        t2 = VH(lambda e, lo, hi, blo, bhi: e.tensor_single_scalar(
            out=q["t2"][:, lo:hi], in_=csrc[:, lo:hi], scalar=0.25, op=ALU.is_gt),
            dep=t1)
        cbp = t1
        for th in (0.75, 1.25, 1.75):
            cb = VH(lambda e, lo, hi, blo, bhi, th=th: e.tensor_single_scalar(
                out=q["cb"][:, lo:hi], in_=csrc[:, lo:hi], scalar=th, op=ALU.is_gt),
                dep=t1 if th == 0.75 else cbp)
            t2 = VH(lambda e, lo, hi, blo, bhi: e.tensor_add(
                out=q["t2"][:, lo:hi], in0=q["t2"][:, lo:hi], in1=q["cb"][:, lo:hi]),
                dep=[cb[0], cb[1]], extra=[t2[0], t2[1]])
            cbp = t2
        b1 = VH(lambda e, lo, hi, blo, bhi: e.tensor_single_scalar(
            out=q["cb"][:, lo:hi], in_=csrc[:, lo:hi], scalar=2.5, op=ALU.is_gt),
            dep=t2)
        b3 = VH(lambda e, lo, hi, blo, bhi: e.scalar_tensor_tensor(
            out=q["cb"][:, lo:hi], in0=csrc[:, lo:hi], scalar=3.5, in1=q["cb"][:, lo:hi],
            op0=ALU.is_gt, op1=ALU.add), dep=b1)
        eh = VH(lambda e, lo, hi, blo, bhi: e.tensor_single_scalar(
            out=q["t2"][:, lo:hi], in_=q["t2"][:, lo:hi], scalar=0.5, op=ALU.mult),
            dep=t2)
        e1 = VH(lambda e, lo, hi, blo, bhi: e.tensor_add(
            out=q["t2"][:, lo:hi], in0=q["t2"][:, lo:hi], in1=q["cb"][:, lo:hi]),
            dep=[b3[0], b3[1]], extra=[eh[0], eh[1]])
        c7 = VH(lambda e, lo, hi, blo, bhi: e.tensor_scalar(
            out=q["cb"][:, lo:hi], in0=csrc[:, lo:hi], scalar1=5.0, scalar2=2.0,
            op0=ALU.is_gt, op1=ALU.mult), dep=e1)
        if rel_key is not None:
            self.qrel[rel_key] = [c7[0], c7[1]]
        e2 = VH(lambda e, lo, hi, blo, bhi: e.tensor_add(
            out=q["t2"][:, lo:hi], in0=q["t2"][:, lo:hi], in1=q["cb"][:, lo:hi]),
            dep=[c7[0], c7[1]], extra=[e1[0], e1[1]])
        ss = VH(lambda e, lo, hi, blo, bhi: e.tensor_tensor(
            out=q["axn"][:, lo:hi].rearrange("p (b i) -> p b i", i=16),
            in0=sgbuf[:, lo:hi].rearrange("p (b i) -> p b i", i=16),
            in1=q["s32"][:, blo:bhi].rearrange("p (b i) -> p b i", i=1)
                .broadcast_to([128, NBH, 16]),
            op=ALU.mult), dep=e2, extra=[s32[0], s32[1]])
        self.sg_guard[sp] = [ss[0], ss[1]]
        (lo0, hi0, _, _), (lo1, hi1, _, _) = H
        tok0 = self.v(lambda e: e.tensor_tensor(
            out=out_q[:, lo0:hi0], in0=q["t2"][:, lo0:hi0], in1=q["axn"][:, lo0:hi0],
            op=ALU.mult), waits=[ss[0]] + list(self.qout_free), inc=(pre + "_v", 1))
        return self.v(lambda e: e.tensor_tensor(
            out=out_q[:, lo1:hi1], in0=q["t2"][:, lo1:hi1], in1=q["axn"][:, lo1:hi1],
            op=ALU.mult), waits=[ss[1], tok0], inc=(pre + "_q", 1))

    def emit_ln_stats(self, src, N, pre, use_expln, rcol=0):
        st = self.lnb["stats"][rcol % 4]
        rs = self.lnb["rs"][rcol % 4]
        s = self.sc
        nch = (N + 511) // 512
        prev_rs = max(0, s.count(pre + "_rs") - 1)

        w0 = list(self.lnb["waits"]) + ([(pre + "_rs", prev_rs)] if prev_rs else [])
        btok = []
        for c in range(nch):
            lo = c * 512
            hi = min(N, lo + 512)
            btok.append(self.v(lambda e, c=c, lo=lo, hi=hi: e.bn_stats(
                out=st[:, 6 * c:6 * c + 6], in_=src[:, lo:hi]), waits=w0,
                inc=("qs_v", 1)))
        ag = self.v(lambda e: e.bn_aggr(out=st[:, 12:14], in_=st[:, 0:6 * nch]),
                    waits=btok, inc=("qs_v", 1))
        ve = self.v(lambda e: e.tensor_scalar(out=st[:, 14:15], in0=st[:, 13:14],
                                              scalar1=1e-5, scalar2=None, op0=ALU.add),
                    waits=[ag], inc=("qs_v", 1))
        v1 = self.v(lambda e: e.reciprocal(out=st[:, 15:16], in_=st[:, 14:15]),
                    waits=[ve], inc=(pre + "_st", 1))
        rc = rs[:, 0:1]
        if use_expln:
            lt = self.scl(lambda e: e.activation(out=rc, in_=st[:, 13:14],
                                                 func=ACT.Ln, bias=self.c_eps,
                                                 scale=self.c_one),
                          waits=[v1, ("c_ini", 1)], inc=("qs_s", 1))
            r1 = self.scl(lambda e: e.activation(out=rc, in_=rc,
                                                 func=ACT.Exp, scale=self.c_mh),
                          waits=[lt], inc=(pre + "_rs", 1))
        else:
            r1 = self.scl(lambda e: e.activation(out=rc, in_=st[:, 15:16],
                                                 func=ACT.Sqrt),
                          waits=[v1, ("c_ini", 1)], inc=(pre + "_rs", 1))
        return r1

    def alloc_ffn_bufs(self, with_prep):
        AB, AF = self.AB, self.AF
        I, D, IH = self.I, self.D, self.IH
        b = {}
        b["wgqT"] = AB.alloc(self.DCH * I)
        b["wuqT"] = AB.alloc(self.DCH * I)
        b["wdqT"] = AB.alloc(self.ICH * D)
        b["txq"] = [AB.alloc(D) for _ in range(2)]
        b["hsg"] = AB.alloc(IH)
        b["qout"] = AB.alloc(IH)
        b["thq"] = [AB.alloc(IH), AB.alloc(IH)]
        b["xst"] = [AB.alloc(D) for _ in range(2)] if with_prep else None
        if with_prep:
            b["wst"] = [AF.alloc(IH) for _ in range(2)]
            b["wab"] = [AB.alloc(D), AB.alloc(D)]
        q = {}
        q["axn"] = AB.alloc(IH)
        q["sg"] = [AB.alloc(IH), AB.alloc(IH)]
        q["t2"] = AB.alloc(IH)
        q["cb"] = AB.alloc(IH)
        _qs = AF.alloc(3 * (IH // 16))
        q["bm"] = _qs[:, 0:IH // 16]
        q["s32"] = _qs[:, IH // 16:2 * (IH // 16)]
        q["r32"] = _qs[:, 2 * (IH // 16):3 * (IH // 16)]
        q["in32"] = [AF.alloc(IH), AF.alloc(IH)]
        b["q"] = q
        b["xin"] = [AF.alloc(D) for _ in range(3)] if with_prep else None
        b["x2in"] = None if with_prep else [AB.alloc(D) for _ in range(3)]
        b["yst"] = None if with_prep else [AF.alloc(D) for _ in range(2)]
        stt_ = AF.alloc(128)
        b["stats"] = [stt_[:, 32 * i:32 * i + 16] for i in range(4)]
        b["rs"] = [stt_[:, 32 * i + 16:32 * i + 20] for i in range(4)]
        return b

    def emit_consts(self):
        self.v(lambda e: e.memset(self.c_eps, 1e-5))
        self.v(lambda e: e.memset(self.c_b20, -20.0))
        self.v(lambda e: e.memset(self.c_s32, 1.0 / 32.0))
        self.v(lambda e: e.memset(self.c_one, 1.0))
        self.v(lambda e: e.memset(self.c_mh, -0.5), inc=("c_ini", 1))
        self.sy(lambda e: e.dma_start(out=self.tri, in_=self.tri_in), inc=("tri_ld", 16))

    def emit_wchunk(self, kind, c, sub):
        s = self.sc
        fb = self.fb
        D, I, IH = self.D, self.I, self.IH
        j = self.wj % 2
        self.wj += 1
        wst = fb["wst"][j]
        wprev = self.qrel.get(("wst", j), [])
        if kind in ("g", "u"):
            N = D
            src_h = self.wg_in if kind == "g" else self.wu_in
            ld = self.gdma(lambda e: e.dma_start(out=wst[:, :N],
                                                 in_=src_h[c * 128:(c + 1) * 128, :]),
                           waits=wprev + [("w_ld", s.count("w_ld"))],
                           inc=("w_ld", 16))
        else:
            N = IH
            ld = self.gdma(lambda e: e.dma_start(
                out=wst[:, :N], in_=self.wd_in[c * 128:(c + 1) * 128, sub * IH:(sub + 1) * IH]),
                waits=wprev + [("w_ld", s.count("w_ld"))],
                inc=("w_ld", 16))
        wsb = fb["qout"]
        qt = self.emit_quant(wst[:, :N], N, [ld, ("w_t", s.count("w_t"))], wsb[:, :N],
                             rel_key=("wst", j))
        if kind in ("g", "u"):
            dst = fb["wgqT"] if kind == "g" else fb["wuqT"]
            out_ap = dst.rearrange("p (d i) -> p d i", i=I)[:, :, c * 128:(c + 1) * 128]
            in_ap = wsb[:, :N].rearrange("p (d r) -> p d r", r=128)
        else:
            dst = fb["wdqT"]
            out_ap = dst.rearrange("p (i d) -> p i d", d=D)[:, sub * 11:(sub + 1) * 11,
                                                           c * 128:(c + 1) * 128]
            in_ap = wsb[:, :N].rearrange("p (i r) -> p i r", r=128)
        tt = self.sy(lambda e: e.dma_start_transpose(out=out_ap, in_=in_ap),
                     waits=[qt, ("w_t", s.count("w_t"))], inc=("w_t", 16))
        self.qout_free = [tt]

    def emit_ffn_phase(self, first):
        s = self.sc
        NT = self.NT
        pre = "a" if first else "c"
        self.AB.mark()
        self.AF.mark()
        fb = self.alloc_ffn_bufs(with_prep=first)
        self.fb = fb
        self.qb = fb["q"]
        self.qpar = 0
        self.qinpar = 0
        self.sg_guard = [None, None]
        self.qrel = {}
        self.qout_free = []
        self.lnb = {"stats": fb["stats"], "rs": fb["rs"], "waits": []}
        if first:
            self.wj = 0
            order = ([("g", c, 0) for c in range(11)] + [("u", c, 0) for c in range(11)]
                     + [("g", c, 0) for c in range(11, 22)]
                     + [("u", c, 0) for c in range(11, 22)]
                     + [("d", c, sub) for c in range(8) for sub in range(2)])
            for (kind, c, sub) in order:
                self.emit_wchunk(kind, c, sub)
            self.w_thr = {"gu0": 16 * 22, "gu1": 16 * 44, "d": 16 * 60}
            wt_all = 16 * 60
            for (dst_h, src_a) in ((self.wg_hbm, fb["wgqT"]), (self.wu_hbm, fb["wuqT"]),
                                   (self.wd_hbm, fb["wdqT"])):
                self.gdma(lambda e, d=dst_h, a=src_a: e.dma_start(out=d, in_=a),
                          waits=[("w_t", wt_all)], inc=("w_sp", 16))
            self.emit_ffn_xpart(0, pre, first)
            self.emit_ffn_xrest(pre, first)
            self.emit_ffn_xpart(1, pre, first)
            xpre = 2
            self.wa_state = {"z_ld": 0, "z_cs": 0,
                             "srcs": (self.wq_in, self.wk_in, self.wv_in, self.wfc_in)}
            self.emit_wa_ld()
            self.emit_wa_ld()
        else:
            xpre = 0
            nld = 0
            early = [("b_qkmm", self.sc.count("b_qkmm")), ("b_vmm", self.sc.count("b_vmm")),
                     ("b_smm", self.sc.count("b_smm")), ("b_exp", self.sc.count("b_exp")),
                     ("b_att_done", self.sc.count("b_att_done")),
                     ("b_x2s", self.sc.count("b_x2s"))]
            for (src_h, dst_a) in ((self.wg_hbm, fb["wgqT"]), (self.wu_hbm, fb["wuqT"]),
                                   (self.wd_hbm, fb["wdqT"])):
                for piece in range(4):
                    n = src_h.shape[1] // 4
                    gate = early if nld < 3 else list(self.barrier_waits("gpsimd"))
                    self.gdma(lambda e, sh=src_h, da=dst_a, p=piece, n=n:
                              e.dma_start(out=da[:, p * n:(p + 1) * n],
                                          in_=sh[:, p * n:(p + 1) * n]),
                              waits=gate + [("w3_ld", self.sc.count("w3_ld"))],
                              inc=("w3_ld", 16))
                    nld += 1
            self.w_thr = {"gu0": 16 * nld, "gu1": 16 * nld, "d": 16 * nld}
            self.w_thr["sem"] = "w3_ld"

        if xpre == 0:
            self.emit_ffn_xpart(0, pre, first)
            self.emit_ffn_xrest(pre, first)
        for t in range(NT):
            self.emit_ffn_gu(t, 0, pre, "w_t" if first else "w3_ld")
            if t + 1 < NT and t + 1 >= xpre:
                self.emit_ffn_xpart(t + 1, pre, first)
            self.emit_ffn_evhm(t, 0, pre)
            if t >= 1:
                self.emit_ffn_down(t - 1, pre, first, 0)
            if t + 1 < NT:
                self.emit_ffn_xrest(pre, first)
            self.emit_ffn_hqchain(pre)
            self.emit_ffn_gu(t, 1, pre, "w_t" if first else "w3_ld")
            self.emit_ffn_evhm(t, 1, pre)
            if t >= 1:
                self.emit_ffn_down(t - 1, pre, first, 1)
                self.emit_ffn_resid(t - 1, pre, first)
            self.emit_ffn_hqchain(pre)
            if first:
                self.emit_wa_pair()
        self.emit_ffn_down(NT - 1, pre, first, 0)
        self.emit_ffn_down(NT - 1, pre, first, 1)
        self.emit_ffn_resid(NT - 1, pre, first)
        if first:
            while self.wa_state["z_cs"] < 4 * self.DCH:
                self.emit_wa_pair()
        for eng in Sched.ENGINES:
            bw = []
            if eng == "gpsimd":
                bw = [(pre + "_x1s", s.count(pre + "_x1s"))]
                if first:
                    bw.append(("w_sp", s.count("w_sp")))
            s.op(eng, lambda e: None, waits=bw,
                 inc=(("p1b_" if first else "p3b_") + eng, 1))
        self.AB.release()
        self.AF.release()

    def barrier_waits(self, eng, which="p2b_"):
        return [(which + e2, self.sc.count(which + e2)) for e2 in Sched.ENGINES]

    def emit_wa_ld(self):
        st_ = self.wa_state
        z = st_["z_ld"]
        if z >= 4 * self.DCH:
            return
        st_["z_ld"] += 1
        s = self.sc
        D = self.D
        fb = self.fb
        src_h = st_["srcs"][z // self.DCH]
        c = z % self.DCH
        j = self.wj % 2
        self.wj += 1
        wst = fb["wst"][j]
        wprev = self.qrel.get(("wst", j), [])
        ld = self.gdma(lambda e, sh=src_h, c=c, wst=wst: e.dma_start(
            out=wst[:, :D], in_=sh[c * 128:(c + 1) * 128, :]),
            waits=wprev + [("w_ld", s.count("w_ld"))], inc=("w_ld", 16))
        st_[("ld", z)] = (ld, wst)

    def emit_wa_pair(self):
        st_ = self.wa_state
        s = self.sc
        D = self.D
        fb = self.fb
        for _ in range(2):
            z = st_["z_cs"]
            if z >= 4 * self.DCH:
                return
            st_["z_cs"] += 1
            (ld, wst) = st_.pop(("ld", z))
            wb = fb["wab"][z % 2]
            cz = self.scl(lambda e, wst=wst, wb=wb: e.activation(
                out=wb[:, :D], in_=wst[:, :D], func=ACT.Copy),
                waits=[ld, ("wa_sp", max(0, 16 * (z - 1)))], inc=("w_ac", 1))
            self.qrel[("wst", z % 2)] = [cz]
            self.gdma(lambda e, z=z, wb=wb: e.dma_start(
                out=self.wa_hbm[:, z * D:(z + 1) * D], in_=wb[:, :D]),
                waits=[cz, ("wa_sp", s.count("wa_sp"))], inc=("wa_sp", 16))
        self.emit_wa_ld()
        self.emit_wa_ld()

    def emit_ffn_xpart(self, t, pre, first):
        s = self.sc
        fb = self.fb
        D = self.D
        if first:
            xsrc = fb["xin"][t % 3]
            ld = self.gdma(lambda e: e.dma_start(out=xsrc[:, :D],
                                                 in_=self.x_in[t * 128:(t + 1) * 128, :]),
                           waits=[(pre + "_x1", max(0, t - 2)),
                                  (pre + "_x0", s.count(pre + "_x0"))],
                           inc=(pre + "_x0", 16))
        else:
            xsrc = fb["x2in"][t % 3]
            ld = self.gdma(lambda e: e.dma_start(out=xsrc[:, :D],
                                                 in_=self.x2_hbm[t * 128:(t + 1) * 128, :]),
                           waits=[(pre + "_x1", max(0, t - 2)), ("b_x2s", 16 * (t + 1)),
                                  (pre + "_x0", s.count(pre + "_x0"))]
                           + (self.barrier_waits("gpsimd") if t < 3 else []),
                           inc=(pre + "_x0", 16))
        self.lnb["waits"] = [ld]
        r1 = self.emit_ln_stats(xsrc[:, :D], D, pre + "_ln", use_expln=False, rcol=t % 2)
        self.xp_state = (t, xsrc, r1)

    def emit_ffn_xrest(self, pre, first):
        s = self.sc
        fb = self.fb
        D = self.D
        (t, xsrc, r1) = self.xp_state
        st = fb["stats"][t % 2]
        qp = self.qinpar
        self.qinpar ^= 1
        qin = fb["q"]["in32"][qp]
        rcap = fb["rs"][t % 2][:, 0:1]
        ax = self.v(lambda e: e.tensor_scalar(out=qin[:, :D], in0=xsrc[:, :D],
                                              scalar1=st[:, 12:13], scalar2=rcap,
                                              op0=ALU.subtract, op1=ALU.mult),
                    waits=[r1] + self.qrel.get(("qin", qp), []),
                    inc=(pre + "_xn", 1))
        qx = self.emit_quant(qin[:, :D], D, [ax], fb["qout"][:, :D],
                             rel_key=("qin", qp))
        par = t % 2
        tt = self.sy(lambda e: e.dma_start_transpose(
            out=fb["txq"][par].rearrange("p (d r) -> p d r", r=128),
            in_=fb["qout"][:, :D].rearrange("p (d r) -> p d r", r=128)),
            waits=[qx, (pre + "_gu", max(0, 2 * (t - 1))),
                   (pre + "_txq", s.count(pre + "_txq"))], inc=(pre + "_txq", 16))
        self.qout_free = [tt]

    def emit_ffn_gu(self, t, h, pre, wsem):
        fb = self.fb
        IH = self.IH
        par = t % 2
        ncol = [512, 512, IH - 1024]
        wthr = self.w_thr["gu0"] if h == 0 else self.w_thr["gu1"]

        def gu(e):
            last = None
            for d in range(self.DCH):
                lhsT = fb["txq"][par][:, d * 128:(d + 1) * 128]
                for c in range(3):
                    w = ncol[c]
                    off = d * self.I + h * IH + c * 512
                    e.matmul(self.ps[c].ap()[:, :w], lhsT,
                             fb["wgqT"][:, off:off + w],
                             start=(d == 0), stop=(d == self.DCH - 1))
                    last = e.matmul(self.ps[3 + c].ap()[:, :w], lhsT,
                                    fb["wuqT"][:, off:off + w],
                                    start=(d == 0), stop=(d == self.DCH - 1))
            return last
        self.t(gu, waits=[(pre + "_txq", 16 * (t + 1)), (wsem, wthr),
                          (pre + "_ev", max(0, 2 * t + h)),
                          (pre + "_hm", max(0, 2 * t + h))], inc=(pre + "_gu", 1))

    def emit_ffn_evhm(self, t, h, pre):
        s = self.sc
        fb = self.fb
        IH = self.IH
        ncol = [512, 512, IH - 1024]

        def ev(e):
            last = None
            for c in range(3):
                w = ncol[c]
                last = e.activation(out=fb["hsg"][:, c * 512:c * 512 + w],
                                    in_=self.ps[c].ap()[:, :w], func=ACT.Silu)
            return last
        evt = self.sc.op("scalar", ev,
                         waits=[(pre + "_gu", 2 * t + h + 1), (pre + "_hm", max(0, 2 * t + h))],
                         inc=(pre + "_ev", 1))
        qp = self.qinpar
        self.qinpar ^= 1
        qin = fb["q"]["in32"][qp]

        def hmul(e):
            last = None
            for c in range(3):
                w = ncol[c]
                last = e.tensor_tensor(out=qin[:, c * 512:c * 512 + w],
                                       in0=fb["hsg"][:, c * 512:c * 512 + w],
                                       in1=self.ps[3 + c].ap()[:, :w], op=ALU.mult)
            return last
        hm = self.sc.op("vector", hmul,
                        waits=[evt] + self.qrel.get(("qin", qp), []),
                        inc=(pre + "_hm", 1))
        self.hq_state = (t, h, qp, qin, hm)

    def emit_ffn_hqchain(self, pre):
        s = self.sc
        fb = self.fb
        IH = self.IH
        (t, h, qp, qin, hm) = self.hq_state
        qh = self.emit_quant(qin[:, :IH], IH, [hm], fb["qout"][:, :IH],
                             rel_key=("qin", qp))
        tt = self.sy(lambda e: e.dma_start_transpose(
            out=fb["thq"][h].rearrange("p (i r) -> p i r", r=128),
            in_=fb["qout"][:, :IH].rearrange("p (i r) -> p i r", r=128)),
            waits=[qh, (pre + "_dn", max(0, t)),
                   (pre + "_thq", self.sc.count(pre + "_thq"))], inc=(pre + "_thq", 16))
        self.qout_free = [tt]

    def emit_ffn_down(self, t, pre, first, colh):
        fb = self.fb
        D = self.D
        wsem = "w_t" if first else "w3_ld"

        def dn(e):
            last = None
            for i in range(self.ICH):
                hh = 0 if i < 11 else 1
                ioff = (i - 11 * hh) * 128
                last = e.matmul(self.ps[6 + colh].ap()[:, :512],
                                fb["thq"][hh][:, ioff:ioff + 128],
                                fb["wdqT"][:, i * D + colh * 512: i * D + colh * 512 + 512],
                                start=(i == 0), stop=(i == self.ICH - 1))
            return last
        if colh == 0:
            self.t(dn, waits=[(pre + "_thq", 16 * (2 * t + 2)),
                              (wsem, self.w_thr["d"]),
                              (pre + "_x1", max(0, t))])
        else:
            self.t(dn, inc=(pre + "_dn", 1))

    def emit_ffn_resid(self, t, pre, first):
        s = self.sc
        fb = self.fb
        D = self.D
        par = t % 2
        xsrc = (fb["xin"] if first else fb["x2in"])[t % 3]
        dst = (fb["xst"] if first else fb["yst"])[par]
        out_hbm = self.x1_hbm if first else self.y_out
        self.v(lambda e: e.scalar_tensor_tensor(
            out=dst[:, 0:512], in0=self.ps[6].ap()[:, :512],
            scalar=0.5, in1=xsrc[:, 0:512], op0=ALU.mult, op1=ALU.add),
            waits=[(pre + "_dn", s.count(pre + "_dn")),
                   (pre + "_x1s", max(0, 16 * (t - 1)))])
        r2 = self.v(lambda e: e.scalar_tensor_tensor(
            out=dst[:, 512:1024], in0=self.ps[7].ap()[:, :512],
            scalar=0.5, in1=xsrc[:, 512:1024], op0=ALU.mult, op1=ALU.add),
            inc=(pre + "_x1", 1))
        self.gdma(lambda e: e.dma_start(out=out_hbm[t * 128:(t + 1) * 128, :],
                                        in_=dst[:, :D]),
                  waits=[r2, (pre + "_x1s", s.count(pre + "_x1s"))],
                  inc=(pre + "_x1s", 16))

    def finalize(self):
        self.gdma(lambda e: None, waits=[("c_x1s", self.sc.count("c_x1s"))])

    def alloc_attn_bufs(self):
        AB, AF = self.AB, self.AF
        D, QT = self.D, self.QT
        b = {}
        b["wq"] = AB.alloc(self.DCH * D)
        b["wk"] = AB.alloc(self.DCH * D)
        b["wv"] = AB.alloc(self.DCH * D)
        b["wfc"] = AB.alloc(self.DCH * D)
        b["xnT"] = AB.alloc(self.DCH * QT * 128)
        b["qT"] = AB.alloc(self.DCH * QT * 128)
        b["ktc"] = AB.alloc(self.DCH * QT * 128)
        b["vc"] = AB.alloc(QT * D)
        gkn = self.DCH * min(4, QT) * 128
        if 2 * gkn <= self.DCH * QT * 128:
            b["ktg"] = [b["xnT"][:, 0:gkn], b["xnT"][:, gkn:2 * gkn]]
        else:
            b["ktg"] = [AB.alloc(gkn) for _ in range(2)]
        b["vg"] = [AB.alloc(4 * D) for _ in range(2)]
        b["x1q"] = [AB.alloc(D) for _ in range(2)]
        b["x1r"] = [AB.alloc(D) for _ in range(2)]
        b["x2st"] = [AB.alloc(D) for _ in range(2)]
        b["att"] = [AB.alloc(512) for _ in range(3)]
        b["attT"] = [AB.alloc(512) for _ in range(3)]
        b["lnr"] = AB.alloc(D)
        b["xnb"] = [b["lnr"]]
        b["lnrT"] = [AB.alloc(D), AB.alloc(D)]
        b["gam"] = AF.alloc(D)
        b["bet"] = AF.alloc(D)
        b["bfc"] = AF.alloc(D)
        _st2 = AF.alloc(128)
        b["stats"] = [_st2[:, 32 * i:32 * i + 16] for i in range(4)]
        b["rs"] = [_st2[:, 32 * i + 16:32 * i + 20] for i in range(4)]
        b["wstg"] = [AF.alloc(D) for _ in range(2)]
        b["den"] = AF.alloc(2 * QT)
        b["dpart"] = AF.alloc(4)
        b["ov"] = AF.alloc(D)
        return b

    def emit_p2(self):
        s = self.sc
        NT, QT, NQ, D = self.NT, self.QT, self.NQ, self.D
        self.AB.mark()
        self.AF.mark()
        ab = self.alloc_attn_bufs()
        self.ab2 = ab
        self.qb = None
        self.lnb = {"stats": ab["stats"], "rs": ab["rs"], "waits": []}
        p1w = self.barrier_waits("gpsimd", "p1b_")

        for (wi, dst) in enumerate((ab["wq"], ab["wk"], ab["wv"], ab["wfc"])):
            self.gdma(lambda e, wi=wi, dst=dst: e.dma_start(
                out=dst, in_=self.wa_hbm[:, wi * self.DCH * self.D:
                                         (wi + 1) * self.DCH * self.D]),
                waits=p1w + [("wa_sp", 16 * 32), ("b_wld", s.count("b_wld"))],
                inc=("b_wld", 16))
        self.sc.op("scalar", lambda e: None,
                   waits=[("b_wld", s.count("b_wld"))], inc=("b_wc", 32))
        for (src_h, dst) in ((self.gam_in, ab["gam"]), (self.bet_in, ab["bet"]),
                             (self.bfc_in, ab["bfc"])):
            self.sy(lambda e, sh=src_h, dst=dst: e.dma_start(
                out=dst, in_=sh.broadcast_to([128, D])),
                waits=self.barrier_waits("sync", "p1b_") + [("b_cld", s.count("b_cld"))],
                inc=("b_cld", 16))

        self.pend_fin = []
        self.fc_pred = 0
        self.gr_last_av = {0: 0, 1: 0}
        for k in range(NQ):
            self.den_first = {tau: True for tau in range(self.QT)}
            self.o_first = {tau: [True, True] for tau in range(self.QT)}
            self.gr_free = {0: [], 1: []}
            self.emit_p2_qkv(k)
            self.emit_p2_attn(k)
            while self.pend_fin:
                self.emit_p2_finB()
        for eng in Sched.ENGINES:
            bw = []
            if eng == "gpsimd":
                bw = [(n, s.count(n)) for n in
                      ("b_ktsp", "b_vsp", "b_x2s", "b_gl", "b_x1l", "b_x1rl", "b_wld")]
            s.op(eng, lambda e: None, waits=bw, inc=("p2b_" + eng, 1))
        self.AB.release()
        self.AF.release()

    def emit_p2_qkv(self, k):
        s = self.sc
        ab = self.ab2
        D, QT = self.D, self.QT
        xnt_base = s.count("b_xnt")
        for tau in range(QT):
            t = k * QT + tau
            xb = ab["x1q"][tau % 2]
            ld = self.gdma(lambda e, t=t, xb=xb: e.dma_start(
                out=xb[:, :D], in_=self.x1_hbm[t * 128:(t + 1) * 128, :]),
                waits=[("a_x1s", 16 * self.NT), ("b_x1f", max(0, s.count("b_x1f") - 1)),
                       ("b_x1l", s.count("b_x1l"))],
                inc=("b_x1l", 16))
            self.lnb["waits"] = [ld]
            r1 = self.emit_ln_stats(xb[:, :D], D, "b_ln", use_expln=True, rcol=tau % 2)
            xn = ab["xnb"][0]
            st = ab["stats"][tau % 2]
            rcap = ab["rs"][tau % 2][:, 0:1]
            ax = self.v(lambda e, xb=xb, xn=xn, st=st, rcap=rcap: e.tensor_scalar(
                out=xn[:, :D], in0=xb[:, :D], scalar1=st[:, 12:13], scalar2=rcap,
                op0=ALU.subtract, op1=ALU.mult),
                waits=[r1, ("b_xnt", s.count("b_xnt")),
                       ("b_lnt", s.count("b_lnt"))],
                inc=("b_x1f", 1))
            self.sy(lambda e, xn=xn, tau=tau: e.dma_start_transpose(
                out=ab["xnT"].rearrange("p (d r) -> p d r", r=QT * 128)[:, :, tau * 128:(tau + 1) * 128],
                in_=xn[:, :D].rearrange("p (d r) -> p d r", r=128)),
                waits=[ax, ("b_xnt", s.count("b_xnt")),
                       ("b_qkmm", s.count("b_qkmm")), ("b_vmm", s.count("b_vmm")),
                       ("b_exp", s.count("b_exp")), ("b_gl", s.count("b_gl"))],
                inc=("b_xnt", 16))
        rhw = min(512, QT * 128)
        nrh = (QT * 128) // rhw
        qkctr = 0
        for rh in range(nrh):
            xnt_rh = xnt_base + 16 * (rh + 1) * (rhw // 128)
            for (wname, dsts, spill) in (("wq", "qT", False), ("wk", "ktc", True)):
                w = ab[wname]
                dst = ab[dsts]
                for dk in range(self.DCH):
                    pspar = qkctr % 2
                    qkctr += 1
                    def mm(e, w=w, dk=dk, rh=rh, pspar=pspar):
                        last = None
                        for d in range(self.DCH):
                            last = e.matmul(self.ps[pspar].ap()[:, :rhw],
                                            w[:, d * D + dk * 128: d * D + dk * 128 + 128],
                                            ab["xnT"][:, d * QT * 128 + rh * rhw:
                                                      d * QT * 128 + rh * rhw + rhw],
                                            start=(d == 0), stop=(d == self.DCH - 1))
                        return last
                    mt = self.t(mm, waits=[("b_xnt", xnt_rh), ("b_wc", 32),
                                           ("b_exp", s.count("b_exp")),
                                           ("b_qkev", max(0, s.count("b_qkev") - 1))],
                                inc=("b_qkmm", 1))
                    self.scl(lambda e, dst=dst, dk=dk, rh=rh, pspar=pspar: e.activation(
                        out=dst[:, dk * QT * 128 + rh * rhw: dk * QT * 128 + rh * rhw + rhw],
                        in_=self.ps[pspar].ap()[:, :rhw], func=ACT.Copy),
                        waits=[mt, ("b_att_done", s.count("b_att_done")),
                               ("b_ktsp", s.count("b_ktsp"))],
                        inc=("b_qkev", 1))
        for tau in range(QT):
            for vh in range(2):
                pspar = (tau * 2 + vh) % 2
                def mmv(e, tau=tau, vh=vh, pspar=pspar):
                    last = None
                    for d in range(self.DCH):
                        last = e.matmul(self.ps[2 + pspar].ap()[:, :512],
                                        ab["xnT"][:, d * QT * 128 + tau * 128:
                                                  d * QT * 128 + tau * 128 + 128],
                                        ab["wv"][:, d * D + vh * 512: d * D + vh * 512 + 512],
                                        start=(d == 0), stop=(d == self.DCH - 1))
                    return last
                mt = self.t(mmv, waits=[("b_xnt", xnt_base + 16 * (tau + 1)), ("b_wc", 32),
                                        ("b_oev", s.count("b_oev")),
                                        ("b_vev", max(0, s.count("b_vev") - 1))],
                            inc=("b_vmm", 1))
                self.scl(lambda e, tau=tau, vh=vh, pspar=pspar: e.activation(
                    out=ab["vc"][:, tau * D + vh * 512: tau * D + vh * 512 + 512],
                    in_=self.ps[2 + pspar].ap()[:, :512], func=ACT.Copy),
                    waits=[mt, ("b_att_done", s.count("b_att_done")),
                           ("b_vsp", s.count("b_vsp"))],
                    inc=("b_vev", 1))
        kev = s.count("b_qkev")
        vev = s.count("b_vev")
        self.gdma(lambda e, k=k: e.dma_start(out=self.kt_hbm[k], in_=ab["ktc"]),
                  waits=[("b_qkev", kev), ("b_ktsp", s.count("b_ktsp"))],
                  inc=("b_ktsp", 16))
        self.gdma(lambda e, k=k: e.dma_start(
            out=self.v_hbm[k * QT:(k + 1) * QT].rearrange("t p d -> p t d"),
            in_=ab["vc"].rearrange("p (t d) -> p t d", d=D)),
            waits=[("b_vev", vev), ("b_vsp", s.count("b_vsp"))], inc=("b_vsp", 16))

    def emit_p2_attn(self, k):
        s = self.sc
        ab = self.ab2
        D, QT = self.D, self.QT
        gsz = min(4, self.QT)
        ngran = (self.QT // gsz) * k
        for g in range(QT // 2):
            taus = (2 * g, 2 * g + 1)
            gtoks = []
            base_av = s.count("b_av")
            for gr in range(ngran):
                q, gi = gr // 2, gr % 2
                par = gr % 2
                wk = [("b_ktsp", 16 * (q + 1)), ("b_vsp", 16 * (q + 1))]
                if gr >= 2:
                    wk.append(("b_av", base_av + 2 * (gr - 2) + 2))
                else:
                    if self.gr_last_av[par]:
                        wk.append(("b_av", self.gr_last_av[par]))
                    wk.append(("b_qkmm", s.count("b_qkmm")))
                    wk.append(("b_vmm", s.count("b_vmm")))
                gw = gsz * 128
                l1 = self.gdma(lambda e, q=q, gi=gi, par=par, gw=gw: e.dma_start(
                    out=ab["ktg"][par][:, :self.DCH * gw].rearrange("p (d r) -> p d r", r=gw),
                    in_=self.kt_hbm[q].rearrange("p (d r) -> p d r", r=QT * 128)
                    [:, :, gi * gw:(gi + 1) * gw]),
                    waits=wk + [("b_gl", s.count("b_gl"))], inc=("b_gl", 16))
                l2 = self.gdma(lambda e, q=q, gi=gi, par=par: e.dma_start(
                    out=ab["vg"][par].rearrange("p (t d) -> p t d", d=D)[:, :gsz],
                    in_=self.v_hbm[q * QT + gi * gsz: q * QT + gi * gsz + gsz]
                    .rearrange("t p d -> p t d")),
                    waits=[("b_gl", s.count("b_gl"))], inc=("b_gl", 16))
                gtoks.append((l1, l2, par))
            chunks = []
            for ci in range(ngran):
                for tau in taus:
                    chunks.append((tau, "g", ci, gsz, False))
            loc = {tau: [] for tau in taus}
            for tau in taus:
                lo = 0
                while lo < tau:
                    wdt = min(4, tau - lo)
                    loc[tau].append((tau, "l", lo, wdt, False))
                    lo += wdt
                loc[tau].append((tau, "l", tau, 1, True))
            mx = max(len(loc[t]) for t in taus)
            for i in range(mx):
                for tau in taus:
                    if i < len(loc[tau]):
                        chunks.append(loc[tau][i])
            for a_ in range(ngran):
                self.gr_last_av[a_ % 2] = base_av + 2 * a_ + 2
            self.oev_gate = s.count("b_oev")
            self.av_base = s.count("b_av")
            pend = []
            seen_gr = set()
            for ci, ch in enumerate(chunks):
                if ch[1] == "g" and ch[2] not in seen_gr:
                    seen_gr.add(ch[2])
                    if ch[2] >= 2:
                        while len(pend) > 2:
                            self.emit_p2_av(*pend.pop(0))
                info = self.emit_p2_scores(k, g, ci, ch, gtoks)
                pend.append(info)
                if ci in (1, 3):
                    self.emit_p2_finB()
                if len(pend) > 3:
                    self.emit_p2_av(*pend.pop(0))
            for p_ in pend:
                self.emit_p2_av(*p_)
            for tau in taus:
                self.emit_p2_fin(k, tau)
            if os.environ.get("MK_NODEFER") == "1":
                while self.pend_fin:
                    self.emit_p2_finB()

    def emit_p2_scores(self, k, g, ci, ch, gtoks):
        s = self.sc
        ab = self.ab2
        D, QT = self.D, self.QT
        (tau, kind, a, wt, diag) = ch
        w = wt * 128
        par = ci % 2
        apar = ci % 3
        gpar = gtoks[a][2] if kind == "g" else 0

        def mm(e):
            last = None
            for dk in range(self.DCH):
                lhsT = ab["qT"][:, dk * QT * 128 + tau * 128: dk * QT * 128 + tau * 128 + 128]
                if kind == "g":
                    gw = min(4, self.QT) * 128
                    rhs = ab["ktg"][gpar][:, dk * gw: dk * gw + w]
                else:
                    rhs = ab["ktc"][:, dk * QT * 128 + a * 128: dk * QT * 128 + a * 128 + w]
                last = e.matmul(self.ps[par].ap()[:, :w], lhsT, rhs,
                                start=(dk == 0), stop=(dk == self.DCH - 1))
            return last
        waits = [("b_exp", max(0, s.count("b_exp") - 1)),
                 ("b_qkev", s.count("b_qkev"))]
        if kind == "g":
            waits.append(gtoks[a][0])
        mt = self.t(mm, waits=waits, inc=("b_smm", 1))
        att = ab["att"][apar]
        dguard = ("b_den", max(0, s.count("b_den") - 1))
        if not diag:
            ex = self.scl(lambda e: e.activation(
                out=att[:, :w], in_=self.ps[par].ap()[:, :w], func=ACT.Exp,
                bias=self.c_b20, scale=self.c_s32, accum_out=ab["dpart"][:, par:par + 1]),
                waits=[mt, ("c_ini", 1), dguard,
                       ("b_att_t", max(0, s.count("b_att_t") - 32))],
                inc=("b_exp", 1))
            self.v(self.den_update_op(tau, par),
                   waits=[ex, ("b_oev", self.oev_gate)], inc=("b_den", 1))
            mk_tok = ex
        else:
            ex = self.scl(lambda e: e.activation(
                out=att[:, :w], in_=self.ps[par].ap()[:, :w], func=ACT.Exp,
                bias=self.c_b20, scale=self.c_s32),
                waits=[mt, ("c_ini", 1), dguard,
                       ("b_att_t", max(0, s.count("b_att_t") - 32))],
                inc=("b_exp", 1))
            mk_tok = self.v(lambda e: e.tensor_tensor(out=att[:, :w], in0=att[:, :w],
                                                      in1=self.tri, op=ALU.mult),
                            waits=[ex, ("tri_ld", 16)], inc=("b_msk", 1))
            rr = self.v(lambda e: e.tensor_reduce(
                out=ab["dpart"][:, par:par + 1].rearrange("p (a b) -> p a b", b=1),
                in_=att[:, :w].rearrange("p (a b) -> p a b", b=w),
                axis=AX.X, op=ALU.add), waits=[mk_tok], inc=("qs_v", 1))
            self.v(self.den_update_op(tau, par),
                   waits=[rr, ("b_oev", self.oev_gate)], inc=("b_den", 1))
        tr = self.sy(lambda e: e.dma_start_transpose(
            out=ab["attT"][apar][:, :w].rearrange("p (c r) -> p c r", r=128),
            in_=att[:, :w].rearrange("p (c r) -> p c r", r=128)),
            waits=[mk_tok, ("b_av", max(self.av_base, self.av_base + ci - 2)),
                   ("b_att_t", s.count("b_att_t"))],
            inc=("b_att_t", 16))
        vtok = gtoks[a][1] if kind == "g" else None
        return (ch, apar, gpar, tr, vtok)

    def den_update_op(self, tau, par):
        first = self.den_first[tau]
        self.den_first[tau] = False
        dst = self.ab2["den"][:, tau:tau + 1]
        srcp = self.ab2["dpart"][:, par:par + 1]
        if first:
            return lambda e: e.tensor_copy(out=dst, in_=srcp)
        return lambda e: e.tensor_tensor(out=dst, in0=dst, in1=srcp, op=ALU.add)

    def emit_p2_av(self, ch, par, gpar, tr, vtok=None):
        s = self.sc
        ab = self.ab2
        D = self.D
        (tau, kind, a, wt, diag) = ch
        obank = 2 + 2 * (tau % 2)

        firsts = (self.o_first[tau][0], self.o_first[tau][1])
        self.o_first[tau][0] = False
        self.o_first[tau][1] = False

        def mm(e):
            last = None
            for vh in range(2):
                for mt_ in range(wt):
                    lhsT = ab["attT"][par][:, mt_ * 128: mt_ * 128 + 128]
                    if kind == "g":
                        rhs = ab["vg"][gpar][:, mt_ * D + vh * 512: mt_ * D + vh * 512 + 512]
                    else:
                        rhs = ab["vc"][:, (a + mt_) * D + vh * 512: (a + mt_) * D + vh * 512 + 512]
                    last = e.matmul(self.ps[obank + vh].ap()[:, :512], lhsT, rhs,
                                    start=(firsts[vh] and mt_ == 0),
                                    stop=(diag and mt_ == wt - 1))
            return last
        w_ = [tr, ("b_oev", self.oev_gate)]
        if vtok is not None:
            w_.append(vtok)
        tok = self.t(mm, waits=w_, inc=("b_av", 1))
        if kind == "g":
            self.gr_free[gpar] = [tok]

    def emit_p2_fin(self, k, tau):
        s = self.sc
        ab = self.ab2
        D, QT = self.D, self.QT
        t = k * QT + tau
        obank = 2 + 2 * (tau % 2)
        den_n = s.count("b_den")
        par = tau % 2
        x1r = ab["x1r"][par]
        ld = self.gdma(lambda e: e.dma_start(out=x1r[:, :D],
                                             in_=self.x1_hbm[t * 128:(t + 1) * 128, :]),
                       waits=[("b_x1rl", s.count("b_x1rl")),
                              ("b_x2w", max(0, s.count("b_x2w") - 1))],
                       inc=("b_x1rl", 16))
        rd = self.v(lambda e: e.reciprocal(out=ab["den"][:, QT + tau:QT + tau + 1],
                                           in_=ab["den"][:, tau:tau + 1]),
                    waits=[("b_den", den_n), ("b_av", s.count("b_av"))],
                    inc=("qs_v", 1))
        rden = ab["den"][:, QT + tau:QT + tau + 1]
        e1 = self.v(lambda e: e.tensor_scalar(out=ab["ov"][:, 0:512],
                                              in0=self.ps[obank].ap()[:, :512],
                                              scalar1=rden, scalar2=None, op0=ALU.mult),
                    waits=[rd], inc=("b_oev", 1))
        e2 = self.v(lambda e: e.tensor_scalar(out=ab["ov"][:, 512:1024],
                                              in0=self.ps[obank + 1].ap()[:, :512],
                                              scalar1=rden, scalar2=None, op0=ALU.mult),
                    waits=[e1], inc=("b_oev", 1))
        av = self.v(lambda e: e.tensor_tensor(
            out=ab["ov"][:, :D], in0=ab["ov"][:, :D],
            in1=ab["vc"][:, tau * D:(tau + 1) * D], op=ALU.add),
            waits=[e2], inc=("qs_v", 1))
        self.lnb["waits"] = [av]
        r1 = self.emit_ln_stats(ab["ov"][:, :D], D, "b_ln2", use_expln=True,
                                rcol=2 + tau % 2)
        stt_ = ab["stats"][2 + tau % 2]
        rcap = ab["rs"][2 + tau % 2][:, 0:1]
        n1 = self.v(lambda e, stt_=stt_, rcap=rcap: e.tensor_scalar(
            out=ab["ov"][:, :D], in0=ab["ov"][:, :D],
            scalar1=stt_[:, 12:13], scalar2=rcap,
            op0=ALU.subtract, op1=ALU.mult),
                    waits=[r1], inc=("qs_v", 1))
        n2 = self.v(lambda e: e.tensor_tensor(out=ab["ov"][:, :D], in0=ab["ov"][:, :D],
                                              in1=ab["gam"][:, :D], op=ALU.mult),
                    waits=[n1, ("b_cld", 48)], inc=("qs_v", 1))
        n3 = self.v(lambda e: e.tensor_tensor(out=ab["lnr"][:, :D], in0=ab["ov"][:, :D],
                                              in1=ab["bet"][:, :D], op=ALU.add),
                    waits=[n2, ("b_lnt", s.count("b_lnt")),
                           ("b_xnt", s.count("b_xnt"))], inc=("qs_v", 1))
        lt = ab["lnrT"][par]
        tr = self.sy(lambda e: e.dma_start_transpose(
            out=lt.rearrange("p (d r) -> p d r", r=128),
            in_=ab["lnr"][:, :D].rearrange("p (d r) -> p d r", r=128)),
            waits=[n3, ("b_lnt", s.count("b_lnt")),
                   ("b_fc", self.fc_pred)],
            inc=("b_lnt", 16))
        self.fc_pred += 1
        self.pend_fin.append((k, tau, t, tr, ld))

    def emit_p2_finB(self):
        if not self.pend_fin:
            return
        (k, tau, t, tr, ld) = self.pend_fin.pop(0)
        s = self.sc
        ab = self.ab2
        D, QT = self.D, self.QT
        par = tau % 2
        lt = ab["lnrT"][par]

        def fc(e):
            last = None
            for oh in range(2):
                for d in range(self.DCH):
                    last = e.matmul(self.ps[6 + oh].ap()[:, :512],
                                    lt[:, d * 128: d * 128 + 128],
                                    ab["wfc"][:, d * D + oh * 512: d * D + oh * 512 + 512],
                                    start=(d == 0), stop=(d == self.DCH - 1))
            return last
        ft = self.t(fc, waits=[tr, ("b_wc", 32), ("b_x2w", s.count("b_x2w"))],
                    inc=("b_fc", 1))
        x2 = ab["x2st"][par]
        w1 = self.v(lambda e: e.scalar_tensor_tensor(
            out=x2[:, 0:512], in0=self.ps[6].ap()[:, :512], scalar=1.0,
            in1=ab["bfc"][:, 0:512], op0=ALU.mult, op1=ALU.add),
            waits=[ft, ld, ("b_x2s", max(0, 16 * (t - 1)))], inc=("qs_v", 1))
        w2 = self.v(lambda e: e.scalar_tensor_tensor(
            out=x2[:, 512:1024], in0=self.ps[7].ap()[:, :512], scalar=1.0,
            in1=ab["bfc"][:, 512:1024], op0=ALU.mult, op1=ALU.add),
            waits=[w1], inc=("qs_v", 1))
        w3 = self.v(lambda e: e.tensor_tensor(out=x2[:, :D], in0=x2[:, :D],
                                              in1=ab["x1r"][par][:, :D], op=ALU.add),
                    waits=[w2], inc=("b_x2w", 1))
        self.gdma(lambda e: e.dma_start(out=self.x2_hbm[t * 128:(t + 1) * 128, :],
                                        in_=x2[:, :D]),
                  waits=[w3, ("b_x2s", s.count("b_x2s"))], inc=("b_x2s", 16))
        self.v(lambda e: None, inc=("b_att_done", 1))

import ml_dtypes
from concourse.bass_utils import run_bass_kernel_spmd

_NT, _QT, _D, _I = 32, 8, 1024, 2816


def _build():
    nc = bass.Bass("TRN2", target_bir_lowering=False)
    mk = MK(nc, NT=_NT, QT=_QT, D=_D, I=_I)
    mk.emit_consts()
    mk.emit_ffn_phase(first=True)
    mk.emit_p2()
    mk.emit_ffn_phase(first=False)
    mk.sc.finalize_waits()
    with nc.Block() as blk:
        @blk.sync
        def _(e):
            mk.sc.emit_stream("sync", e)

        @blk.scalar
        def _(e):
            mk.sc.emit_stream("scalar", e)

        @blk.vector
        def _(e):
            mk.sc.emit_stream("vector", e)

        @blk.tensor
        def _(e):
            mk.sc.emit_stream("tensor", e)

        @blk.gpsimd
        def _(e):
            mk.sc.emit_stream("gpsimd", e)
    return nc


def kernel(L0_attn_x, L0_attn_Wq, L0_attn_Wk, L0_attn_Wv, L0_attn_Wfc,
           L0_attn_bfc, L0_attn_gamma, L0_attn_beta,
           L0_ffn_w_gate, L0_ffn_w_up, L0_ffn_w_down):
    nc = _build()
    x = np.ascontiguousarray(L0_attn_x)
    tri = np.tril(np.ones((128, 128), np.float32)).astype(ml_dtypes.bfloat16)
    base = {
        "Wq": np.ascontiguousarray(L0_attn_Wq),
        "Wk": np.ascontiguousarray(L0_attn_Wk),
        "Wv": np.ascontiguousarray(L0_attn_Wv),
        "Wfc": np.ascontiguousarray(L0_attn_Wfc),
        "bfc": np.ascontiguousarray(L0_attn_bfc).reshape(1, _D),
        "gamma": np.ascontiguousarray(L0_attn_gamma).reshape(1, _D),
        "beta": np.ascontiguousarray(L0_attn_beta).reshape(1, _D),
        "w_gate": np.ascontiguousarray(L0_ffn_w_gate),
        "w_up": np.ascontiguousarray(L0_ffn_w_up),
        "w_down": np.ascontiguousarray(L0_ffn_w_down),
        "tri": tri,
    }
    in_maps = [dict(base, x=np.ascontiguousarray(x[b])) for b in range(8)]
    res = run_bass_kernel_spmd(nc, in_maps, core_ids=list(range(8)))
    out = np.stack([res.results[b]["y"] for b in range(8)], axis=0)
    return out.astype(np.float32)


# version 25
# speedup vs baseline: 1.0950x; 1.0547x over previous
import os
import numpy as np
import concourse.bass as bass
import concourse.mybir as mybir

SIM_DRAIN = False
SCL_DRAIN = False

F32 = mybir.dt.float32
BF16 = mybir.dt.bfloat16
AX = mybir.AxisListType
ALU = mybir.AluOpType
ACT = mybir.ActivationFunctionType


class Sched:
    ENGINES = ("sync", "scalar", "vector", "tensor", "gpsimd")

    def __init__(self, nc):
        self.nc = nc
        self.streams = {e: [] for e in self.ENGINES}
        self.sem_handles = {}
        self.sem_counts = {}
        self._sem_ctx = []

    def sem(self, name):
        if name not in self.sem_handles:
            cm = self.nc.semaphore(name)
            h = cm.__enter__()
            self._sem_ctx.append(cm)
            self.sem_handles[name] = h
            self.sem_counts[name] = 0
        return self.sem_handles[name]

    def op(self, eng, emit, waits=(), inc=None, tag=""):
        n_after = None
        if inc is not None:
            name, amt = inc
            self.sem(name)
            self.sem_counts[name] += amt
            n_after = self.sem_counts[name]
        for (s, v) in waits:
            self.sem(s)
        self.streams[eng].append((emit, tuple(waits), inc, tag))
        return (inc[0], n_after) if inc is not None else None

    def count(self, name):
        self.sem(name)
        return self.sem_counts[name]

    def emit_stream(self, eng_name, e):
        for (emit, waits, inc, tag) in self.streams[eng_name]:
            for (s, v) in waits:
                if v > 0:
                    e.wait_ge(self.sem_handles[s], v)
            ins = emit(e)
            if inc is not None:
                name, amt = inc
                if ins is None:
                    e.sem_inc(self.sem_handles[name], amt)
                else:
                    ins.then_inc(self.sem_handles[name], amt)

    def n_ops(self):
        return {k: len(v) for k, v in self.streams.items()}

    def finalize_waits(self, eng="gpsimd"):
        names = [n for n, c in self.sem_counts.items() if c > 0]
        def w(e):
            for n in names:
                e.wait_ge(self.sem_handles[n], self.sem_counts[n])
            return None
        self.streams[eng].append((w, (), None, "final_waits"))


class Arena:
    def __init__(self, nc, name, n_elems, dtype):
        self.t = nc.alloc_sbuf_tensor(name, [128, n_elems], dtype)
        self.n = n_elems
        self.off = 0
        self.marks = []
        self.peak = 0

    def alloc(self, n, align=32):
        o = (self.off + align - 1) // align * align
        assert o + n <= self.n, f"arena {self.t.name} overflow: {o}+{n} > {self.n}"
        self.off = o + n
        self.peak = max(self.peak, self.off)
        return self.t.ap()[:, o:o + n]

    def mark(self):
        self.marks.append(self.off)

    def release(self):
        self.off = self.marks.pop()


class MK:
    def __init__(self, nc, NT=32, QT=8, D=1024, I=2816, expose=()):
        self.nc = nc
        self.NT = NT
        self.QT = QT
        self.NQ = NT // QT
        self.D = D
        self.DCH = D // 128
        self.I = I
        self.ICH = I // 128
        self.IH = I // 2
        self.sc = Sched(nc)
        self.expose = set(expose)

        def dram(name, shape, dt, kind="Internal"):
            if kind == "Internal" and name in self.expose:
                kind = "ExternalOutput"
            return nc.dram_tensor(name, shape, dt, kind=kind)
        self.x_in = dram("x", [NT * 128, D], F32, kind="ExternalInput").ap()
        self.wq_in = dram("Wq", [D, D], F32, kind="ExternalInput").ap()
        self.wk_in = dram("Wk", [D, D], F32, kind="ExternalInput").ap()
        self.wv_in = dram("Wv", [D, D], F32, kind="ExternalInput").ap()
        self.wfc_in = dram("Wfc", [D, D], F32, kind="ExternalInput").ap()
        self.bfc_in = dram("bfc", [1, D], F32, kind="ExternalInput").ap()
        self.gam_in = dram("gamma", [1, D], F32, kind="ExternalInput").ap()
        self.bet_in = dram("beta", [1, D], F32, kind="ExternalInput").ap()
        self.wg_in = dram("w_gate", [I, D], F32, kind="ExternalInput").ap()
        self.wu_in = dram("w_up", [I, D], F32, kind="ExternalInput").ap()
        self.wd_in = dram("w_down", [D, I], F32, kind="ExternalInput").ap()
        self.tri_in = dram("tri", [128, 128], BF16, kind="ExternalInput").ap()
        self.y_out = dram("y", [NT * 128, D], F32, kind="ExternalOutput").ap()
        self.x1_hbm = dram("x1s", [NT * 128, D], BF16).ap()
        self.x2_hbm = dram("x2s", [NT * 128, D], BF16).ap()
        self.kt_hbm = dram("kts", [self.NQ, 128, self.DCH * QT * 128], BF16).ap()
        self.v_hbm = dram("vs", [NT, 128, D], BF16).ap()
        self.wg_hbm = dram("wgs", [128, self.DCH * I], BF16).ap()
        self.wu_hbm = dram("wus", [128, self.DCH * I], BF16).ap()
        self.wd_hbm = dram("wds", [128, self.ICH * D], BF16).ap()
        self.wa_hbm = dram("was", [128, 4 * self.DCH * D], BF16).ap()

        self.AB = Arena(nc, "AB", 86784, BF16)
        self.AF = Arena(nc, "AF", 9504, F32)
        self.c_eps = nc.alloc_sbuf_tensor("c_eps", [128, 1], F32).ap()
        self.c_b20 = nc.alloc_sbuf_tensor("c_b20", [128, 1], F32).ap()
        self.c_s32 = nc.alloc_sbuf_tensor("c_s32", [128, 1], F32).ap()
        self.c_one = nc.alloc_sbuf_tensor("c_one", [128, 1], F32).ap()
        self.c_mh = nc.alloc_sbuf_tensor("c_mh", [128, 1], F32).ap()
        self.tri = nc.alloc_sbuf_tensor("tri_sb", [128, 128], BF16).ap()
        self.ps = [nc.alloc_psum_tensor(f"ps{i}", [128, 512], F32) for i in range(8)]

    def v(self, emit, waits=(), inc=None, tag=""):
        return self.sc.op("vector", emit, waits, inc, tag)

    def scl(self, emit, waits=(), inc=None, tag=""):
        return self.sc.op("scalar", emit, waits, inc, tag)

    def gdma(self, emit, waits=(), inc=None, tag=""):
        return self.sc.op("gpsimd", emit, waits, inc, tag)

    def sy(self, emit, waits=(), inc=None, tag=""):
        return self.sc.op("sync", emit, waits, inc, tag)

    def t(self, emit, waits=(), inc=None, tag=""):
        return self.sc.op("tensor", emit, waits, inc, tag)

    def emit_quant(self, qin32, N, waits_src, out_q, rel_key=None, cmp_bf16=False):
        s = self.sc
        q = self.qb
        NB = N // 16
        NH = N // 2
        NBH = NB // 2
        pre = "qs"
        w_src = list(waits_src)
        H = ((0, NH, 0, NBH), (NH, N, NBH, NB))
        sp = self.qpar
        self.qpar ^= 1
        sgbuf = q["sg"][sp]
        sgg = self.sg_guard[sp] or []

        gtok = []
        for (lo, hi, _, _) in H:
            gtok.append(self.scl(lambda e, lo=lo, hi=hi: e.activation(
                out=sgbuf[:, lo:hi], in_=qin32[:, lo:hi], func=ACT.Sign),
                waits=w_src + sgg, inc=(pre + "_as", 1)))

        def VH(emitfn, dep=(), extra=()):
            toks = []
            for i, (lo, hi, blo, bhi) in enumerate(H):
                w = list(extra)
                if dep:
                    w.append(dep[i])
                toks.append(self.v(lambda e, lo=lo, hi=hi, blo=blo, bhi=bhi:
                                   emitfn(e, lo, hi, blo, bhi),
                                   waits=w, inc=(pre + "_v", 1)))
            return toks

        ab_ = []
        for i, (lo, hi, _, _) in enumerate(H):
            ab_.append(self.scl(lambda e, lo=lo, hi=hi: e.activation(
                out=qin32[:, lo:hi], in_=qin32[:, lo:hi], func=ACT.Abs),
                waits=[gtok[i]], inc=(pre + "_as", 1)))
        bm = VH(lambda e, lo, hi, blo, bhi: e.tensor_reduce(
            out=q["bm"][:, blo:bhi].rearrange("p (b i) -> p b i", i=1),
            in_=qin32[:, lo:hi].rearrange("p (b i) -> p b i", i=16),
            axis=AX.X, op=ALU.max), dep=ab_)
        s32 = VH(lambda e, lo, hi, blo, bhi: e.tensor_scalar(
            out=q["s32"][:, blo:bhi], in0=q["bm"][:, blo:bhi],
            scalar1=1.0 / 6.0, scalar2=1e-12, op0=ALU.mult, op1=ALU.max), dep=bm)
        r32 = VH(lambda e, lo, hi, blo, bhi: e.reciprocal(
            out=q["r32"][:, blo:bhi], in_=q["s32"][:, blo:bhi]), dep=s32)
        adst = q["axn"] if cmp_bf16 else qin32
        t1 = VH(lambda e, lo, hi, blo, bhi: e.tensor_tensor(
            out=adst[:, lo:hi].rearrange("p (b i) -> p b i", i=16),
            in0=qin32[:, lo:hi].rearrange("p (b i) -> p b i", i=16),
            in1=q["r32"][:, blo:bhi].rearrange("p (b i) -> p b i", i=1)
                .broadcast_to([128, NBH, 16]),
            op=ALU.mult), dep=r32)
        csrc = adst
        t2 = VH(lambda e, lo, hi, blo, bhi: e.tensor_single_scalar(
            out=q["t2"][:, lo:hi], in_=csrc[:, lo:hi], scalar=0.25, op=ALU.is_gt),
            dep=t1)
        cbp = t1
        for th in (0.75, 1.25, 1.75):
            cb = VH(lambda e, lo, hi, blo, bhi, th=th: e.tensor_single_scalar(
                out=q["cb"][:, lo:hi], in_=csrc[:, lo:hi], scalar=th, op=ALU.is_gt),
                dep=t1 if th == 0.75 else cbp)
            t2 = VH(lambda e, lo, hi, blo, bhi: e.tensor_add(
                out=q["t2"][:, lo:hi], in0=q["t2"][:, lo:hi], in1=q["cb"][:, lo:hi]),
                dep=[cb[0], cb[1]], extra=[t2[0], t2[1]])
            cbp = t2
        b1 = VH(lambda e, lo, hi, blo, bhi: e.tensor_single_scalar(
            out=q["cb"][:, lo:hi], in_=csrc[:, lo:hi], scalar=2.5, op=ALU.is_gt),
            dep=t2)
        b3 = VH(lambda e, lo, hi, blo, bhi: e.scalar_tensor_tensor(
            out=q["cb"][:, lo:hi], in0=csrc[:, lo:hi], scalar=3.5, in1=q["cb"][:, lo:hi],
            op0=ALU.is_gt, op1=ALU.add), dep=b1)
        eh = VH(lambda e, lo, hi, blo, bhi: e.tensor_single_scalar(
            out=q["t2"][:, lo:hi], in_=q["t2"][:, lo:hi], scalar=0.5, op=ALU.mult),
            dep=t2)
        e1 = VH(lambda e, lo, hi, blo, bhi: e.tensor_add(
            out=q["t2"][:, lo:hi], in0=q["t2"][:, lo:hi], in1=q["cb"][:, lo:hi]),
            dep=[b3[0], b3[1]], extra=[eh[0], eh[1]])
        c7 = VH(lambda e, lo, hi, blo, bhi: e.tensor_scalar(
            out=q["cb"][:, lo:hi], in0=csrc[:, lo:hi], scalar1=5.0, scalar2=2.0,
            op0=ALU.is_gt, op1=ALU.mult), dep=e1)
        if rel_key is not None:
            self.qrel[rel_key] = [c7[0], c7[1]]
        e2 = VH(lambda e, lo, hi, blo, bhi: e.tensor_add(
            out=q["t2"][:, lo:hi], in0=q["t2"][:, lo:hi], in1=q["cb"][:, lo:hi]),
            dep=[c7[0], c7[1]], extra=[e1[0], e1[1]])
        ss = VH(lambda e, lo, hi, blo, bhi: e.tensor_tensor(
            out=q["axn"][:, lo:hi].rearrange("p (b i) -> p b i", i=16),
            in0=sgbuf[:, lo:hi].rearrange("p (b i) -> p b i", i=16),
            in1=q["s32"][:, blo:bhi].rearrange("p (b i) -> p b i", i=1)
                .broadcast_to([128, NBH, 16]),
            op=ALU.mult), dep=e2, extra=[s32[0], s32[1]])
        self.sg_guard[sp] = [ss[0], ss[1]]
        (lo0, hi0, _, _), (lo1, hi1, _, _) = H
        tok0 = self.v(lambda e: e.tensor_tensor(
            out=out_q[:, lo0:hi0], in0=q["t2"][:, lo0:hi0], in1=q["axn"][:, lo0:hi0],
            op=ALU.mult), waits=[ss[0]] + list(self.qout_free), inc=(pre + "_v", 1))
        return self.v(lambda e: e.tensor_tensor(
            out=out_q[:, lo1:hi1], in0=q["t2"][:, lo1:hi1], in1=q["axn"][:, lo1:hi1],
            op=ALU.mult), waits=[ss[1], tok0], inc=(pre + "_q", 1))

    def emit_ln_stats(self, src, N, pre, use_expln, rcol=0):
        st = self.lnb["stats"][rcol % 4]
        rs = self.lnb["rs"][rcol % 4]
        s = self.sc
        nch = (N + 511) // 512
        prev_rs = max(0, s.count(pre + "_rs") - 1)

        w0 = list(self.lnb["waits"]) + ([(pre + "_rs", prev_rs)] if prev_rs else [])
        btok = []
        for c in range(nch):
            lo = c * 512
            hi = min(N, lo + 512)
            btok.append(self.v(lambda e, c=c, lo=lo, hi=hi: e.bn_stats(
                out=st[:, 6 * c:6 * c + 6], in_=src[:, lo:hi]), waits=w0,
                inc=("qs_v", 1)))
        ag = self.v(lambda e: e.bn_aggr(out=st[:, 12:14], in_=st[:, 0:6 * nch]),
                    waits=btok, inc=("qs_v", 1))
        ve = self.v(lambda e: e.tensor_scalar(out=st[:, 14:15], in0=st[:, 13:14],
                                              scalar1=1e-5, scalar2=None, op0=ALU.add),
                    waits=[ag], inc=("qs_v", 1))
        v1 = self.v(lambda e: e.reciprocal(out=st[:, 15:16], in_=st[:, 14:15]),
                    waits=[ve], inc=(pre + "_st", 1))
        rc = rs[:, 0:1]
        if use_expln:
            lt = self.scl(lambda e: e.activation(out=rc, in_=st[:, 13:14],
                                                 func=ACT.Ln, bias=self.c_eps,
                                                 scale=self.c_one),
                          waits=[v1, ("c_ini", 1)], inc=("qs_s", 1))
            r1 = self.scl(lambda e: e.activation(out=rc, in_=rc,
                                                 func=ACT.Exp, scale=self.c_mh),
                          waits=[lt], inc=(pre + "_rs", 1))
        else:
            r1 = self.scl(lambda e: e.activation(out=rc, in_=st[:, 15:16],
                                                 func=ACT.Sqrt),
                          waits=[v1, ("c_ini", 1)], inc=(pre + "_rs", 1))
        return r1

    def alloc_ffn_bufs(self, with_prep):
        AB, AF = self.AB, self.AF
        I, D, IH = self.I, self.D, self.IH
        b = {}
        b["wgqT"] = AB.alloc(self.DCH * I)
        b["wuqT"] = AB.alloc(self.DCH * I)
        b["wdqT"] = AB.alloc(self.ICH * D)
        b["txq"] = [AB.alloc(D) for _ in range(2)]
        b["hsg"] = AB.alloc(IH)
        b["qout"] = AB.alloc(IH)
        b["thq"] = [AB.alloc(IH), AB.alloc(IH)]
        b["xst"] = [AB.alloc(D) for _ in range(2)] if with_prep else None
        if with_prep:
            b["wst"] = [AF.alloc(IH) for _ in range(2)]
            b["wab"] = [AB.alloc(D), AB.alloc(D)]
        q = {}
        q["axn"] = AB.alloc(IH)
        q["sg"] = [AB.alloc(IH), AB.alloc(IH)]
        q["t2"] = AB.alloc(IH)
        q["cb"] = AB.alloc(IH)
        _qs = AF.alloc(3 * (IH // 16))
        q["bm"] = _qs[:, 0:IH // 16]
        q["s32"] = _qs[:, IH // 16:2 * (IH // 16)]
        q["r32"] = _qs[:, 2 * (IH // 16):3 * (IH // 16)]
        q["in32"] = [AF.alloc(IH), AF.alloc(IH)]
        b["q"] = q
        b["xin"] = [AF.alloc(D) for _ in range(3)] if with_prep else None
        b["x2in"] = None if with_prep else [AB.alloc(D) for _ in range(3)]
        b["yst"] = None if with_prep else [AF.alloc(D) for _ in range(2)]
        stt_ = AF.alloc(128)
        b["stats"] = [stt_[:, 32 * i:32 * i + 16] for i in range(4)]
        b["rs"] = [stt_[:, 32 * i + 16:32 * i + 20] for i in range(4)]
        return b

    def emit_consts(self):
        self.v(lambda e: e.memset(self.c_eps, 1e-5))
        self.v(lambda e: e.memset(self.c_b20, -20.0))
        self.v(lambda e: e.memset(self.c_s32, 1.0 / 32.0))
        self.v(lambda e: e.memset(self.c_one, 1.0))
        self.v(lambda e: e.memset(self.c_mh, -0.5), inc=("c_ini", 1))
        self.sy(lambda e: e.dma_start(out=self.tri, in_=self.tri_in), inc=("tri_ld", 16))

    def emit_wchunk(self, kind, c, sub):
        s = self.sc
        fb = self.fb
        D, I, IH = self.D, self.I, self.IH
        j = self.wj % 2
        self.wj += 1
        wst = fb["wst"][j]
        wprev = self.qrel.get(("wst", j), [])
        if kind in ("g", "u"):
            N = D
            src_h = self.wg_in if kind == "g" else self.wu_in
            ld = self.gdma(lambda e: e.dma_start(out=wst[:, :N],
                                                 in_=src_h[c * 128:(c + 1) * 128, :]),
                           waits=wprev + [("w_ld", s.count("w_ld"))],
                           inc=("w_ld", 16))
        else:
            N = IH
            ld = self.gdma(lambda e: e.dma_start(
                out=wst[:, :N], in_=self.wd_in[c * 128:(c + 1) * 128, sub * IH:(sub + 1) * IH]),
                waits=wprev + [("w_ld", s.count("w_ld"))],
                inc=("w_ld", 16))
        wsb = fb["qout"]
        qt = self.emit_quant(wst[:, :N], N, [ld], wsb[:, :N],
                             rel_key=("wst", j))
        if kind in ("g", "u"):
            dst = fb["wgqT"] if kind == "g" else fb["wuqT"]
            out_ap = dst.rearrange("p (d i) -> p d i", i=I)[:, :, c * 128:(c + 1) * 128]
            in_ap = wsb[:, :N].rearrange("p (d r) -> p d r", r=128)
        else:
            dst = fb["wdqT"]
            out_ap = dst.rearrange("p (i d) -> p i d", d=D)[:, sub * 11:(sub + 1) * 11,
                                                           c * 128:(c + 1) * 128]
            in_ap = wsb[:, :N].rearrange("p (i r) -> p i r", r=128)
        tt = self.sy(lambda e: e.dma_start_transpose(out=out_ap, in_=in_ap),
                     waits=[qt, ("w_t", s.count("w_t"))], inc=("w_t", 16))
        self.qout_free = [tt]

    def emit_ffn_phase(self, first):
        s = self.sc
        NT = self.NT
        pre = "a" if first else "c"
        self.AB.mark()
        self.AF.mark()
        fb = self.alloc_ffn_bufs(with_prep=first)
        self.fb = fb
        self.qb = fb["q"]
        self.qpar = 0
        self.qinpar = 0
        self.sg_guard = [None, None]
        self.qrel = {}
        self.qout_free = []
        self.lnb = {"stats": fb["stats"], "rs": fb["rs"], "waits": []}
        if first:
            self.wj = 0
            order = ([("g", c, 0) for c in range(11)] + [("u", c, 0) for c in range(11)]
                     + [("g", c, 0) for c in range(11, 22)]
                     + [("u", c, 0) for c in range(11, 22)]
                     + [("d", c, sub) for c in range(8) for sub in range(2)])
            for (kind, c, sub) in order:
                self.emit_wchunk(kind, c, sub)
            self.w_thr = {"gu0": 16 * 22, "gu1": 16 * 44, "d": 16 * 60}
            wt_all = 16 * 60
            for (dst_h, src_a) in ((self.wg_hbm, fb["wgqT"]), (self.wu_hbm, fb["wuqT"]),
                                   (self.wd_hbm, fb["wdqT"])):
                self.gdma(lambda e, d=dst_h, a=src_a: e.dma_start(out=d, in_=a),
                          waits=[("w_t", wt_all)], inc=("w_sp", 16))
            self.emit_ffn_xpart(0, pre, first)
            self.emit_ffn_xrest(pre, first)
            self.emit_ffn_xpart(1, pre, first)
            xpre = 2
            self.wa_state = {"z_ld": 0, "z_cs": 0,
                             "srcs": (self.wq_in, self.wk_in, self.wv_in, self.wfc_in)}
            self.emit_wa_ld()
            self.emit_wa_ld()
        else:
            xpre = 0
            nld = 0
            early = [("b_qkmm", self.sc.count("b_qkmm")), ("b_vmm", self.sc.count("b_vmm")),
                     ("b_smm", self.sc.count("b_smm")), ("b_exp", self.sc.count("b_exp")),
                     ("b_att_done", self.sc.count("b_att_done")),
                     ("b_x2s", self.sc.count("b_x2s"))]
            for (src_h, dst_a) in ((self.wg_hbm, fb["wgqT"]), (self.wu_hbm, fb["wuqT"]),
                                   (self.wd_hbm, fb["wdqT"])):
                for piece in range(4):
                    n = src_h.shape[1] // 4
                    gate = early if nld < 3 else list(self.barrier_waits("gpsimd"))
                    self.gdma(lambda e, sh=src_h, da=dst_a, p=piece, n=n:
                              e.dma_start(out=da[:, p * n:(p + 1) * n],
                                          in_=sh[:, p * n:(p + 1) * n]),
                              waits=gate + [("w3_ld", self.sc.count("w3_ld"))],
                              inc=("w3_ld", 16))
                    nld += 1
                    if nld == 3:
                        self.emit_ffn_xpart(0, pre, first)
                        self.emit_ffn_xrest(pre, first)
                        self.emit_ffn_xpart(1, pre, first)
                        xpre = 2
            self.w_thr = {"gu0": 16 * nld, "gu1": 16 * nld, "d": 16 * nld}
            self.w_thr["sem"] = "w3_ld"

        if xpre == 0:
            self.emit_ffn_xpart(0, pre, first)
            self.emit_ffn_xrest(pre, first)
        for t in range(NT):
            self.emit_ffn_gu(t, 0, pre, "w_t" if first else "w3_ld")
            if t + 1 < NT and t + 1 >= xpre:
                self.emit_ffn_xpart(t + 1, pre, first)
            self.emit_ffn_evhm(t, 0, pre)
            if t >= 1:
                self.emit_ffn_down(t - 1, pre, first, 0)
            if t + 1 < NT:
                self.emit_ffn_xrest(pre, first)
            self.emit_ffn_hqchain(pre)
            self.emit_ffn_gu(t, 1, pre, "w_t" if first else "w3_ld")
            self.emit_ffn_evhm(t, 1, pre)
            if t >= 1:
                self.emit_ffn_down(t - 1, pre, first, 1)
                self.emit_ffn_resid(t - 1, pre, first)
            self.emit_ffn_hqchain(pre)
            if first:
                self.emit_wa_pair()
        self.emit_ffn_down(NT - 1, pre, first, 0)
        self.emit_ffn_down(NT - 1, pre, first, 1)
        self.emit_ffn_resid(NT - 1, pre, first)
        if first:
            while self.wa_state["z_cs"] < 4 * self.DCH:
                self.emit_wa_pair()
        for eng in Sched.ENGINES:
            bw = []
            if eng == "gpsimd":
                bw = [(pre + "_x1s", s.count(pre + "_x1s"))]
                if first:
                    bw.append(("w_sp", s.count("w_sp")))
            s.op(eng, lambda e: None, waits=bw,
                 inc=(("p1b_" if first else "p3b_") + eng, 1))
        self.AB.release()
        self.AF.release()

    def barrier_waits(self, eng, which="p2b_"):
        return [(which + e2, self.sc.count(which + e2)) for e2 in Sched.ENGINES]

    def emit_wa_ld(self):
        st_ = self.wa_state
        z = st_["z_ld"]
        if z >= 4 * self.DCH:
            return
        st_["z_ld"] += 1
        s = self.sc
        D = self.D
        fb = self.fb
        src_h = st_["srcs"][z // self.DCH]
        c = z % self.DCH
        j = self.wj % 2
        self.wj += 1
        wst = fb["wst"][j]
        wprev = self.qrel.get(("wst", j), [])
        ld = self.gdma(lambda e, sh=src_h, c=c, wst=wst: e.dma_start(
            out=wst[:, :D], in_=sh[c * 128:(c + 1) * 128, :]),
            waits=wprev + [("w_ld", s.count("w_ld"))], inc=("w_ld", 16))
        st_[("ld", z)] = (ld, wst)

    def emit_wa_pair(self):
        st_ = self.wa_state
        s = self.sc
        D = self.D
        fb = self.fb
        for _ in range(2):
            z = st_["z_cs"]
            if z >= 4 * self.DCH:
                return
            st_["z_cs"] += 1
            (ld, wst) = st_.pop(("ld", z))
            wb = fb["wab"][z % 2]
            cz = self.scl(lambda e, wst=wst, wb=wb: e.activation(
                out=wb[:, :D], in_=wst[:, :D], func=ACT.Copy),
                waits=[ld, ("wa_sp", max(0, 16 * (z - 1)))], inc=("w_ac", 1))
            self.qrel[("wst", z % 2)] = [cz]
            self.gdma(lambda e, z=z, wb=wb: e.dma_start(
                out=self.wa_hbm[:, z * D:(z + 1) * D], in_=wb[:, :D]),
                waits=[cz, ("wa_sp", s.count("wa_sp"))], inc=("wa_sp", 16))
        self.emit_wa_ld()
        self.emit_wa_ld()

    def emit_ffn_xpart(self, t, pre, first):
        s = self.sc
        fb = self.fb
        D = self.D
        if first:
            xsrc = fb["xin"][t % 3]
            ld = self.gdma(lambda e: e.dma_start(out=xsrc[:, :D],
                                                 in_=self.x_in[t * 128:(t + 1) * 128, :]),
                           waits=[(pre + "_x1", max(0, t - 2)),
                                  (pre + "_x0", s.count(pre + "_x0"))],
                           inc=(pre + "_x0", 16))
        else:
            xsrc = fb["x2in"][t % 3]
            ld = self.gdma(lambda e: e.dma_start(out=xsrc[:, :D],
                                                 in_=self.x2_hbm[t * 128:(t + 1) * 128, :]),
                           waits=[(pre + "_x1", max(0, t - 2)), ("b_x2s", 16 * (t + 1)),
                                  (pre + "_x0", s.count(pre + "_x0"))]
                           + (self.barrier_waits("gpsimd") if t < 3 else []),
                           inc=(pre + "_x0", 16))
        self.lnb["waits"] = [ld]
        r1 = self.emit_ln_stats(xsrc[:, :D], D, pre + "_ln", use_expln=False, rcol=t % 2)
        self.xp_state = (t, xsrc, r1)

    def emit_ffn_xrest(self, pre, first):
        s = self.sc
        fb = self.fb
        D = self.D
        (t, xsrc, r1) = self.xp_state
        st = fb["stats"][t % 2]
        qp = self.qinpar
        self.qinpar ^= 1
        qin = fb["q"]["in32"][qp]
        rcap = fb["rs"][t % 2][:, 0:1]
        ax = self.v(lambda e: e.tensor_scalar(out=qin[:, :D], in0=xsrc[:, :D],
                                              scalar1=st[:, 12:13], scalar2=rcap,
                                              op0=ALU.subtract, op1=ALU.mult),
                    waits=[r1] + self.qrel.get(("qin", qp), []),
                    inc=(pre + "_xn", 1))
        qx = self.emit_quant(qin[:, :D], D, [ax], fb["qout"][:, :D],
                             rel_key=("qin", qp))
        par = t % 2
        tt = self.sy(lambda e: e.dma_start_transpose(
            out=fb["txq"][par].rearrange("p (d r) -> p d r", r=128),
            in_=fb["qout"][:, :D].rearrange("p (d r) -> p d r", r=128)),
            waits=[qx, (pre + "_gu", max(0, 2 * (t - 1))),
                   (pre + "_txq", s.count(pre + "_txq"))], inc=(pre + "_txq", 16))
        self.qout_free = [tt]

    def emit_ffn_gu(self, t, h, pre, wsem):
        fb = self.fb
        IH = self.IH
        par = t % 2
        ncol = [512, 512, IH - 1024]
        wthr = self.w_thr["gu0"] if h == 0 else self.w_thr["gu1"]

        def gu(e):
            last = None
            for d in range(self.DCH):
                lhsT = fb["txq"][par][:, d * 128:(d + 1) * 128]
                for c in range(3):
                    w = ncol[c]
                    off = d * self.I + h * IH + c * 512
                    e.matmul(self.ps[c].ap()[:, :w], lhsT,
                             fb["wgqT"][:, off:off + w],
                             start=(d == 0), stop=(d == self.DCH - 1))
                    last = e.matmul(self.ps[3 + c].ap()[:, :w], lhsT,
                                    fb["wuqT"][:, off:off + w],
                                    start=(d == 0), stop=(d == self.DCH - 1))
            return last
        self.t(gu, waits=[(pre + "_txq", 16 * (t + 1)), (wsem, wthr),
                          (pre + "_ev", max(0, 2 * t + h)),
                          (pre + "_hm", max(0, 2 * t + h))], inc=(pre + "_gu", 1))

    def emit_ffn_evhm(self, t, h, pre):
        s = self.sc
        fb = self.fb
        IH = self.IH
        ncol = [512, 512, IH - 1024]

        def ev(e):
            last = None
            for c in range(3):
                w = ncol[c]
                last = e.activation(out=fb["hsg"][:, c * 512:c * 512 + w],
                                    in_=self.ps[c].ap()[:, :w], func=ACT.Silu)
            return last
        evt = self.sc.op("scalar", ev,
                         waits=[(pre + "_gu", 2 * t + h + 1), (pre + "_hm", max(0, 2 * t + h))],
                         inc=(pre + "_ev", 1))
        qp = self.qinpar
        self.qinpar ^= 1
        qin = fb["q"]["in32"][qp]

        def hmul(e):
            last = None
            for c in range(3):
                w = ncol[c]
                last = e.tensor_tensor(out=qin[:, c * 512:c * 512 + w],
                                       in0=fb["hsg"][:, c * 512:c * 512 + w],
                                       in1=self.ps[3 + c].ap()[:, :w], op=ALU.mult)
            return last
        hm = self.sc.op("vector", hmul,
                        waits=[evt] + self.qrel.get(("qin", qp), []),
                        inc=(pre + "_hm", 1))
        self.hq_state = (t, h, qp, qin, hm)

    def emit_ffn_hqchain(self, pre):
        s = self.sc
        fb = self.fb
        IH = self.IH
        (t, h, qp, qin, hm) = self.hq_state
        qh = self.emit_quant(qin[:, :IH], IH, [hm], fb["qout"][:, :IH],
                             rel_key=("qin", qp))
        tt = self.sy(lambda e: e.dma_start_transpose(
            out=fb["thq"][h].rearrange("p (i r) -> p i r", r=128),
            in_=fb["qout"][:, :IH].rearrange("p (i r) -> p i r", r=128)),
            waits=[qh, (pre + "_dn", max(0, t)),
                   (pre + "_thq", self.sc.count(pre + "_thq"))], inc=(pre + "_thq", 16))
        self.qout_free = [tt]

    def emit_ffn_down(self, t, pre, first, colh):
        fb = self.fb
        D = self.D
        wsem = "w_t" if first else "w3_ld"

        def dn(e):
            last = None
            for i in range(self.ICH):
                hh = 0 if i < 11 else 1
                ioff = (i - 11 * hh) * 128
                last = e.matmul(self.ps[6 + colh].ap()[:, :512],
                                fb["thq"][hh][:, ioff:ioff + 128],
                                fb["wdqT"][:, i * D + colh * 512: i * D + colh * 512 + 512],
                                start=(i == 0), stop=(i == self.ICH - 1))
            return last
        if colh == 0:
            self.t(dn, waits=[(pre + "_thq", 16 * (2 * t + 2)),
                              (wsem, self.w_thr["d"]),
                              (pre + "_x1", max(0, t))])
        else:
            self.t(dn, inc=(pre + "_dn", 1))

    def emit_ffn_resid(self, t, pre, first):
        s = self.sc
        fb = self.fb
        D = self.D
        par = t % 2
        xsrc = (fb["xin"] if first else fb["x2in"])[t % 3]
        dst = (fb["xst"] if first else fb["yst"])[par]
        out_hbm = self.x1_hbm if first else self.y_out
        self.v(lambda e: e.scalar_tensor_tensor(
            out=dst[:, 0:512], in0=self.ps[6].ap()[:, :512],
            scalar=0.5, in1=xsrc[:, 0:512], op0=ALU.mult, op1=ALU.add),
            waits=[(pre + "_dn", s.count(pre + "_dn")),
                   (pre + "_x1s", max(0, 16 * (t - 1)))])
        r2 = self.v(lambda e: e.scalar_tensor_tensor(
            out=dst[:, 512:1024], in0=self.ps[7].ap()[:, :512],
            scalar=0.5, in1=xsrc[:, 512:1024], op0=ALU.mult, op1=ALU.add),
            inc=(pre + "_x1", 1))
        self.gdma(lambda e: e.dma_start(out=out_hbm[t * 128:(t + 1) * 128, :],
                                        in_=dst[:, :D]),
                  waits=[r2, (pre + "_x1s", s.count(pre + "_x1s"))],
                  inc=(pre + "_x1s", 16))

    def finalize(self):
        self.gdma(lambda e: None, waits=[("c_x1s", self.sc.count("c_x1s"))])

    def alloc_attn_bufs(self):
        AB, AF = self.AB, self.AF
        D, QT = self.D, self.QT
        b = {}
        b["wq"] = AB.alloc(self.DCH * D)
        b["wk"] = AB.alloc(self.DCH * D)
        b["wv"] = AB.alloc(self.DCH * D)
        b["wfc"] = AB.alloc(self.DCH * D)
        b["xnT"] = AB.alloc(self.DCH * QT * 128)
        b["qT"] = AB.alloc(self.DCH * QT * 128)
        b["ktc"] = AB.alloc(self.DCH * QT * 128)
        b["vc"] = AB.alloc(QT * D)
        gkn = self.DCH * min(4, QT) * 128
        if 2 * gkn <= self.DCH * QT * 128:
            b["ktg"] = [b["xnT"][:, 0:gkn], b["xnT"][:, gkn:2 * gkn]]
        else:
            b["ktg"] = [AB.alloc(gkn) for _ in range(2)]
        b["vg"] = [AB.alloc(4 * D) for _ in range(2)]
        b["x1q"] = [AB.alloc(D) for _ in range(2)]
        b["x1r"] = [AB.alloc(D) for _ in range(2)]
        b["x2st"] = [AB.alloc(D) for _ in range(2)]
        b["att"] = [AB.alloc(512) for _ in range(3)]
        b["attT"] = [AB.alloc(512) for _ in range(3)]
        b["lnr"] = AB.alloc(D)
        b["xnb"] = [b["lnr"]]
        b["lnrT"] = [AB.alloc(D), AB.alloc(D)]
        b["gam"] = AF.alloc(D)
        b["bet"] = AF.alloc(D)
        b["bfc"] = AF.alloc(D)
        _st2 = AF.alloc(128)
        b["stats"] = [_st2[:, 32 * i:32 * i + 16] for i in range(4)]
        b["rs"] = [_st2[:, 32 * i + 16:32 * i + 20] for i in range(4)]
        b["wstg"] = [AF.alloc(D) for _ in range(2)]
        b["den"] = AF.alloc(2 * QT)
        b["dpart"] = AF.alloc(4)
        b["ov"] = AF.alloc(D)
        return b

    def emit_p2(self):
        s = self.sc
        NT, QT, NQ, D = self.NT, self.QT, self.NQ, self.D
        self.AB.mark()
        self.AF.mark()
        ab = self.alloc_attn_bufs()
        self.ab2 = ab
        self.qb = None
        self.lnb = {"stats": ab["stats"], "rs": ab["rs"], "waits": []}
        p1w = self.barrier_waits("gpsimd", "p1b_")

        for (wi, dst) in enumerate((ab["wq"], ab["wk"], ab["wv"], ab["wfc"])):
            self.gdma(lambda e, wi=wi, dst=dst: e.dma_start(
                out=dst, in_=self.wa_hbm[:, wi * self.DCH * self.D:
                                         (wi + 1) * self.DCH * self.D]),
                waits=p1w + [("wa_sp", 16 * 32), ("b_wld", s.count("b_wld"))],
                inc=("b_wld", 16))
        self.sc.op("scalar", lambda e: None,
                   waits=[("b_wld", s.count("b_wld"))], inc=("b_wc", 32))
        for (src_h, dst) in ((self.gam_in, ab["gam"]), (self.bet_in, ab["bet"]),
                             (self.bfc_in, ab["bfc"])):
            self.sy(lambda e, sh=src_h, dst=dst: e.dma_start(
                out=dst, in_=sh.broadcast_to([128, D])),
                waits=self.barrier_waits("sync", "p1b_") + [("b_cld", s.count("b_cld"))],
                inc=("b_cld", 16))

        self.pend_fin = []
        self.fc_pred = 0
        self.gr_last_av = {0: 0, 1: 0}
        for k in range(NQ):
            self.den_first = {tau: True for tau in range(self.QT)}
            self.o_first = {tau: [True, True] for tau in range(self.QT)}
            self.gr_free = {0: [], 1: []}
            self.emit_p2_qkv(k)
            self.emit_p2_attn(k)
            while self.pend_fin:
                self.emit_p2_finB()
        for eng in Sched.ENGINES:
            bw = []
            if eng == "gpsimd":
                bw = [(n, s.count(n)) for n in
                      ("b_ktsp", "b_vsp", "b_x2s", "b_gl", "b_x1l", "b_x1rl", "b_wld")]
            s.op(eng, lambda e: None, waits=bw, inc=("p2b_" + eng, 1))
        self.AB.release()
        self.AF.release()

    def emit_p2_qkv(self, k):
        s = self.sc
        ab = self.ab2
        D, QT = self.D, self.QT
        xnt_base = s.count("b_xnt")
        for tau in range(QT):
            t = k * QT + tau
            xb = ab["x1q"][tau % 2]
            ld = self.gdma(lambda e, t=t, xb=xb: e.dma_start(
                out=xb[:, :D], in_=self.x1_hbm[t * 128:(t + 1) * 128, :]),
                waits=[("a_x1s", 16 * self.NT), ("b_x1f", max(0, s.count("b_x1f") - 1)),
                       ("b_x1l", s.count("b_x1l"))],
                inc=("b_x1l", 16))
            self.lnb["waits"] = [ld]
            r1 = self.emit_ln_stats(xb[:, :D], D, "b_ln", use_expln=True, rcol=tau % 2)
            xn = ab["xnb"][0]
            st = ab["stats"][tau % 2]
            rcap = ab["rs"][tau % 2][:, 0:1]
            ax = self.v(lambda e, xb=xb, xn=xn, st=st, rcap=rcap: e.tensor_scalar(
                out=xn[:, :D], in0=xb[:, :D], scalar1=st[:, 12:13], scalar2=rcap,
                op0=ALU.subtract, op1=ALU.mult),
                waits=[r1, ("b_xnt", s.count("b_xnt")),
                       ("b_lnt", s.count("b_lnt"))],
                inc=("b_x1f", 1))
            self.sy(lambda e, xn=xn, tau=tau: e.dma_start_transpose(
                out=ab["xnT"].rearrange("p (d r) -> p d r", r=QT * 128)[:, :, tau * 128:(tau + 1) * 128],
                in_=xn[:, :D].rearrange("p (d r) -> p d r", r=128)),
                waits=[ax, ("b_xnt", s.count("b_xnt")),
                       ("b_qkmm", s.count("b_qkmm")), ("b_vmm", s.count("b_vmm")),
                       ("b_exp", s.count("b_exp")), ("b_gl", s.count("b_gl"))],
                inc=("b_xnt", 16))
        rhw = min(512, QT * 128)
        nrh = (QT * 128) // rhw
        qkctr = 0
        for rh in range(nrh):
            xnt_rh = xnt_base + 16 * (rh + 1) * (rhw // 128)
            for (wname, dsts, spill) in (("wq", "qT", False), ("wk", "ktc", True)):
                w = ab[wname]
                dst = ab[dsts]
                for dk in range(self.DCH):
                    pspar = qkctr % 2
                    qkctr += 1
                    def mm(e, w=w, dk=dk, rh=rh, pspar=pspar):
                        last = None
                        for d in range(self.DCH):
                            last = e.matmul(self.ps[pspar].ap()[:, :rhw],
                                            w[:, d * D + dk * 128: d * D + dk * 128 + 128],
                                            ab["xnT"][:, d * QT * 128 + rh * rhw:
                                                      d * QT * 128 + rh * rhw + rhw],
                                            start=(d == 0), stop=(d == self.DCH - 1))
                        return last
                    mt = self.t(mm, waits=[("b_xnt", xnt_rh), ("b_wc", 32),
                                           ("b_exp", s.count("b_exp")),
                                           ("b_qkev", max(0, s.count("b_qkev") - 1))],
                                inc=("b_qkmm", 1))
                    self.scl(lambda e, dst=dst, dk=dk, rh=rh, pspar=pspar: e.activation(
                        out=dst[:, dk * QT * 128 + rh * rhw: dk * QT * 128 + rh * rhw + rhw],
                        in_=self.ps[pspar].ap()[:, :rhw], func=ACT.Copy),
                        waits=[mt, ("b_att_done", s.count("b_att_done")),
                               ("b_ktsp", s.count("b_ktsp"))],
                        inc=("b_qkev", 1))
        for tau in range(QT):
            for vh in range(2):
                pspar = (tau * 2 + vh) % 2
                def mmv(e, tau=tau, vh=vh, pspar=pspar):
                    last = None
                    for d in range(self.DCH):
                        last = e.matmul(self.ps[2 + pspar].ap()[:, :512],
                                        ab["xnT"][:, d * QT * 128 + tau * 128:
                                                  d * QT * 128 + tau * 128 + 128],
                                        ab["wv"][:, d * D + vh * 512: d * D + vh * 512 + 512],
                                        start=(d == 0), stop=(d == self.DCH - 1))
                    return last
                mt = self.t(mmv, waits=[("b_xnt", xnt_base + 16 * (tau + 1)), ("b_wc", 32),
                                        ("b_oev", s.count("b_oev")),
                                        ("b_vev", max(0, s.count("b_vev") - 1))],
                            inc=("b_vmm", 1))
                self.scl(lambda e, tau=tau, vh=vh, pspar=pspar: e.activation(
                    out=ab["vc"][:, tau * D + vh * 512: tau * D + vh * 512 + 512],
                    in_=self.ps[2 + pspar].ap()[:, :512], func=ACT.Copy),
                    waits=[mt, ("b_att_done", s.count("b_att_done")),
                           ("b_vsp", s.count("b_vsp"))],
                    inc=("b_vev", 1))
        kev = s.count("b_qkev")
        vev = s.count("b_vev")
        self.gdma(lambda e, k=k: e.dma_start(out=self.kt_hbm[k], in_=ab["ktc"]),
                  waits=[("b_qkev", kev), ("b_ktsp", s.count("b_ktsp"))],
                  inc=("b_ktsp", 16))
        self.gdma(lambda e, k=k: e.dma_start(
            out=self.v_hbm[k * QT:(k + 1) * QT].rearrange("t p d -> p t d"),
            in_=ab["vc"].rearrange("p (t d) -> p t d", d=D)),
            waits=[("b_vev", vev), ("b_vsp", s.count("b_vsp"))], inc=("b_vsp", 16))

    def emit_p2_attn(self, k):
        s = self.sc
        ab = self.ab2
        D, QT = self.D, self.QT
        gsz = min(4, self.QT)
        ngran = (self.QT // gsz) * k
        for g in range(QT // 2):
            taus = (2 * g, 2 * g + 1)
            gtoks = []
            base_av = s.count("b_av")
            for gr in range(ngran):
                q, gi = gr // 2, gr % 2
                par = gr % 2
                wk = [("b_ktsp", 16 * (q + 1)), ("b_vsp", 16 * (q + 1))]
                if gr >= 2:
                    wk.append(("b_av", base_av + 2 * (gr - 2) + 2))
                else:
                    if self.gr_last_av[par]:
                        wk.append(("b_av", self.gr_last_av[par]))
                    wk.append(("b_qkmm", s.count("b_qkmm")))
                    wk.append(("b_vmm", s.count("b_vmm")))
                gw = gsz * 128
                l1 = self.gdma(lambda e, q=q, gi=gi, par=par, gw=gw: e.dma_start(
                    out=ab["ktg"][par][:, :self.DCH * gw].rearrange("p (d r) -> p d r", r=gw),
                    in_=self.kt_hbm[q].rearrange("p (d r) -> p d r", r=QT * 128)
                    [:, :, gi * gw:(gi + 1) * gw]),
                    waits=wk + [("b_gl", s.count("b_gl"))], inc=("b_gl", 16))
                l2 = self.gdma(lambda e, q=q, gi=gi, par=par: e.dma_start(
                    out=ab["vg"][par].rearrange("p (t d) -> p t d", d=D)[:, :gsz],
                    in_=self.v_hbm[q * QT + gi * gsz: q * QT + gi * gsz + gsz]
                    .rearrange("t p d -> p t d")),
                    waits=[("b_gl", s.count("b_gl"))], inc=("b_gl", 16))
                gtoks.append((l1, l2, par))
            chunks = []
            for ci in range(ngran):
                for tau in taus:
                    chunks.append((tau, "g", ci, gsz, False))
            loc = {tau: [] for tau in taus}
            for tau in taus:
                lo = 0
                while lo < tau:
                    wdt = min(4, tau - lo)
                    loc[tau].append((tau, "l", lo, wdt, False))
                    lo += wdt
                loc[tau].append((tau, "l", tau, 1, True))
            mx = max(len(loc[t]) for t in taus)
            for i in range(mx):
                for tau in taus:
                    if i < len(loc[tau]):
                        chunks.append(loc[tau][i])
            for a_ in range(ngran):
                self.gr_last_av[a_ % 2] = base_av + 2 * a_ + 2
            self.oev_gate = s.count("b_oev")
            self.av_base = s.count("b_av")
            pend = []
            seen_gr = set()
            for ci, ch in enumerate(chunks):
                if ch[1] == "g" and ch[2] not in seen_gr:
                    seen_gr.add(ch[2])
                    if ch[2] >= 2:
                        while len(pend) > 2:
                            self.emit_p2_av(*pend.pop(0))
                info = self.emit_p2_scores(k, g, ci, ch, gtoks)
                pend.append(info)
                if ci in (1, 3):
                    self.emit_p2_finB()
                if len(pend) > 3:
                    self.emit_p2_av(*pend.pop(0))
            for p_ in pend:
                self.emit_p2_av(*p_)
            for tau in taus:
                self.emit_p2_fin(k, tau)
            if os.environ.get("MK_NODEFER") == "1":
                while self.pend_fin:
                    self.emit_p2_finB()

    def emit_p2_scores(self, k, g, ci, ch, gtoks):
        s = self.sc
        ab = self.ab2
        D, QT = self.D, self.QT
        (tau, kind, a, wt, diag) = ch
        w = wt * 128
        par = ci % 2
        apar = ci % 3
        gpar = gtoks[a][2] if kind == "g" else 0

        def mm(e):
            last = None
            for dk in range(self.DCH):
                lhsT = ab["qT"][:, dk * QT * 128 + tau * 128: dk * QT * 128 + tau * 128 + 128]
                if kind == "g":
                    gw = min(4, self.QT) * 128
                    rhs = ab["ktg"][gpar][:, dk * gw: dk * gw + w]
                else:
                    rhs = ab["ktc"][:, dk * QT * 128 + a * 128: dk * QT * 128 + a * 128 + w]
                last = e.matmul(self.ps[par].ap()[:, :w], lhsT, rhs,
                                start=(dk == 0), stop=(dk == self.DCH - 1))
            return last
        waits = [("b_exp", max(0, s.count("b_exp") - 1)),
                 ("b_qkev", s.count("b_qkev"))]
        if kind == "g":
            waits.append(gtoks[a][0])
        mt = self.t(mm, waits=waits, inc=("b_smm", 1))
        att = ab["att"][apar]
        dguard = ("b_den", max(0, s.count("b_den") - 1))
        if not diag:
            ex = self.scl(lambda e: e.activation(
                out=att[:, :w], in_=self.ps[par].ap()[:, :w], func=ACT.Exp,
                bias=self.c_b20, scale=self.c_s32, accum_out=ab["dpart"][:, par:par + 1]),
                waits=[mt, ("c_ini", 1), dguard,
                       ("b_att_t", max(0, s.count("b_att_t") - 32))],
                inc=("b_exp", 1))
            self.v(self.den_update_op(tau, par),
                   waits=[ex, ("b_oev", self.oev_gate)], inc=("b_den", 1))
            mk_tok = ex
        else:
            ex = self.scl(lambda e: e.activation(
                out=att[:, :w], in_=self.ps[par].ap()[:, :w], func=ACT.Exp,
                bias=self.c_b20, scale=self.c_s32),
                waits=[mt, ("c_ini", 1), dguard,
                       ("b_att_t", max(0, s.count("b_att_t") - 32))],
                inc=("b_exp", 1))
            mk_tok = self.v(lambda e: e.tensor_tensor(out=att[:, :w], in0=att[:, :w],
                                                      in1=self.tri, op=ALU.mult),
                            waits=[ex, ("tri_ld", 16)], inc=("b_msk", 1))
            rr = self.v(lambda e: e.tensor_reduce(
                out=ab["dpart"][:, par:par + 1].rearrange("p (a b) -> p a b", b=1),
                in_=att[:, :w].rearrange("p (a b) -> p a b", b=w),
                axis=AX.X, op=ALU.add), waits=[mk_tok], inc=("qs_v", 1))
            self.v(self.den_update_op(tau, par),
                   waits=[rr, ("b_oev", self.oev_gate)], inc=("b_den", 1))
        tr = self.sy(lambda e: e.dma_start_transpose(
            out=ab["attT"][apar][:, :w].rearrange("p (c r) -> p c r", r=128),
            in_=att[:, :w].rearrange("p (c r) -> p c r", r=128)),
            waits=[mk_tok, ("b_av", max(self.av_base, self.av_base + ci - 2)),
                   ("b_att_t", s.count("b_att_t"))],
            inc=("b_att_t", 16))
        vtok = gtoks[a][1] if kind == "g" else None
        return (ch, apar, gpar, tr, vtok)

    def den_update_op(self, tau, par):
        first = self.den_first[tau]
        self.den_first[tau] = False
        dst = self.ab2["den"][:, tau:tau + 1]
        srcp = self.ab2["dpart"][:, par:par + 1]
        if first:
            return lambda e: e.tensor_copy(out=dst, in_=srcp)
        return lambda e: e.tensor_tensor(out=dst, in0=dst, in1=srcp, op=ALU.add)

    def emit_p2_av(self, ch, par, gpar, tr, vtok=None):
        s = self.sc
        ab = self.ab2
        D = self.D
        (tau, kind, a, wt, diag) = ch
        obank = 2 + 2 * (tau % 2)

        firsts = (self.o_first[tau][0], self.o_first[tau][1])
        self.o_first[tau][0] = False
        self.o_first[tau][1] = False

        def mm(e):
            last = None
            for vh in range(2):
                for mt_ in range(wt):
                    lhsT = ab["attT"][par][:, mt_ * 128: mt_ * 128 + 128]
                    if kind == "g":
                        rhs = ab["vg"][gpar][:, mt_ * D + vh * 512: mt_ * D + vh * 512 + 512]
                    else:
                        rhs = ab["vc"][:, (a + mt_) * D + vh * 512: (a + mt_) * D + vh * 512 + 512]
                    last = e.matmul(self.ps[obank + vh].ap()[:, :512], lhsT, rhs,
                                    start=(firsts[vh] and mt_ == 0),
                                    stop=(diag and mt_ == wt - 1))
            return last
        w_ = [tr, ("b_oev", self.oev_gate)]
        if vtok is not None:
            w_.append(vtok)
        tok = self.t(mm, waits=w_, inc=("b_av", 1))
        if kind == "g":
            self.gr_free[gpar] = [tok]

    def emit_p2_fin(self, k, tau):
        s = self.sc
        ab = self.ab2
        D, QT = self.D, self.QT
        t = k * QT + tau
        obank = 2 + 2 * (tau % 2)
        den_n = s.count("b_den")
        par = tau % 2
        x1r = ab["x1r"][par]
        ld = self.gdma(lambda e: e.dma_start(out=x1r[:, :D],
                                             in_=self.x1_hbm[t * 128:(t + 1) * 128, :]),
                       waits=[("b_x1rl", s.count("b_x1rl")),
                              ("b_x2w", max(0, s.count("b_x2w") - 1))],
                       inc=("b_x1rl", 16))
        rd = self.v(lambda e: e.reciprocal(out=ab["den"][:, QT + tau:QT + tau + 1],
                                           in_=ab["den"][:, tau:tau + 1]),
                    waits=[("b_den", den_n), ("b_av", s.count("b_av"))],
                    inc=("qs_v", 1))
        rden = ab["den"][:, QT + tau:QT + tau + 1]
        e1 = self.v(lambda e: e.tensor_scalar(out=ab["ov"][:, 0:512],
                                              in0=self.ps[obank].ap()[:, :512],
                                              scalar1=rden, scalar2=None, op0=ALU.mult),
                    waits=[rd], inc=("b_oev", 1))
        e2 = self.v(lambda e: e.tensor_scalar(out=ab["ov"][:, 512:1024],
                                              in0=self.ps[obank + 1].ap()[:, :512],
                                              scalar1=rden, scalar2=None, op0=ALU.mult),
                    waits=[e1], inc=("b_oev", 1))
        av = self.v(lambda e: e.tensor_tensor(
            out=ab["ov"][:, :D], in0=ab["ov"][:, :D],
            in1=ab["vc"][:, tau * D:(tau + 1) * D], op=ALU.add),
            waits=[e2], inc=("qs_v", 1))
        self.lnb["waits"] = [av]
        r1 = self.emit_ln_stats(ab["ov"][:, :D], D, "b_ln2", use_expln=True,
                                rcol=2 + tau % 2)
        stt_ = ab["stats"][2 + tau % 2]
        rcap = ab["rs"][2 + tau % 2][:, 0:1]
        n1 = self.v(lambda e, stt_=stt_, rcap=rcap: e.tensor_scalar(
            out=ab["ov"][:, :D], in0=ab["ov"][:, :D],
            scalar1=stt_[:, 12:13], scalar2=rcap,
            op0=ALU.subtract, op1=ALU.mult),
                    waits=[r1], inc=("qs_v", 1))
        n2 = self.v(lambda e: e.tensor_tensor(out=ab["ov"][:, :D], in0=ab["ov"][:, :D],
                                              in1=ab["gam"][:, :D], op=ALU.mult),
                    waits=[n1, ("b_cld", 48)], inc=("qs_v", 1))
        n3 = self.v(lambda e: e.tensor_tensor(out=ab["lnr"][:, :D], in0=ab["ov"][:, :D],
                                              in1=ab["bet"][:, :D], op=ALU.add),
                    waits=[n2, ("b_lnt", s.count("b_lnt")),
                           ("b_xnt", s.count("b_xnt"))], inc=("qs_v", 1))
        lt = ab["lnrT"][par]
        tr = self.sy(lambda e: e.dma_start_transpose(
            out=lt.rearrange("p (d r) -> p d r", r=128),
            in_=ab["lnr"][:, :D].rearrange("p (d r) -> p d r", r=128)),
            waits=[n3, ("b_lnt", s.count("b_lnt")),
                   ("b_fc", self.fc_pred)],
            inc=("b_lnt", 16))
        self.fc_pred += 1
        self.pend_fin.append((k, tau, t, tr, ld))

    def emit_p2_finB(self):
        if not self.pend_fin:
            return
        (k, tau, t, tr, ld) = self.pend_fin.pop(0)
        s = self.sc
        ab = self.ab2
        D, QT = self.D, self.QT
        par = tau % 2
        lt = ab["lnrT"][par]

        def fc(e):
            last = None
            for oh in range(2):
                for d in range(self.DCH):
                    last = e.matmul(self.ps[6 + oh].ap()[:, :512],
                                    lt[:, d * 128: d * 128 + 128],
                                    ab["wfc"][:, d * D + oh * 512: d * D + oh * 512 + 512],
                                    start=(d == 0), stop=(d == self.DCH - 1))
            return last
        ft = self.t(fc, waits=[tr, ("b_wc", 32), ("b_x2w", s.count("b_x2w"))],
                    inc=("b_fc", 1))
        x2 = ab["x2st"][par]
        w1 = self.v(lambda e: e.scalar_tensor_tensor(
            out=x2[:, 0:512], in0=self.ps[6].ap()[:, :512], scalar=1.0,
            in1=ab["bfc"][:, 0:512], op0=ALU.mult, op1=ALU.add),
            waits=[ft, ld, ("b_x2s", max(0, 16 * (t - 1)))], inc=("qs_v", 1))
        w2 = self.v(lambda e: e.scalar_tensor_tensor(
            out=x2[:, 512:1024], in0=self.ps[7].ap()[:, :512], scalar=1.0,
            in1=ab["bfc"][:, 512:1024], op0=ALU.mult, op1=ALU.add),
            waits=[w1], inc=("qs_v", 1))
        w3 = self.v(lambda e: e.tensor_tensor(out=x2[:, :D], in0=x2[:, :D],
                                              in1=ab["x1r"][par][:, :D], op=ALU.add),
                    waits=[w2], inc=("b_x2w", 1))
        self.gdma(lambda e: e.dma_start(out=self.x2_hbm[t * 128:(t + 1) * 128, :],
                                        in_=x2[:, :D]),
                  waits=[w3, ("b_x2s", s.count("b_x2s"))], inc=("b_x2s", 16))
        self.v(lambda e: None, inc=("b_att_done", 1))

import ml_dtypes
from concourse.bass_utils import run_bass_kernel_spmd

_NT, _QT, _D, _I = 32, 8, 1024, 2816


def _build():
    nc = bass.Bass("TRN2", target_bir_lowering=False)
    mk = MK(nc, NT=_NT, QT=_QT, D=_D, I=_I)
    mk.emit_consts()
    mk.emit_ffn_phase(first=True)
    mk.emit_p2()
    mk.emit_ffn_phase(first=False)
    mk.sc.finalize_waits()
    with nc.Block() as blk:
        @blk.sync
        def _(e):
            mk.sc.emit_stream("sync", e)

        @blk.scalar
        def _(e):
            mk.sc.emit_stream("scalar", e)

        @blk.vector
        def _(e):
            mk.sc.emit_stream("vector", e)

        @blk.tensor
        def _(e):
            mk.sc.emit_stream("tensor", e)

        @blk.gpsimd
        def _(e):
            mk.sc.emit_stream("gpsimd", e)
    return nc


def kernel(L0_attn_x, L0_attn_Wq, L0_attn_Wk, L0_attn_Wv, L0_attn_Wfc,
           L0_attn_bfc, L0_attn_gamma, L0_attn_beta,
           L0_ffn_w_gate, L0_ffn_w_up, L0_ffn_w_down):
    nc = _build()
    x = np.ascontiguousarray(L0_attn_x)
    tri = np.tril(np.ones((128, 128), np.float32)).astype(ml_dtypes.bfloat16)
    base = {
        "Wq": np.ascontiguousarray(L0_attn_Wq),
        "Wk": np.ascontiguousarray(L0_attn_Wk),
        "Wv": np.ascontiguousarray(L0_attn_Wv),
        "Wfc": np.ascontiguousarray(L0_attn_Wfc),
        "bfc": np.ascontiguousarray(L0_attn_bfc).reshape(1, _D),
        "gamma": np.ascontiguousarray(L0_attn_gamma).reshape(1, _D),
        "beta": np.ascontiguousarray(L0_attn_beta).reshape(1, _D),
        "w_gate": np.ascontiguousarray(L0_ffn_w_gate),
        "w_up": np.ascontiguousarray(L0_ffn_w_up),
        "w_down": np.ascontiguousarray(L0_ffn_w_down),
        "tri": tri,
    }
    in_maps = [dict(base, x=np.ascontiguousarray(x[b])) for b in range(8)]
    res = run_bass_kernel_spmd(nc, in_maps, core_ids=list(range(8)))
    out = np.stack([res.results[b]["y"] for b in range(8)], axis=0)
    return out.astype(np.float32)


# version 26 (final)
# speedup vs baseline: 1.1076x; 1.0115x over previous
import os
import numpy as np
import concourse.bass as bass
import concourse.mybir as mybir

SIM_DRAIN = False
SCL_DRAIN = False

F32 = mybir.dt.float32
BF16 = mybir.dt.bfloat16
AX = mybir.AxisListType
ALU = mybir.AluOpType
ACT = mybir.ActivationFunctionType


class Sched:
    ENGINES = ("sync", "scalar", "vector", "tensor", "gpsimd")

    def __init__(self, nc):
        self.nc = nc
        self.streams = {e: [] for e in self.ENGINES}
        self.sem_handles = {}
        self.sem_counts = {}
        self._sem_ctx = []

    def sem(self, name):
        if name not in self.sem_handles:
            cm = self.nc.semaphore(name)
            h = cm.__enter__()
            self._sem_ctx.append(cm)
            self.sem_handles[name] = h
            self.sem_counts[name] = 0
        return self.sem_handles[name]

    def op(self, eng, emit, waits=(), inc=None, tag=""):
        n_after = None
        if inc is not None:
            name, amt = inc
            self.sem(name)
            self.sem_counts[name] += amt
            n_after = self.sem_counts[name]
        for (s, v) in waits:
            self.sem(s)
        self.streams[eng].append((emit, tuple(waits), inc, tag))
        return (inc[0], n_after) if inc is not None else None

    def count(self, name):
        self.sem(name)
        return self.sem_counts[name]

    def emit_stream(self, eng_name, e):
        for (emit, waits, inc, tag) in self.streams[eng_name]:
            for (s, v) in waits:
                if v > 0:
                    e.wait_ge(self.sem_handles[s], v)
            ins = emit(e)
            if inc is not None:
                name, amt = inc
                if ins is None:
                    e.sem_inc(self.sem_handles[name], amt)
                else:
                    ins.then_inc(self.sem_handles[name], amt)

    def n_ops(self):
        return {k: len(v) for k, v in self.streams.items()}

    def finalize_waits(self, eng="gpsimd"):
        names = [n for n, c in self.sem_counts.items() if c > 0]
        def w(e):
            for n in names:
                e.wait_ge(self.sem_handles[n], self.sem_counts[n])
            return None
        self.streams[eng].append((w, (), None, "final_waits"))


class Arena:
    def __init__(self, nc, name, n_elems, dtype):
        self.t = nc.alloc_sbuf_tensor(name, [128, n_elems], dtype)
        self.n = n_elems
        self.off = 0
        self.marks = []
        self.peak = 0

    def alloc(self, n, align=32):
        o = (self.off + align - 1) // align * align
        assert o + n <= self.n, f"arena {self.t.name} overflow: {o}+{n} > {self.n}"
        self.off = o + n
        self.peak = max(self.peak, self.off)
        return self.t.ap()[:, o:o + n]

    def mark(self):
        self.marks.append(self.off)

    def release(self):
        self.off = self.marks.pop()


class MK:
    def __init__(self, nc, NT=32, QT=8, D=1024, I=2816, expose=()):
        self.nc = nc
        self.NT = NT
        self.QT = QT
        self.NQ = NT // QT
        self.D = D
        self.DCH = D // 128
        self.I = I
        self.ICH = I // 128
        self.IH = I // 2
        self.sc = Sched(nc)
        self.expose = set(expose)

        def dram(name, shape, dt, kind="Internal"):
            if kind == "Internal" and name in self.expose:
                kind = "ExternalOutput"
            return nc.dram_tensor(name, shape, dt, kind=kind)
        self.x_in = dram("x", [NT * 128, D], F32, kind="ExternalInput").ap()
        self.wq_in = dram("Wq", [D, D], F32, kind="ExternalInput").ap()
        self.wk_in = dram("Wk", [D, D], F32, kind="ExternalInput").ap()
        self.wv_in = dram("Wv", [D, D], F32, kind="ExternalInput").ap()
        self.wfc_in = dram("Wfc", [D, D], F32, kind="ExternalInput").ap()
        self.bfc_in = dram("bfc", [1, D], F32, kind="ExternalInput").ap()
        self.gam_in = dram("gamma", [1, D], F32, kind="ExternalInput").ap()
        self.bet_in = dram("beta", [1, D], F32, kind="ExternalInput").ap()
        self.wg_in = dram("w_gate", [I, D], F32, kind="ExternalInput").ap()
        self.wu_in = dram("w_up", [I, D], F32, kind="ExternalInput").ap()
        self.wd_in = dram("w_down", [D, I], F32, kind="ExternalInput").ap()
        self.tri_in = dram("tri", [128, 128], BF16, kind="ExternalInput").ap()
        self.y_out = dram("y", [NT * 128, D], F32, kind="ExternalOutput").ap()
        self.x1_hbm = dram("x1s", [NT * 128, D], BF16).ap()
        self.x2_hbm = dram("x2s", [NT * 128, D], BF16).ap()
        self.kt_hbm = dram("kts", [self.NQ, 128, self.DCH * QT * 128], BF16).ap()
        self.v_hbm = dram("vs", [NT, 128, D], BF16).ap()
        self.wg_hbm = dram("wgs", [128, self.DCH * I], BF16).ap()
        self.wu_hbm = dram("wus", [128, self.DCH * I], BF16).ap()
        self.wd_hbm = dram("wds", [128, self.ICH * D], BF16).ap()
        self.wa_hbm = dram("was", [128, 4 * self.DCH * D], BF16).ap()

        self.AB = Arena(nc, "AB", 86784, BF16)
        self.AF = Arena(nc, "AF", 9504, F32)
        self.c_eps = nc.alloc_sbuf_tensor("c_eps", [128, 1], F32).ap()
        self.c_b20 = nc.alloc_sbuf_tensor("c_b20", [128, 1], F32).ap()
        self.c_s32 = nc.alloc_sbuf_tensor("c_s32", [128, 1], F32).ap()
        self.c_one = nc.alloc_sbuf_tensor("c_one", [128, 1], F32).ap()
        self.c_mh = nc.alloc_sbuf_tensor("c_mh", [128, 1], F32).ap()
        self.tri = nc.alloc_sbuf_tensor("tri_sb", [128, 128], BF16).ap()
        self.ps = [nc.alloc_psum_tensor(f"ps{i}", [128, 512], F32) for i in range(8)]

    def v(self, emit, waits=(), inc=None, tag=""):
        return self.sc.op("vector", emit, waits, inc, tag)

    def scl(self, emit, waits=(), inc=None, tag=""):
        return self.sc.op("scalar", emit, waits, inc, tag)

    def gdma(self, emit, waits=(), inc=None, tag=""):
        return self.sc.op("gpsimd", emit, waits, inc, tag)

    def sy(self, emit, waits=(), inc=None, tag=""):
        return self.sc.op("sync", emit, waits, inc, tag)

    def t(self, emit, waits=(), inc=None, tag=""):
        return self.sc.op("tensor", emit, waits, inc, tag)

    def emit_quant(self, qin32, N, waits_src, out_q, rel_key=None, cmp_bf16=False):
        s = self.sc
        q = self.qb
        NB = N // 16
        NH = N // 2
        NBH = NB // 2
        pre = "qs"
        w_src = list(waits_src)
        H = ((0, NH, 0, NBH), (NH, N, NBH, NB))
        sp = self.qpar
        self.qpar ^= 1
        sgbuf = q["sg"][sp]
        sgg = self.sg_guard[sp] or []

        gtok = []
        for (lo, hi, _, _) in H:
            gtok.append(self.scl(lambda e, lo=lo, hi=hi: e.activation(
                out=sgbuf[:, lo:hi], in_=qin32[:, lo:hi], func=ACT.Sign),
                waits=w_src + sgg, inc=(pre + "_as", 1)))

        def VH(emitfn, dep=(), extra=()):
            toks = []
            for i, (lo, hi, blo, bhi) in enumerate(H):
                w = list(extra)
                if dep:
                    w.append(dep[i])
                toks.append(self.v(lambda e, lo=lo, hi=hi, blo=blo, bhi=bhi:
                                   emitfn(e, lo, hi, blo, bhi),
                                   waits=w, inc=(pre + "_v", 1)))
            return toks

        ab_ = []
        for i, (lo, hi, _, _) in enumerate(H):
            ab_.append(self.scl(lambda e, lo=lo, hi=hi: e.activation(
                out=qin32[:, lo:hi], in_=qin32[:, lo:hi], func=ACT.Abs),
                waits=[gtok[i]], inc=(pre + "_as", 1)))
        bm = VH(lambda e, lo, hi, blo, bhi: e.tensor_reduce(
            out=q["bm"][:, blo:bhi].rearrange("p (b i) -> p b i", i=1),
            in_=qin32[:, lo:hi].rearrange("p (b i) -> p b i", i=16),
            axis=AX.X, op=ALU.max), dep=ab_)
        s32 = VH(lambda e, lo, hi, blo, bhi: e.tensor_scalar(
            out=q["s32"][:, blo:bhi], in0=q["bm"][:, blo:bhi],
            scalar1=1.0 / 6.0, scalar2=1e-12, op0=ALU.mult, op1=ALU.max), dep=bm)
        r32 = VH(lambda e, lo, hi, blo, bhi: e.reciprocal(
            out=q["r32"][:, blo:bhi], in_=q["s32"][:, blo:bhi]), dep=s32)
        adst = q["axn"] if cmp_bf16 else qin32
        t1 = VH(lambda e, lo, hi, blo, bhi: e.tensor_tensor(
            out=adst[:, lo:hi].rearrange("p (b i) -> p b i", i=16),
            in0=qin32[:, lo:hi].rearrange("p (b i) -> p b i", i=16),
            in1=q["r32"][:, blo:bhi].rearrange("p (b i) -> p b i", i=1)
                .broadcast_to([128, NBH, 16]),
            op=ALU.mult), dep=r32)
        csrc = adst
        t2 = VH(lambda e, lo, hi, blo, bhi: e.tensor_single_scalar(
            out=q["t2"][:, lo:hi], in_=csrc[:, lo:hi], scalar=0.25, op=ALU.is_gt),
            dep=t1)
        cbp = t1
        for th in (0.75, 1.25, 1.75):
            cb = VH(lambda e, lo, hi, blo, bhi, th=th: e.tensor_single_scalar(
                out=q["cb"][:, lo:hi], in_=csrc[:, lo:hi], scalar=th, op=ALU.is_gt),
                dep=t1 if th == 0.75 else cbp)
            t2 = VH(lambda e, lo, hi, blo, bhi: e.tensor_add(
                out=q["t2"][:, lo:hi], in0=q["t2"][:, lo:hi], in1=q["cb"][:, lo:hi]),
                dep=[cb[0], cb[1]], extra=[t2[0], t2[1]])
            cbp = t2
        b1 = VH(lambda e, lo, hi, blo, bhi: e.tensor_single_scalar(
            out=q["cb"][:, lo:hi], in_=csrc[:, lo:hi], scalar=2.5, op=ALU.is_gt),
            dep=t2)
        b3 = VH(lambda e, lo, hi, blo, bhi: e.scalar_tensor_tensor(
            out=q["cb"][:, lo:hi], in0=csrc[:, lo:hi], scalar=3.5, in1=q["cb"][:, lo:hi],
            op0=ALU.is_gt, op1=ALU.add), dep=b1)
        eh = VH(lambda e, lo, hi, blo, bhi: e.tensor_single_scalar(
            out=q["t2"][:, lo:hi], in_=q["t2"][:, lo:hi], scalar=0.5, op=ALU.mult),
            dep=t2)
        e1 = VH(lambda e, lo, hi, blo, bhi: e.tensor_add(
            out=q["t2"][:, lo:hi], in0=q["t2"][:, lo:hi], in1=q["cb"][:, lo:hi]),
            dep=[b3[0], b3[1]], extra=[eh[0], eh[1]])
        c7 = VH(lambda e, lo, hi, blo, bhi: e.tensor_scalar(
            out=q["cb"][:, lo:hi], in0=csrc[:, lo:hi], scalar1=5.0, scalar2=2.0,
            op0=ALU.is_gt, op1=ALU.mult), dep=e1)
        if rel_key is not None:
            self.qrel[rel_key] = [c7[0], c7[1]]
        e2 = VH(lambda e, lo, hi, blo, bhi: e.tensor_add(
            out=q["t2"][:, lo:hi], in0=q["t2"][:, lo:hi], in1=q["cb"][:, lo:hi]),
            dep=[c7[0], c7[1]], extra=[e1[0], e1[1]])
        ss = VH(lambda e, lo, hi, blo, bhi: e.tensor_tensor(
            out=q["axn"][:, lo:hi].rearrange("p (b i) -> p b i", i=16),
            in0=sgbuf[:, lo:hi].rearrange("p (b i) -> p b i", i=16),
            in1=q["s32"][:, blo:bhi].rearrange("p (b i) -> p b i", i=1)
                .broadcast_to([128, NBH, 16]),
            op=ALU.mult), dep=e2, extra=[s32[0], s32[1]])
        self.sg_guard[sp] = [ss[0], ss[1]]
        (lo0, hi0, _, _), (lo1, hi1, _, _) = H
        tok0 = self.v(lambda e: e.tensor_tensor(
            out=out_q[:, lo0:hi0], in0=q["t2"][:, lo0:hi0], in1=q["axn"][:, lo0:hi0],
            op=ALU.mult), waits=[ss[0]] + list(self.qout_free), inc=(pre + "_v", 1))
        return self.v(lambda e: e.tensor_tensor(
            out=out_q[:, lo1:hi1], in0=q["t2"][:, lo1:hi1], in1=q["axn"][:, lo1:hi1],
            op=ALU.mult), waits=[ss[1], tok0], inc=(pre + "_q", 1))

    def emit_ln_stats(self, src, N, pre, use_expln, rcol=0):
        st = self.lnb["stats"][rcol % 4]
        rs = self.lnb["rs"][rcol % 4]
        s = self.sc
        nch = (N + 511) // 512
        prev_rs = max(0, s.count(pre + "_rs") - 1)

        w0 = list(self.lnb["waits"]) + ([(pre + "_rs", prev_rs)] if prev_rs else [])
        btok = []
        for c in range(nch):
            lo = c * 512
            hi = min(N, lo + 512)
            btok.append(self.v(lambda e, c=c, lo=lo, hi=hi: e.bn_stats(
                out=st[:, 6 * c:6 * c + 6], in_=src[:, lo:hi]), waits=w0,
                inc=("qs_v", 1)))
        ag = self.v(lambda e: e.bn_aggr(out=st[:, 12:14], in_=st[:, 0:6 * nch]),
                    waits=btok, inc=("qs_v", 1))
        ve = self.v(lambda e: e.tensor_scalar(out=st[:, 14:15], in0=st[:, 13:14],
                                              scalar1=1e-5, scalar2=None, op0=ALU.add),
                    waits=[ag], inc=("qs_v", 1))
        v1 = self.v(lambda e: e.reciprocal(out=st[:, 15:16], in_=st[:, 14:15]),
                    waits=[ve], inc=(pre + "_st", 1))
        rc = rs[:, 0:1]
        if use_expln:
            lt = self.scl(lambda e: e.activation(out=rc, in_=st[:, 13:14],
                                                 func=ACT.Ln, bias=self.c_eps,
                                                 scale=self.c_one),
                          waits=[v1, ("c_ini", 1)], inc=("qs_s", 1))
            r1 = self.scl(lambda e: e.activation(out=rc, in_=rc,
                                                 func=ACT.Exp, scale=self.c_mh),
                          waits=[lt], inc=(pre + "_rs", 1))
        else:
            r1 = self.scl(lambda e: e.activation(out=rc, in_=st[:, 15:16],
                                                 func=ACT.Sqrt),
                          waits=[v1, ("c_ini", 1)], inc=(pre + "_rs", 1))
        return r1

    def alloc_ffn_bufs(self, with_prep):
        AB, AF = self.AB, self.AF
        I, D, IH = self.I, self.D, self.IH
        b = {}
        b["wgqT"] = AB.alloc(self.DCH * I)
        b["wuqT"] = AB.alloc(self.DCH * I)
        b["wdqT"] = AB.alloc(self.ICH * D)
        b["txq"] = [AB.alloc(D) for _ in range(2)]
        b["hsg"] = AB.alloc(IH)
        b["qout"] = AB.alloc(IH)
        b["thq"] = [AB.alloc(IH), AB.alloc(IH)]
        b["xst"] = [AB.alloc(D) for _ in range(2)] if with_prep else None
        if with_prep:
            b["wst"] = [AF.alloc(IH) for _ in range(2)]
            b["wab"] = [AB.alloc(D), AB.alloc(D)]
        q = {}
        q["axn"] = AB.alloc(IH)
        q["sg"] = [AB.alloc(IH), AB.alloc(IH)]
        q["t2"] = AB.alloc(IH)
        q["cb"] = AB.alloc(IH)
        _qs = AF.alloc(3 * (IH // 16))
        q["bm"] = _qs[:, 0:IH // 16]
        q["s32"] = _qs[:, IH // 16:2 * (IH // 16)]
        q["r32"] = _qs[:, 2 * (IH // 16):3 * (IH // 16)]
        q["in32"] = [AF.alloc(IH), AF.alloc(IH)]
        b["q"] = q
        b["xin"] = [AF.alloc(D) for _ in range(3)] if with_prep else None
        b["x2in"] = None if with_prep else [AB.alloc(D) for _ in range(3)]
        b["yst"] = None if with_prep else [AF.alloc(D) for _ in range(2)]
        stt_ = AF.alloc(128)
        b["stats"] = [stt_[:, 32 * i:32 * i + 16] for i in range(4)]
        b["rs"] = [stt_[:, 32 * i + 16:32 * i + 20] for i in range(4)]
        return b

    def emit_consts(self):
        self.v(lambda e: e.memset(self.c_eps, 1e-5))
        self.v(lambda e: e.memset(self.c_b20, -20.0))
        self.v(lambda e: e.memset(self.c_s32, 1.0 / 32.0))
        self.v(lambda e: e.memset(self.c_one, 1.0))
        self.v(lambda e: e.memset(self.c_mh, -0.5), inc=("c_ini", 1))
        self.sy(lambda e: e.dma_start(out=self.tri, in_=self.tri_in), inc=("tri_ld", 16))

    def emit_wchunk(self, kind, c, sub):
        s = self.sc
        fb = self.fb
        D, I, IH = self.D, self.I, self.IH
        j = self.wj % 2
        self.wj += 1
        wst = fb["wst"][j]
        wprev = self.qrel.get(("wst", j), [])
        if kind in ("g", "u"):
            N = D
            src_h = self.wg_in if kind == "g" else self.wu_in
            ld = self.gdma(lambda e: e.dma_start(out=wst[:, :N],
                                                 in_=src_h[c * 128:(c + 1) * 128, :]),
                           waits=wprev + [("w_ld", s.count("w_ld"))],
                           inc=("w_ld", 16))
        else:
            N = IH
            ld = self.gdma(lambda e: e.dma_start(
                out=wst[:, :N], in_=self.wd_in[c * 128:(c + 1) * 128, sub * IH:(sub + 1) * IH]),
                waits=wprev + [("w_ld", s.count("w_ld"))],
                inc=("w_ld", 16))
        wsb = fb["qout"]
        qt = self.emit_quant(wst[:, :N], N, [ld], wsb[:, :N],
                             rel_key=("wst", j))
        if kind in ("g", "u"):
            dst = fb["wgqT"] if kind == "g" else fb["wuqT"]
            out_ap = dst.rearrange("p (d i) -> p d i", i=I)[:, :, c * 128:(c + 1) * 128]
            in_ap = wsb[:, :N].rearrange("p (d r) -> p d r", r=128)
        else:
            dst = fb["wdqT"]
            out_ap = dst.rearrange("p (i d) -> p i d", d=D)[:, sub * 11:(sub + 1) * 11,
                                                           c * 128:(c + 1) * 128]
            in_ap = wsb[:, :N].rearrange("p (i r) -> p i r", r=128)
        tt = self.sy(lambda e: e.dma_start_transpose(out=out_ap, in_=in_ap),
                     waits=[qt, ("w_t", s.count("w_t"))], inc=("w_t", 16))
        self.qout_free = [tt]

    def emit_ffn_phase(self, first):
        s = self.sc
        NT = self.NT
        pre = "a" if first else "c"
        self.AB.mark()
        self.AF.mark()
        fb = self.alloc_ffn_bufs(with_prep=first)
        self.fb = fb
        self.qb = fb["q"]
        self.qpar = 0
        self.qinpar = 0
        self.sg_guard = [None, None]
        self.qrel = {}
        self.qout_free = []
        self.lnb = {"stats": fb["stats"], "rs": fb["rs"], "waits": []}
        if first:
            self.wj = 0
            order = ([("g", c, 0) for c in range(11)] + [("u", c, 0) for c in range(11)]
                     + [("g", c, 0) for c in range(11, 22)]
                     + [("u", c, 0) for c in range(11, 22)]
                     + [("d", c, sub) for c in range(8) for sub in range(2)])
            for (kind, c, sub) in order:
                self.emit_wchunk(kind, c, sub)
            self.w_thr = {"gu0": 16 * 22, "gu1": 16 * 44, "d": 16 * 60}
            wt_all = 16 * 60
            for (dst_h, src_a) in ((self.wg_hbm, fb["wgqT"]), (self.wu_hbm, fb["wuqT"]),
                                   (self.wd_hbm, fb["wdqT"])):
                self.gdma(lambda e, d=dst_h, a=src_a: e.dma_start(out=d, in_=a),
                          waits=[("w_t", wt_all)], inc=("w_sp", 16))
            self.emit_ffn_xpart(0, pre, first)
            self.emit_ffn_xrest(pre, first)
            self.emit_ffn_xpart(1, pre, first)
            xpre = 2
            self.wa_state = {"z_ld": 0, "z_cs": 0,
                             "srcs": (self.wq_in, self.wk_in, self.wv_in, self.wfc_in)}
            self.emit_wa_ld()
            self.emit_wa_ld()
        else:
            xpre = 0
            nld = 0
            early = [("b_qkmm", self.sc.count("b_qkmm")), ("b_vmm", self.sc.count("b_vmm")),
                     ("b_smm", self.sc.count("b_smm")), ("b_exp", self.sc.count("b_exp")),
                     ("b_att_done", self.sc.count("b_att_done")),
                     ("b_x2s", self.sc.count("b_x2s"))]
            for (src_h, dst_a) in ((self.wg_hbm, fb["wgqT"]), (self.wu_hbm, fb["wuqT"]),
                                   (self.wd_hbm, fb["wdqT"])):
                for piece in range(4):
                    n = src_h.shape[1] // 4
                    gate = early if nld < 3 else list(self.barrier_waits("gpsimd"))
                    self.gdma(lambda e, sh=src_h, da=dst_a, p=piece, n=n:
                              e.dma_start(out=da[:, p * n:(p + 1) * n],
                                          in_=sh[:, p * n:(p + 1) * n]),
                              waits=gate + [("w3_ld", self.sc.count("w3_ld"))],
                              inc=("w3_ld", 16))
                    nld += 1
                    if nld == 3:
                        self.emit_ffn_xpart(0, pre, first)
                        self.emit_ffn_xrest(pre, first)
                        self.emit_ffn_xpart(1, pre, first)
                        xpre = 2
            self.w_thr = {"gu0": 16 * nld, "gu1": 16 * nld, "d": 16 * nld}
            self.w_thr["sem"] = "w3_ld"

        if xpre == 0:
            self.emit_ffn_xpart(0, pre, first)
            self.emit_ffn_xrest(pre, first)
        for t in range(NT):
            self.emit_ffn_gu(t, 0, pre, "w_t" if first else "w3_ld")
            if t + 1 < NT and t + 1 >= xpre:
                self.emit_ffn_xpart(t + 1, pre, first)
            self.emit_ffn_evhm(t, 0, pre)
            if t >= 1:
                self.emit_ffn_down(t - 1, pre, first, 0)
            if t + 1 < NT:
                self.emit_ffn_xrest(pre, first)
            self.emit_ffn_hqchain(pre)
            self.emit_ffn_gu(t, 1, pre, "w_t" if first else "w3_ld")
            self.emit_ffn_evhm(t, 1, pre)
            if t >= 1:
                self.emit_ffn_down(t - 1, pre, first, 1)
                self.emit_ffn_resid(t - 1, pre, first)
            self.emit_ffn_hqchain(pre)
            if first:
                self.emit_wa_pair()
        self.emit_ffn_down(NT - 1, pre, first, 0)
        self.emit_ffn_down(NT - 1, pre, first, 1)
        self.emit_ffn_resid(NT - 1, pre, first)
        if first:
            while self.wa_state["z_cs"] < 4 * self.DCH:
                self.emit_wa_pair()
        for eng in Sched.ENGINES:
            bw = []
            if eng == "gpsimd":
                bw = [(pre + "_x1s", s.count(pre + "_x1s"))]
                if first:
                    bw.append(("w_sp", s.count("w_sp")))
            s.op(eng, lambda e: None, waits=bw,
                 inc=(("p1b_" if first else "p3b_") + eng, 1))
        self.AB.release()
        self.AF.release()

    def barrier_waits(self, eng, which="p2b_"):
        return [(which + e2, self.sc.count(which + e2)) for e2 in Sched.ENGINES]

    def emit_wa_ld(self):
        st_ = self.wa_state
        z = st_["z_ld"]
        if z >= 4 * self.DCH:
            return
        st_["z_ld"] += 1
        s = self.sc
        D = self.D
        fb = self.fb
        src_h = st_["srcs"][z // self.DCH]
        c = z % self.DCH
        j = self.wj % 2
        self.wj += 1
        wst = fb["wst"][j]
        wprev = self.qrel.get(("wst", j), [])
        ld = self.gdma(lambda e, sh=src_h, c=c, wst=wst: e.dma_start(
            out=wst[:, :D], in_=sh[c * 128:(c + 1) * 128, :]),
            waits=wprev + [("w_ld", s.count("w_ld"))], inc=("w_ld", 16))
        st_[("ld", z)] = (ld, wst)

    def emit_wa_pair(self):
        st_ = self.wa_state
        s = self.sc
        D = self.D
        fb = self.fb
        for _ in range(2):
            z = st_["z_cs"]
            if z >= 4 * self.DCH:
                return
            st_["z_cs"] += 1
            (ld, wst) = st_.pop(("ld", z))
            wb = fb["wab"][z % 2]
            cz = self.scl(lambda e, wst=wst, wb=wb: e.activation(
                out=wb[:, :D], in_=wst[:, :D], func=ACT.Copy),
                waits=[ld, ("wa_sp", max(0, 16 * (z - 1)))], inc=("w_ac", 1))
            self.qrel[("wst", z % 2)] = [cz]
            self.gdma(lambda e, z=z, wb=wb: e.dma_start(
                out=self.wa_hbm[:, z * D:(z + 1) * D], in_=wb[:, :D]),
                waits=[cz, ("wa_sp", s.count("wa_sp"))], inc=("wa_sp", 16))
        self.emit_wa_ld()
        self.emit_wa_ld()

    def emit_ffn_xpart(self, t, pre, first):
        s = self.sc
        fb = self.fb
        D = self.D
        if first:
            xsrc = fb["xin"][t % 3]
            ld = self.gdma(lambda e: e.dma_start(out=xsrc[:, :D],
                                                 in_=self.x_in[t * 128:(t + 1) * 128, :]),
                           waits=[(pre + "_x1", max(0, t - 2)),
                                  (pre + "_x0", s.count(pre + "_x0"))],
                           inc=(pre + "_x0", 16))
        else:
            xsrc = fb["x2in"][t % 3]
            ld = self.gdma(lambda e: e.dma_start(out=xsrc[:, :D],
                                                 in_=self.x2_hbm[t * 128:(t + 1) * 128, :]),
                           waits=[(pre + "_x1", max(0, t - 2)), ("b_x2s", 16 * (t + 1)),
                                  (pre + "_x0", s.count(pre + "_x0"))]
                           + (self.barrier_waits("gpsimd") if t < 3 else []),
                           inc=(pre + "_x0", 16))
        self.lnb["waits"] = [ld]
        r1 = self.emit_ln_stats(xsrc[:, :D], D, pre + "_ln", use_expln=False, rcol=t % 2)
        self.xp_state = (t, xsrc, r1)

    def emit_ffn_xrest(self, pre, first):
        s = self.sc
        fb = self.fb
        D = self.D
        (t, xsrc, r1) = self.xp_state
        st = fb["stats"][t % 2]
        qp = self.qinpar
        self.qinpar ^= 1
        qin = fb["q"]["in32"][qp]
        rcap = fb["rs"][t % 2][:, 0:1]
        ax = self.v(lambda e: e.tensor_scalar(out=qin[:, :D], in0=xsrc[:, :D],
                                              scalar1=st[:, 12:13], scalar2=rcap,
                                              op0=ALU.subtract, op1=ALU.mult),
                    waits=[r1] + self.qrel.get(("qin", qp), []),
                    inc=(pre + "_xn", 1))
        qx = self.emit_quant(qin[:, :D], D, [ax], fb["qout"][:, :D],
                             rel_key=("qin", qp))
        par = t % 2
        tt = self.sy(lambda e: e.dma_start_transpose(
            out=fb["txq"][par].rearrange("p (d r) -> p d r", r=128),
            in_=fb["qout"][:, :D].rearrange("p (d r) -> p d r", r=128)),
            waits=[qx, (pre + "_gu", max(0, 2 * (t - 1))),
                   (pre + "_txq", s.count(pre + "_txq"))], inc=(pre + "_txq", 16))
        self.qout_free = [tt]

    def emit_ffn_gu(self, t, h, pre, wsem):
        fb = self.fb
        IH = self.IH
        par = t % 2
        ncol = [512, 512, IH - 1024]
        wthr = self.w_thr["gu0"] if h == 0 else self.w_thr["gu1"]

        def gu(e):
            last = None
            for d in range(self.DCH):
                lhsT = fb["txq"][par][:, d * 128:(d + 1) * 128]
                for c in range(3):
                    w = ncol[c]
                    off = d * self.I + h * IH + c * 512
                    e.matmul(self.ps[c].ap()[:, :w], lhsT,
                             fb["wgqT"][:, off:off + w],
                             start=(d == 0), stop=(d == self.DCH - 1))
                    last = e.matmul(self.ps[3 + c].ap()[:, :w], lhsT,
                                    fb["wuqT"][:, off:off + w],
                                    start=(d == 0), stop=(d == self.DCH - 1))
            return last
        self.t(gu, waits=[(pre + "_txq", 16 * (t + 1)), (wsem, wthr),
                          (pre + "_ev", max(0, 2 * t + h)),
                          (pre + "_hm", max(0, 2 * t + h))], inc=(pre + "_gu", 1))

    def emit_ffn_evhm(self, t, h, pre):
        s = self.sc
        fb = self.fb
        IH = self.IH
        ncol = [512, 512, IH - 1024]

        def ev(e):
            last = None
            for c in range(3):
                w = ncol[c]
                last = e.activation(out=fb["hsg"][:, c * 512:c * 512 + w],
                                    in_=self.ps[c].ap()[:, :w], func=ACT.Silu)
            return last
        evt = self.sc.op("scalar", ev,
                         waits=[(pre + "_gu", 2 * t + h + 1), (pre + "_hm", max(0, 2 * t + h))],
                         inc=(pre + "_ev", 1))
        qp = self.qinpar
        self.qinpar ^= 1
        qin = fb["q"]["in32"][qp]

        def hmul(e):
            last = None
            for c in range(3):
                w = ncol[c]
                last = e.tensor_tensor(out=qin[:, c * 512:c * 512 + w],
                                       in0=fb["hsg"][:, c * 512:c * 512 + w],
                                       in1=self.ps[3 + c].ap()[:, :w], op=ALU.mult)
            return last
        hm = self.sc.op("vector", hmul,
                        waits=[evt] + self.qrel.get(("qin", qp), []),
                        inc=(pre + "_hm", 1))
        self.hq_state = (t, h, qp, qin, hm)

    def emit_ffn_hqchain(self, pre):
        s = self.sc
        fb = self.fb
        IH = self.IH
        (t, h, qp, qin, hm) = self.hq_state
        qh = self.emit_quant(qin[:, :IH], IH, [hm], fb["qout"][:, :IH],
                             rel_key=("qin", qp))
        tt = self.sy(lambda e: e.dma_start_transpose(
            out=fb["thq"][h].rearrange("p (i r) -> p i r", r=128),
            in_=fb["qout"][:, :IH].rearrange("p (i r) -> p i r", r=128)),
            waits=[qh, (pre + "_dn", max(0, t)),
                   (pre + "_thq", self.sc.count(pre + "_thq"))], inc=(pre + "_thq", 16))
        self.qout_free = [tt]

    def emit_ffn_down(self, t, pre, first, colh):
        fb = self.fb
        D = self.D
        wsem = "w_t" if first else "w3_ld"

        def dn(e):
            last = None
            for i in range(self.ICH):
                hh = 0 if i < 11 else 1
                ioff = (i - 11 * hh) * 128
                last = e.matmul(self.ps[6 + colh].ap()[:, :512],
                                fb["thq"][hh][:, ioff:ioff + 128],
                                fb["wdqT"][:, i * D + colh * 512: i * D + colh * 512 + 512],
                                start=(i == 0), stop=(i == self.ICH - 1))
            return last
        if colh == 0:
            self.t(dn, waits=[(pre + "_thq", 16 * (2 * t + 2)),
                              (wsem, self.w_thr["d"]),
                              (pre + "_x1", max(0, t))])
        else:
            self.t(dn, inc=(pre + "_dn", 1))

    def emit_ffn_resid(self, t, pre, first):
        s = self.sc
        fb = self.fb
        D = self.D
        par = t % 2
        xsrc = (fb["xin"] if first else fb["x2in"])[t % 3]
        dst = (fb["xst"] if first else fb["yst"])[par]
        out_hbm = self.x1_hbm if first else self.y_out
        self.v(lambda e: e.scalar_tensor_tensor(
            out=dst[:, 0:512], in0=self.ps[6].ap()[:, :512],
            scalar=0.5, in1=xsrc[:, 0:512], op0=ALU.mult, op1=ALU.add),
            waits=[(pre + "_dn", s.count(pre + "_dn")),
                   (pre + "_x1s", max(0, 16 * (t - 1)))])
        r2 = self.v(lambda e: e.scalar_tensor_tensor(
            out=dst[:, 512:1024], in0=self.ps[7].ap()[:, :512],
            scalar=0.5, in1=xsrc[:, 512:1024], op0=ALU.mult, op1=ALU.add),
            inc=(pre + "_x1", 1))
        self.gdma(lambda e: e.dma_start(out=out_hbm[t * 128:(t + 1) * 128, :],
                                        in_=dst[:, :D]),
                  waits=[r2, (pre + "_x1s", s.count(pre + "_x1s"))],
                  inc=(pre + "_x1s", 16))

    def finalize(self):
        self.gdma(lambda e: None, waits=[("c_x1s", self.sc.count("c_x1s"))])

    def alloc_attn_bufs(self):
        AB, AF = self.AB, self.AF
        D, QT = self.D, self.QT
        b = {}
        b["wq"] = AB.alloc(self.DCH * D)
        b["wk"] = AB.alloc(self.DCH * D)
        b["wv"] = AB.alloc(self.DCH * D)
        b["wfc"] = AB.alloc(self.DCH * D)
        b["xnT"] = AB.alloc(self.DCH * QT * 128)
        b["qT"] = AB.alloc(self.DCH * QT * 128)
        b["ktc"] = AB.alloc(self.DCH * QT * 128)
        b["vc"] = AB.alloc(QT * D)
        gkn = self.DCH * min(4, QT) * 128
        if 2 * gkn <= self.DCH * QT * 128:
            b["ktg"] = [b["xnT"][:, 0:gkn], b["xnT"][:, gkn:2 * gkn]]
        else:
            b["ktg"] = [AB.alloc(gkn) for _ in range(2)]
        b["vg"] = [AB.alloc(4 * D) for _ in range(2)]
        b["x1q"] = [AB.alloc(D) for _ in range(2)]
        b["x1r"] = [AB.alloc(D) for _ in range(2)]
        b["x2st"] = [AB.alloc(D) for _ in range(2)]
        b["att"] = [AB.alloc(512) for _ in range(3)]
        b["attT"] = [AB.alloc(512) for _ in range(3)]
        b["lnr"] = AB.alloc(D)
        b["xnb"] = [b["lnr"]]
        b["lnrT"] = [AB.alloc(D), AB.alloc(D)]
        b["gam"] = AF.alloc(D)
        b["bet"] = AF.alloc(D)
        b["bfc"] = AF.alloc(D)
        _st2 = AF.alloc(128)
        b["stats"] = [_st2[:, 32 * i:32 * i + 16] for i in range(4)]
        b["rs"] = [_st2[:, 32 * i + 16:32 * i + 20] for i in range(4)]
        b["wstg"] = [AF.alloc(D) for _ in range(2)]
        b["den"] = AF.alloc(2 * QT)
        b["dpart"] = AF.alloc(4)
        b["ov"] = AF.alloc(D)
        return b

    def emit_p2(self):
        s = self.sc
        NT, QT, NQ, D = self.NT, self.QT, self.NQ, self.D
        self.AB.mark()
        self.AF.mark()
        ab = self.alloc_attn_bufs()
        self.ab2 = ab
        self.qb = None
        self.lnb = {"stats": ab["stats"], "rs": ab["rs"], "waits": []}
        p1w = self.barrier_waits("gpsimd", "p1b_")

        for (wi, dst) in enumerate((ab["wq"], ab["wk"], ab["wv"], ab["wfc"])):
            self.gdma(lambda e, wi=wi, dst=dst: e.dma_start(
                out=dst, in_=self.wa_hbm[:, wi * self.DCH * self.D:
                                         (wi + 1) * self.DCH * self.D]),
                waits=p1w + [("wa_sp", 16 * 32), ("b_wld", s.count("b_wld"))],
                inc=("b_wld", 16))
        self.sc.op("scalar", lambda e: None,
                   waits=[("b_wld", s.count("b_wld"))], inc=("b_wc", 32))
        for (src_h, dst) in ((self.gam_in, ab["gam"]), (self.bet_in, ab["bet"]),
                             (self.bfc_in, ab["bfc"])):
            self.sy(lambda e, sh=src_h, dst=dst: e.dma_start(
                out=dst, in_=sh.broadcast_to([128, D])),
                waits=self.barrier_waits("sync", "p1b_") + [("b_cld", s.count("b_cld"))],
                inc=("b_cld", 16))

        self.pend_fin = []
        self.fc_pred = 0
        self.gr_last_av = {0: 0, 1: 0}
        for k in range(NQ):
            self.den_first = {tau: True for tau in range(self.QT)}
            self.o_first = {tau: [True, True] for tau in range(self.QT)}
            self.gr_free = {0: [], 1: []}
            self.emit_p2_qkv(k)
            self.emit_p2_attn(k)
            while self.pend_fin:
                self.emit_p2_finB()
        for eng in Sched.ENGINES:
            bw = []
            if eng == "gpsimd":
                bw = [(n, s.count(n)) for n in
                      ("b_ktsp", "b_vsp", "b_x2s", "b_gl", "b_x1l", "b_x1rl", "b_wld")]
            s.op(eng, lambda e: None, waits=bw, inc=("p2b_" + eng, 1))
        self.AB.release()
        self.AF.release()

    def emit_p2_qkv(self, k):
        s = self.sc
        ab = self.ab2
        D, QT = self.D, self.QT
        xnt_base = s.count("b_xnt")
        for tau in range(QT):
            t = k * QT + tau
            xb = ab["x1q"][tau % 2]
            ld = self.gdma(lambda e, t=t, xb=xb: e.dma_start(
                out=xb[:, :D], in_=self.x1_hbm[t * 128:(t + 1) * 128, :]),
                waits=[("a_x1s", 16 * self.NT), ("b_x1f", max(0, s.count("b_x1f") - 1)),
                       ("b_x1l", s.count("b_x1l"))],
                inc=("b_x1l", 16))
            self.lnb["waits"] = [ld]
            r1 = self.emit_ln_stats(xb[:, :D], D, "b_ln", use_expln=True, rcol=tau % 2)
            xn = ab["xnb"][0]
            st = ab["stats"][tau % 2]
            rcap = ab["rs"][tau % 2][:, 0:1]
            ax = self.v(lambda e, xb=xb, xn=xn, st=st, rcap=rcap: e.tensor_scalar(
                out=xn[:, :D], in0=xb[:, :D], scalar1=st[:, 12:13], scalar2=rcap,
                op0=ALU.subtract, op1=ALU.mult),
                waits=[r1, ("b_xnt", s.count("b_xnt")),
                       ("b_lnt", s.count("b_lnt"))],
                inc=("b_x1f", 1))
            self.sy(lambda e, xn=xn, tau=tau: e.dma_start_transpose(
                out=ab["xnT"].rearrange("p (d r) -> p d r", r=QT * 128)[:, :, tau * 128:(tau + 1) * 128],
                in_=xn[:, :D].rearrange("p (d r) -> p d r", r=128)),
                waits=[ax, ("b_xnt", s.count("b_xnt")),
                       ("b_qkmm", s.count("b_qkmm")), ("b_vmm", s.count("b_vmm")),
                       ("b_exp", s.count("b_exp")), ("b_gl", s.count("b_gl"))],
                inc=("b_xnt", 16))
        rhw = min(512, QT * 128)
        nrh = (QT * 128) // rhw
        qkctr = 0
        for rh in range(nrh):
            xnt_rh = xnt_base + 16 * (rh + 1) * (rhw // 128)
            for (wname, dsts, spill) in (("wq", "qT", False), ("wk", "ktc", True)):
                w = ab[wname]
                dst = ab[dsts]
                for dk in range(self.DCH):
                    pspar = qkctr % 2
                    qkctr += 1
                    def mm(e, w=w, dk=dk, rh=rh, pspar=pspar):
                        last = None
                        for d in range(self.DCH):
                            last = e.matmul(self.ps[pspar].ap()[:, :rhw],
                                            w[:, d * D + dk * 128: d * D + dk * 128 + 128],
                                            ab["xnT"][:, d * QT * 128 + rh * rhw:
                                                      d * QT * 128 + rh * rhw + rhw],
                                            start=(d == 0), stop=(d == self.DCH - 1))
                        return last
                    mt = self.t(mm, waits=[("b_xnt", xnt_rh), ("b_wc", 32),
                                           ("b_exp", s.count("b_exp")),
                                           ("b_qkev", max(0, s.count("b_qkev") - 1))],
                                inc=("b_qkmm", 1))
                    self.scl(lambda e, dst=dst, dk=dk, rh=rh, pspar=pspar: e.activation(
                        out=dst[:, dk * QT * 128 + rh * rhw: dk * QT * 128 + rh * rhw + rhw],
                        in_=self.ps[pspar].ap()[:, :rhw], func=ACT.Copy),
                        waits=[mt, ("b_att_done", s.count("b_att_done")),
                               ("b_ktsp", s.count("b_ktsp"))],
                        inc=("b_qkev", 1))
        for tau in range(QT):
            for vh in range(2):
                pspar = (tau * 2 + vh) % 2
                def mmv(e, tau=tau, vh=vh, pspar=pspar):
                    last = None
                    for d in range(self.DCH):
                        last = e.matmul(self.ps[2 + pspar].ap()[:, :512],
                                        ab["xnT"][:, d * QT * 128 + tau * 128:
                                                  d * QT * 128 + tau * 128 + 128],
                                        ab["wv"][:, d * D + vh * 512: d * D + vh * 512 + 512],
                                        start=(d == 0), stop=(d == self.DCH - 1))
                    return last
                mt = self.t(mmv, waits=[("b_xnt", xnt_base + 16 * (tau + 1)), ("b_wc", 32),
                                        ("b_oev", s.count("b_oev")),
                                        ("b_vev", max(0, s.count("b_vev") - 1))],
                            inc=("b_vmm", 1))
                self.scl(lambda e, tau=tau, vh=vh, pspar=pspar: e.activation(
                    out=ab["vc"][:, tau * D + vh * 512: tau * D + vh * 512 + 512],
                    in_=self.ps[2 + pspar].ap()[:, :512], func=ACT.Copy),
                    waits=[mt, ("b_att_done", s.count("b_att_done")),
                           ("b_vsp", s.count("b_vsp"))],
                    inc=("b_vev", 1))
        kev = s.count("b_qkev")
        vev = s.count("b_vev")
        self.gdma(lambda e, k=k: e.dma_start(out=self.kt_hbm[k], in_=ab["ktc"]),
                  waits=[("b_qkev", kev), ("b_ktsp", s.count("b_ktsp"))],
                  inc=("b_ktsp", 16))
        self.gdma(lambda e, k=k: e.dma_start(
            out=self.v_hbm[k * QT:(k + 1) * QT].rearrange("t p d -> p t d"),
            in_=ab["vc"].rearrange("p (t d) -> p t d", d=D)),
            waits=[("b_vev", vev), ("b_vsp", s.count("b_vsp"))], inc=("b_vsp", 16))

    def emit_p2_attn(self, k):
        s = self.sc
        ab = self.ab2
        D, QT = self.D, self.QT
        gsz = min(4, self.QT)
        ngran = (self.QT // gsz) * k
        for g in range(QT // 2):
            taus = (2 * g, 2 * g + 1)
            gtoks = []
            base_av = s.count("b_av")
            base_smm = s.count("b_smm")
            for gr in range(ngran):
                q, gi = gr // 2, gr % 2
                par = gr % 2
                wk = [("b_ktsp", 16 * (q + 1)), ("b_vsp", 16 * (q + 1))]
                wv = []
                if gr >= 2:
                    wk.append(("b_smm", base_smm + 2 * (gr - 2) + 2))
                    wv.append(("b_av", base_av + 2 * (gr - 2) + 2))
                else:
                    if self.gr_last_av[par]:
                        wk.append(("b_av", self.gr_last_av[par]))
                    wk.append(("b_qkmm", s.count("b_qkmm")))
                    wk.append(("b_vmm", s.count("b_vmm")))
                gw = gsz * 128
                l1 = self.gdma(lambda e, q=q, gi=gi, par=par, gw=gw: e.dma_start(
                    out=ab["ktg"][par][:, :self.DCH * gw].rearrange("p (d r) -> p d r", r=gw),
                    in_=self.kt_hbm[q].rearrange("p (d r) -> p d r", r=QT * 128)
                    [:, :, gi * gw:(gi + 1) * gw]),
                    waits=wk + [("b_gl", s.count("b_gl"))], inc=("b_gl", 16))
                l2 = self.gdma(lambda e, q=q, gi=gi, par=par: e.dma_start(
                    out=ab["vg"][par].rearrange("p (t d) -> p t d", d=D)[:, :gsz],
                    in_=self.v_hbm[q * QT + gi * gsz: q * QT + gi * gsz + gsz]
                    .rearrange("t p d -> p t d")),
                    waits=wv + [("b_gl", s.count("b_gl"))], inc=("b_gl", 16))
                gtoks.append((l1, l2, par))
            chunks = []
            for ci in range(ngran):
                for tau in taus:
                    chunks.append((tau, "g", ci, gsz, False))
            loc = {tau: [] for tau in taus}
            for tau in taus:
                lo = 0
                while lo < tau:
                    wdt = min(4, tau - lo)
                    loc[tau].append((tau, "l", lo, wdt, False))
                    lo += wdt
                loc[tau].append((tau, "l", tau, 1, True))
            mx = max(len(loc[t]) for t in taus)
            for i in range(mx):
                for tau in taus:
                    if i < len(loc[tau]):
                        chunks.append(loc[tau][i])
            for a_ in range(ngran):
                self.gr_last_av[a_ % 2] = base_av + 2 * a_ + 2
            self.oev_gate = s.count("b_oev")
            self.av_base = s.count("b_av")
            pend = []
            seen_gr = set()
            for ci, ch in enumerate(chunks):
                if ch[1] == "g" and ch[2] not in seen_gr:
                    seen_gr.add(ch[2])
                    if ch[2] >= 2:
                        while len(pend) > 2:
                            self.emit_p2_av(*pend.pop(0))
                info = self.emit_p2_scores(k, g, ci, ch, gtoks)
                pend.append(info)
                if ci in (1, 3):
                    self.emit_p2_finB()
                if len(pend) > 3:
                    self.emit_p2_av(*pend.pop(0))
            for p_ in pend:
                self.emit_p2_av(*p_)
            for tau in taus:
                self.emit_p2_fin(k, tau)
            if os.environ.get("MK_NODEFER") == "1":
                while self.pend_fin:
                    self.emit_p2_finB()

    def emit_p2_scores(self, k, g, ci, ch, gtoks):
        s = self.sc
        ab = self.ab2
        D, QT = self.D, self.QT
        (tau, kind, a, wt, diag) = ch
        w = wt * 128
        par = ci % 2
        apar = ci % 3
        gpar = gtoks[a][2] if kind == "g" else 0

        def mm(e):
            last = None
            for dk in range(self.DCH):
                lhsT = ab["qT"][:, dk * QT * 128 + tau * 128: dk * QT * 128 + tau * 128 + 128]
                if kind == "g":
                    gw = min(4, self.QT) * 128
                    rhs = ab["ktg"][gpar][:, dk * gw: dk * gw + w]
                else:
                    rhs = ab["ktc"][:, dk * QT * 128 + a * 128: dk * QT * 128 + a * 128 + w]
                last = e.matmul(self.ps[par].ap()[:, :w], lhsT, rhs,
                                start=(dk == 0), stop=(dk == self.DCH - 1))
            return last
        waits = [("b_exp", max(0, s.count("b_exp") - 1)),
                 ("b_qkev", s.count("b_qkev"))]
        if kind == "g":
            waits.append(gtoks[a][0])
        mt = self.t(mm, waits=waits, inc=("b_smm", 1))
        att = ab["att"][apar]
        dguard = ("b_den", max(0, s.count("b_den") - 1))
        if not diag:
            ex = self.scl(lambda e: e.activation(
                out=att[:, :w], in_=self.ps[par].ap()[:, :w], func=ACT.Exp,
                bias=self.c_b20, scale=self.c_s32, accum_out=ab["dpart"][:, par:par + 1]),
                waits=[mt, ("c_ini", 1), dguard,
                       ("b_att_t", max(0, s.count("b_att_t") - 32))],
                inc=("b_exp", 1))
            self.v(self.den_update_op(tau, par),
                   waits=[ex, ("b_oev", self.oev_gate)], inc=("b_den", 1))
            mk_tok = ex
        else:
            ex = self.scl(lambda e: e.activation(
                out=att[:, :w], in_=self.ps[par].ap()[:, :w], func=ACT.Exp,
                bias=self.c_b20, scale=self.c_s32),
                waits=[mt, ("c_ini", 1), dguard,
                       ("b_att_t", max(0, s.count("b_att_t") - 32))],
                inc=("b_exp", 1))
            mk_tok = self.v(lambda e: e.tensor_tensor(out=att[:, :w], in0=att[:, :w],
                                                      in1=self.tri, op=ALU.mult),
                            waits=[ex, ("tri_ld", 16)], inc=("b_msk", 1))
            rr = self.v(lambda e: e.tensor_reduce(
                out=ab["dpart"][:, par:par + 1].rearrange("p (a b) -> p a b", b=1),
                in_=att[:, :w].rearrange("p (a b) -> p a b", b=w),
                axis=AX.X, op=ALU.add), waits=[mk_tok], inc=("qs_v", 1))
            self.v(self.den_update_op(tau, par),
                   waits=[rr, ("b_oev", self.oev_gate)], inc=("b_den", 1))
        tr = self.sy(lambda e: e.dma_start_transpose(
            out=ab["attT"][apar][:, :w].rearrange("p (c r) -> p c r", r=128),
            in_=att[:, :w].rearrange("p (c r) -> p c r", r=128)),
            waits=[mk_tok, ("b_av", max(self.av_base, self.av_base + ci - 2)),
                   ("b_att_t", s.count("b_att_t"))],
            inc=("b_att_t", 16))
        vtok = gtoks[a][1] if kind == "g" else None
        return (ch, apar, gpar, tr, vtok)

    def den_update_op(self, tau, par):
        first = self.den_first[tau]
        self.den_first[tau] = False
        dst = self.ab2["den"][:, tau:tau + 1]
        srcp = self.ab2["dpart"][:, par:par + 1]
        if first:
            return lambda e: e.tensor_copy(out=dst, in_=srcp)
        return lambda e: e.tensor_tensor(out=dst, in0=dst, in1=srcp, op=ALU.add)

    def emit_p2_av(self, ch, par, gpar, tr, vtok=None):
        s = self.sc
        ab = self.ab2
        D = self.D
        (tau, kind, a, wt, diag) = ch
        obank = 2 + 2 * (tau % 2)

        firsts = (self.o_first[tau][0], self.o_first[tau][1])
        self.o_first[tau][0] = False
        self.o_first[tau][1] = False

        def mm(e):
            last = None
            for vh in range(2):
                for mt_ in range(wt):
                    lhsT = ab["attT"][par][:, mt_ * 128: mt_ * 128 + 128]
                    if kind == "g":
                        rhs = ab["vg"][gpar][:, mt_ * D + vh * 512: mt_ * D + vh * 512 + 512]
                    else:
                        rhs = ab["vc"][:, (a + mt_) * D + vh * 512: (a + mt_) * D + vh * 512 + 512]
                    last = e.matmul(self.ps[obank + vh].ap()[:, :512], lhsT, rhs,
                                    start=(firsts[vh] and mt_ == 0),
                                    stop=(diag and mt_ == wt - 1))
            return last
        w_ = [tr, ("b_oev", self.oev_gate)]
        if vtok is not None:
            w_.append(vtok)
        tok = self.t(mm, waits=w_, inc=("b_av", 1))
        if kind == "g":
            self.gr_free[gpar] = [tok]

    def emit_p2_fin(self, k, tau):
        s = self.sc
        ab = self.ab2
        D, QT = self.D, self.QT
        t = k * QT + tau
        obank = 2 + 2 * (tau % 2)
        den_n = s.count("b_den")
        par = tau % 2
        x1r = ab["x1r"][par]
        ld = self.gdma(lambda e: e.dma_start(out=x1r[:, :D],
                                             in_=self.x1_hbm[t * 128:(t + 1) * 128, :]),
                       waits=[("b_x1rl", s.count("b_x1rl")),
                              ("b_x2w", max(0, s.count("b_x2w") - 1))],
                       inc=("b_x1rl", 16))
        rd = self.v(lambda e: e.reciprocal(out=ab["den"][:, QT + tau:QT + tau + 1],
                                           in_=ab["den"][:, tau:tau + 1]),
                    waits=[("b_den", den_n), ("b_av", s.count("b_av"))],
                    inc=("qs_v", 1))
        rden = ab["den"][:, QT + tau:QT + tau + 1]
        e1 = self.v(lambda e: e.tensor_scalar(out=ab["ov"][:, 0:512],
                                              in0=self.ps[obank].ap()[:, :512],
                                              scalar1=rden, scalar2=None, op0=ALU.mult),
                    waits=[rd], inc=("b_oev", 1))
        e2 = self.v(lambda e: e.tensor_scalar(out=ab["ov"][:, 512:1024],
                                              in0=self.ps[obank + 1].ap()[:, :512],
                                              scalar1=rden, scalar2=None, op0=ALU.mult),
                    waits=[e1], inc=("b_oev", 1))
        av = self.v(lambda e: e.tensor_tensor(
            out=ab["ov"][:, :D], in0=ab["ov"][:, :D],
            in1=ab["vc"][:, tau * D:(tau + 1) * D], op=ALU.add),
            waits=[e2], inc=("qs_v", 1))
        self.lnb["waits"] = [av]
        r1 = self.emit_ln_stats(ab["ov"][:, :D], D, "b_ln2", use_expln=True,
                                rcol=2 + tau % 2)
        stt_ = ab["stats"][2 + tau % 2]
        rcap = ab["rs"][2 + tau % 2][:, 0:1]
        n1 = self.v(lambda e, stt_=stt_, rcap=rcap: e.tensor_scalar(
            out=ab["ov"][:, :D], in0=ab["ov"][:, :D],
            scalar1=stt_[:, 12:13], scalar2=rcap,
            op0=ALU.subtract, op1=ALU.mult),
                    waits=[r1], inc=("qs_v", 1))
        n2 = self.v(lambda e: e.tensor_tensor(out=ab["ov"][:, :D], in0=ab["ov"][:, :D],
                                              in1=ab["gam"][:, :D], op=ALU.mult),
                    waits=[n1, ("b_cld", 48)], inc=("qs_v", 1))
        n3 = self.v(lambda e: e.tensor_tensor(out=ab["lnr"][:, :D], in0=ab["ov"][:, :D],
                                              in1=ab["bet"][:, :D], op=ALU.add),
                    waits=[n2, ("b_lnt", s.count("b_lnt")),
                           ("b_xnt", s.count("b_xnt"))], inc=("qs_v", 1))
        lt = ab["lnrT"][par]
        tr = self.sy(lambda e: e.dma_start_transpose(
            out=lt.rearrange("p (d r) -> p d r", r=128),
            in_=ab["lnr"][:, :D].rearrange("p (d r) -> p d r", r=128)),
            waits=[n3, ("b_lnt", s.count("b_lnt")),
                   ("b_fc", self.fc_pred)],
            inc=("b_lnt", 16))
        self.fc_pred += 1
        self.pend_fin.append((k, tau, t, tr, ld))

    def emit_p2_finB(self):
        if not self.pend_fin:
            return
        (k, tau, t, tr, ld) = self.pend_fin.pop(0)
        s = self.sc
        ab = self.ab2
        D, QT = self.D, self.QT
        par = tau % 2
        lt = ab["lnrT"][par]

        def fc(e):
            last = None
            for oh in range(2):
                for d in range(self.DCH):
                    last = e.matmul(self.ps[6 + oh].ap()[:, :512],
                                    lt[:, d * 128: d * 128 + 128],
                                    ab["wfc"][:, d * D + oh * 512: d * D + oh * 512 + 512],
                                    start=(d == 0), stop=(d == self.DCH - 1))
            return last
        ft = self.t(fc, waits=[tr, ("b_wc", 32), ("b_x2w", s.count("b_x2w"))],
                    inc=("b_fc", 1))
        x2 = ab["x2st"][par]
        w1 = self.v(lambda e: e.scalar_tensor_tensor(
            out=x2[:, 0:512], in0=self.ps[6].ap()[:, :512], scalar=1.0,
            in1=ab["bfc"][:, 0:512], op0=ALU.mult, op1=ALU.add),
            waits=[ft, ld, ("b_x2s", max(0, 16 * (t - 1)))], inc=("qs_v", 1))
        w2 = self.v(lambda e: e.scalar_tensor_tensor(
            out=x2[:, 512:1024], in0=self.ps[7].ap()[:, :512], scalar=1.0,
            in1=ab["bfc"][:, 512:1024], op0=ALU.mult, op1=ALU.add),
            waits=[w1], inc=("qs_v", 1))
        w3 = self.v(lambda e: e.tensor_tensor(out=x2[:, :D], in0=x2[:, :D],
                                              in1=ab["x1r"][par][:, :D], op=ALU.add),
                    waits=[w2], inc=("b_x2w", 1))
        self.gdma(lambda e: e.dma_start(out=self.x2_hbm[t * 128:(t + 1) * 128, :],
                                        in_=x2[:, :D]),
                  waits=[w3, ("b_x2s", s.count("b_x2s"))], inc=("b_x2s", 16))
        self.v(lambda e: None, inc=("b_att_done", 1))

import ml_dtypes
from concourse.bass_utils import run_bass_kernel_spmd

_NT, _QT, _D, _I = 32, 8, 1024, 2816


def _build():
    nc = bass.Bass("TRN2", target_bir_lowering=False)
    mk = MK(nc, NT=_NT, QT=_QT, D=_D, I=_I)
    mk.emit_consts()
    mk.emit_ffn_phase(first=True)
    mk.emit_p2()
    mk.emit_ffn_phase(first=False)
    mk.sc.finalize_waits()
    with nc.Block() as blk:
        @blk.sync
        def _(e):
            mk.sc.emit_stream("sync", e)

        @blk.scalar
        def _(e):
            mk.sc.emit_stream("scalar", e)

        @blk.vector
        def _(e):
            mk.sc.emit_stream("vector", e)

        @blk.tensor
        def _(e):
            mk.sc.emit_stream("tensor", e)

        @blk.gpsimd
        def _(e):
            mk.sc.emit_stream("gpsimd", e)
    return nc


def kernel(L0_attn_x, L0_attn_Wq, L0_attn_Wk, L0_attn_Wv, L0_attn_Wfc,
           L0_attn_bfc, L0_attn_gamma, L0_attn_beta,
           L0_ffn_w_gate, L0_ffn_w_up, L0_ffn_w_down):
    nc = _build()
    x = np.ascontiguousarray(L0_attn_x)
    tri = np.tril(np.ones((128, 128), np.float32)).astype(ml_dtypes.bfloat16)
    base = {
        "Wq": np.ascontiguousarray(L0_attn_Wq),
        "Wk": np.ascontiguousarray(L0_attn_Wk),
        "Wv": np.ascontiguousarray(L0_attn_Wv),
        "Wfc": np.ascontiguousarray(L0_attn_Wfc),
        "bfc": np.ascontiguousarray(L0_attn_bfc).reshape(1, _D),
        "gamma": np.ascontiguousarray(L0_attn_gamma).reshape(1, _D),
        "beta": np.ascontiguousarray(L0_attn_beta).reshape(1, _D),
        "w_gate": np.ascontiguousarray(L0_ffn_w_gate),
        "w_up": np.ascontiguousarray(L0_ffn_w_up),
        "w_down": np.ascontiguousarray(L0_ffn_w_down),
        "tri": tri,
    }
    in_maps = [dict(base, x=np.ascontiguousarray(x[b])) for b in range(8)]
    res = run_bass_kernel_spmd(nc, in_maps, core_ids=list(range(8)))
    out = np.stack([res.results[b]["y"] for b in range(8)], axis=0)
    return out.astype(np.float32)
